# version 17
# baseline (speedup 1.0000x reference)
import contextlib
import numpy as np
import ml_dtypes
import concourse.bass as bass
import concourse.mybir as mybir
from concourse.bass_utils import run_bass_kernel_spmd

F32 = mybir.dt.float32
BF16 = mybir.dt.bfloat16
AX = mybir.AxisListType.X
ALU = mybir.AluOpType
ACTF = mybir.ActivationFunctionType

T = 2048
D = 1024
NT = 16
NC = 8
H = 16
DH = 64
NPAIR = 8
NE = 8
TOPK = 64
NEG = -1.0e30

N_SUBLAYERS = 4


ENGINES = ("pe", "act", "dve", "pool", "sp")
_ENG_ATTR = {"pe": "tensor", "act": "scalar", "dve": "vector", "pool": "gpsimd",
             "sp": "sync"}


class Buf:
    __slots__ = ("name", "last_w", "readers")

    def __init__(self, name):
        self.name = name
        self.last_w = None
        self.readers = {}


class Sched:
    def __init__(self, nc):
        self.nc = nc
        self.sems = {}
        self.counts = {e: 0 for e in ENGINES}
        self.dma_counts = {}
        self.known = {e: {} for e in ENGINES}
        self.streams = {e: [] for e in ENGINES}
        self._sem_names = set()

    def buf(self, name):
        return Buf(name)

    def _need_waits(self, eng, reads, writes):
        req = {}
        for b in reads:
            if b.last_w is not None:
                s, v = b.last_w
                if req.get(s, -1) < v:
                    req[s] = v
        for b in writes:
            if b.last_w is not None:
                s, v = b.last_w
                if req.get(s, -1) < v:
                    req[s] = v
            for s, v in b.readers.items():
                if req.get(s, -1) < v:
                    req[s] = v
        out = []
        kn = self.known[eng]
        own = "s_" + eng
        for s, v in sorted(req.items()):
            if s == own and kn.get(s, -1) >= v:
                continue
            if kn.get(s, -1) < v:
                out.append((s, v))
                kn[s] = v
        return out

    def _record(self, sem, val, reads, writes):
        for b in reads:
            if b.readers.get(sem, -1) < val:
                b.readers[sem] = val
        for b in writes:
            b.last_w = (sem, val)
            b.readers = {}

    def emit(self, eng, fns, reads=(), writes=()):
        if callable(fns):
            fns = [fns]
        waits = self._need_waits(eng, reads, writes)
        sem = "s_" + eng
        self._sem_names.add(sem)
        self.counts[eng] += 1
        val = self.counts[eng]
        self.streams[eng].append(("op", fns, waits, sem, 1))
        self._record(sem, val, reads, writes)
        return val

    def dma(self, eng, fns, reads=(), writes=(), sem=None):
        if callable(fns):
            fns = [fns]
        waits = self._need_waits(eng, reads, writes)
        sem = sem if sem is not None else ("d_" + eng)
        self._sem_names.add(sem)
        self.dma_counts[sem] = self.dma_counts.get(sem, 0) + 16 * len(fns)
        val = self.dma_counts[sem]
        self.streams[eng].append(("dma", fns, waits, sem, 16))
        self._record(sem, val, reads, writes)
        return val

    def fence(self):
        snap_e = {e: self.counts[e] for e in ENGINES}
        snap_d = dict(self.dma_counts)
        for e in ENGINES:
            ws = [("s_" + o, snap_e[o]) for o in ENGINES
                  if o != e and snap_e[o] > 0]
            ws += [(s, v) for s, v in snap_d.items() if v > 0]
            for s, _ in ws:
                self._sem_names.add(s)
            if ws:
                self.streams[e].append(("waitonly", None, ws, None, 0))
                kn = self.known[e]
                for s, v in ws:
                    if kn.get(s, -1) < v:
                        kn[s] = v

    def drain_all(self, eng="pool"):
        ws = [("s_" + e, self.counts[e]) for e in ENGINES
              if e != eng and self.counts[e] > 0]
        ws += [(s, v) for s, v in self.dma_counts.items() if v > 0]
        for s, _ in ws:
            self._sem_names.add(s)
        self.streams[eng].append(("waitonly", None, ws, None, 0))

    def finalize(self, block, exit_stack):
        for s in sorted(self._sem_names):
            self.sems[s] = exit_stack.enter_context(self.nc.semaphore(s))

        def run_stream(name):
            def body(eng):
                for kind, fns, waits, sem, _amt in self.streams[name]:
                    for ws, wv in waits:
                        eng.wait_ge(self.sems[ws], wv)
                    if kind == "waitonly":
                        continue
                    insts = [fn(eng) for fn in fns]
                    if kind == "op":
                        insts[-1].then_inc(self.sems[sem], 1)
                    else:
                        for inst in insts:
                            inst.then_inc(self.sems[sem], 16)
            return body

        for e in ENGINES:
            if self.streams[e]:
                getattr(block, _ENG_ATTR[e])(run_stream(e))



def _mask_variants():
    r = np.arange(128)[:, None]
    j = np.arange(896)[None, :]
    return np.ascontiguousarray(
        np.where(j <= r + 384, 0.0, NEG)).astype(ml_dtypes.bfloat16)


class Ctx:
    pass


def build_program(n_sub=None):
    n_sub = N_SUBLAYERS if n_sub is None else n_sub
    nc = bass.Bass("TRN2", target_bir_lowering=False)
    c = Ctx()
    c.nc = nc
    S = Sched(nc)
    c.S = S
    est = contextlib.ExitStack()
    c.est = est

    dram = {}

    def din(name, shape, dt=F32):
        dram[name] = nc.dram_tensor(name, list(shape), dt, kind="ExternalInput").ap()
        return dram[name]

    din("x", (T, D))
    for p in ("a", "b"):
        din(f"{p}_wq", (D, D)); din(f"{p}_wk", (D, D)); din(f"{p}_wv", (D, D))
        din(f"{p}_wo", (D, D))
        din(f"{p}_bq", (128, 8)); din(f"{p}_bk", (128, 8)); din(f"{p}_bv", (128, 8))
        din(f"{p}_bo", (1, D))
    din("up_w", (NE * D, D)); din("down_w", (NE * D, D))
    din("sh_up_w", (D, D)); din("sh_down_w", (D, D))
    din("up_b", (NE, D)); din("down_b", (NE, D))
    din("sh_up_b", (1, D)); din("sh_down_b", (1, D))
    din("gate_sw", (128, 64))
    din("idn_bf", (128, 128), BF16)
    din("idn_f32", (128, 128))
    din("ones2", (2, 128), BF16)
    din("masks4", (128, 896), BF16)
    out_h = nc.dram_tensor("out", [T, D], F32, kind="ExternalOutput").ap()

    def dsc(name, shape, dt=BF16):
        t = nc.dram_tensor(name, list(shape), dt, kind="Internal")
        dram[name] = t.ap()
        return dram[name]

    dsc("c_up_h", (NE * 128, NC * D)); dsc("c_up_l", (NE * 128, NC * D))
    dsc("c_dn_h", (NE * 128, NC * D)); dsc("c_dn_l", (NE * 128, NC * D))
    dsc("c_shu_h", (128, NC * D)); dsc("c_shu_l", (128, NC * D))
    dsc("c_shd_h", (128, NC * D)); dsc("c_shd_l", (128, NC * D))
    dsc("c_ub_h", (NE, D)); dsc("c_ub_l", (NE, D))
    dsc("c_sub_h", (2, D)); dsc("c_sub_l", (2, D))
    c.dram = dram

    def sb(name, shape, dt=F32):
        return est.enter_context(nc.sbuf_tensor(name, list(shape), dt))

    c.X = sb("X", (128, NT * D))
    c.idn_bf = sb("idn_bf_t", (128, 128), BF16)
    c.idn_f32 = sb("idn_f32_t", (128, 128))
    c.ones2 = sb("ones2_t", (2, 128), BF16)
    c.cols = sb("cols", (128, 32))
    c.m8 = sb("m8t", (128, 8))

    B = S.buf
    c.b_x = [B(f"x{t}") for t in range(NT)]
    c.b_consts = B("consts")
    c.b_cols = B("cols")
    c.b_m8 = B("m8")
    c.b_out = [B(f"out{t}") for t in range(NT)]
    c.b_cache = B("wcache")

    for tt in range(NT):
        S.dma("sp", (lambda tt: lambda e: e.dma_start(
            out=c.X[:, tt * D:(tt + 1) * D],
            in_=dram["x"][tt * 128:(tt + 1) * 128, :]))(tt),
            writes=[c.b_x[tt]], sem=f"dx{tt}")
    S.dma("sp", [
        lambda e: e.dma_start(out=c.idn_bf[:], in_=dram["idn_bf"][:]),
        lambda e: e.dma_start(out=c.idn_f32[:], in_=dram["idn_f32"][:]),
        lambda e: e.dma_start(out=c.ones2[:], in_=dram["ones2"][:]),
    ], writes=[c.b_consts], sem="dconst")

    if n_sub >= 1:
        _emit_attn(c, "a", causal=True, prepass=(n_sub >= 2))
    if n_sub >= 2:
        S.fence()
        _emit_ffn(c)
    if n_sub >= 3:
        S.fence()
        _emit_attn(c, "b", causal=False, prepass=False)
    if n_sub >= 4:
        S.fence()
        _emit_ffn(c)

    for tt in range(NT):
        S.dma("sp", (lambda tt: lambda e: e.dma_start(
            out=out_h[tt * 128:(tt + 1) * 128, :],
            in_=c.X[:, tt * D:(tt + 1) * D]))(tt),
            reads=[c.b_x[tt]], writes=[c.b_out[tt]], sem=f"do{tt}")
    S.drain_all("pool")

    block = est.enter_context(nc.Block())
    S.finalize(block, est)
    est.close()
    return nc



def _emit_ln_tile(c, tt, xn_hi, xn_lo, b_xn, tloc, t_span, ps, b_ps, sdump, b_sdump):
    S, nc, X = c.S, c.nc, c.X
    xs = X[:, tt * D:(tt + 1) * D]
    su, sq, mn, nv, rs, nm = (c.cols[:, i:i + 1] for i in range(6))
    S.emit("dve", (lambda xs: lambda e: e.reduce_sum(su, xs, axis=AX))(xs),
           reads=[c.b_x[tt]], writes=[c.b_cols])
    S.emit("act", (lambda xs: lambda e: e.activation(
        out=sdump[:, 0:D], in_=xs, func=ACTF.Square, accum_out=sq))(xs),
        reads=[c.b_x[tt]], writes=[c.b_cols, b_sdump])
    S.emit("dve", lambda e: e.tensor_scalar(mn, su, 1.0 / D, None, op0=ALU.mult),
           reads=[c.b_cols], writes=[c.b_cols])
    S.emit("dve", lambda e: e.tensor_scalar(nv, sq, -1.0 / D, -1e-5,
                                            op0=ALU.mult, op1=ALU.add),
           reads=[c.b_cols], writes=[c.b_cols])
    S.emit("dve", lambda e: e.scalar_tensor_tensor(
        nv, in0=mn, scalar=mn, in1=nv, op0=ALU.mult, op1=ALU.add),
        reads=[c.b_cols], writes=[c.b_cols])
    S.emit("act", lambda e: e.activation(out=rs, in_=nv, func=ACTF.Sqrt,
                                         scale=-1.0),
           reads=[c.b_cols], writes=[c.b_cols])
    S.emit("dve", lambda e: e.reciprocal(rs, rs),
           reads=[c.b_cols], writes=[c.b_cols])
    S.emit("dve", lambda e: e.tensor_scalar(nm, mn, rs, -1.0,
                                            op0=ALU.mult, op1=ALU.mult),
           reads=[c.b_cols], writes=[c.b_cols])
    S.emit("act", (lambda xs: lambda e: e.activation(
        out=sdump[:, 0:D], in_=xs, func=ACTF.Identity, scale=rs, bias=nm))(xs),
        reads=[c.b_x[tt], c.b_cols], writes=[b_sdump])
    for half in range(2):
        fns = []
        for q in range(4):
            dc = half * 4 + q
            fns.append((lambda dc, q: lambda e: e.transpose(
                ps[:, q * 128:(q + 1) * 128],
                sdump[:, dc * 128:(dc + 1) * 128], c.idn_f32[:]))(dc, q))
        S.emit("pe", fns, reads=[b_sdump, c.b_consts], writes=[b_ps])
        dst_hi = xn_hi[:, :].rearrange("p (cc t) -> p cc t", t=t_span)[
            :, half * 4:half * 4 + 4, tloc * 128:(tloc + 1) * 128]
        dst_lo = xn_lo[:, :].rearrange("p (cc t) -> p cc t", t=t_span)[
            :, half * 4:half * 4 + 4, tloc * 128:(tloc + 1) * 128]
        src = ps[:, 0:512]
        S.emit("act", (lambda dst_hi, src: lambda e: e.activation(
            out=dst_hi, in_=src, func=ACTF.Copy))(dst_hi, src),
            reads=[b_ps], writes=[b_xn])
        S.emit("dve", (lambda dst_lo, src, dst_hi: lambda e: e.tensor_sub(
            dst_lo, src, dst_hi))(dst_lo, src, dst_hi),
            reads=[b_ps, b_xn], writes=[b_xn])



def _emit_attn(c, pfx, causal, prepass):
    S, nc, dram = c.S, c.nc, c.dram
    B = S.buf
    est = contextlib.ExitStack()

    def sb(name, shape, dt=F32):
        return est.enter_context(nc.sbuf_tensor(f"{pfx}_{name}", list(shape), dt))

    xn_hi = sb("xnh", (128, NC * T), BF16)
    xn_lo = sb("xnl", (128, NC * T), BF16)
    qt_hi = sb("qth", (128, T), BF16); qt_lo = sb("qtl", (128, T), BF16)
    kt_hi = sb("kth", (128, T), BF16); kt_lo = sb("ktl", (128, T), BF16)
    v_hi = sb("vh", (128, T), BF16); v_lo = sb("vl", (128, T), BF16)
    trb_hi = sb("trbh", (128, T), BF16); trb_lo = sb("trbl", (128, T), BF16)
    phi = [sb("phi0", (128, T), BF16), sb("phi1", (128, T), BF16)]
    plo = [sb("plo0", (128, T), BF16), sb("plo1", (128, T), BF16)]
    ssb = [sb("ssb0", (128, T)), sb("ssb1", (128, T))]
    if not causal:
        sx = sb("sx", (128, T))
    otq_hi = sb("otqh", (128, 128), BF16); otq_lo = sb("otql", (128, 128), BF16)
    if causal:
        c.masks = sb("masks", (128, 896), BF16)
        S.dma("sp", lambda e: e.dma_start(out=c.masks[:], in_=dram["masks4"][:]),
              writes=[c.b_consts], sem="dmask")
    big32 = ssb[0] if causal else sx
    b_big32 = None
    wst = sb("wst", (128, 512))
    wB_hi = sb("wBh", (128, D), BF16); wB_lo = sb("wBl", (128, D), BF16)
    bqkv = sb("bqkv", (128, 24))

    ps = est.enter_context(nc.psum_tensor(f"{pfx}_ps", [128, 2048], F32))
    tr0 = est.enter_context(nc.psum_tensor(f"{pfx}_tr0", [128, 1024], BF16))
    tr1 = est.enter_context(nc.psum_tensor(f"{pfx}_tr1", [128, 1024], BF16))
    po = est.enter_context(nc.psum_tensor(f"{pfx}_po", [128, 512], F32))
    yw = est.enter_context(nc.psum_tensor(f"{pfx}_yw", [128, 512], F32))

    b_xn = B("xn"); b_qt = B("qt"); b_kt = B("kt"); b_v = B("v")
    b_trb = B("trb"); b_phi = [B("phi0"), B("phi1")]
    b_psb = [B("psb0"), B("psb1"), B("psb2"), B("psb3")]
    b_ssb = [B("ssb0"), B("ssb1")]; b_sx = B("sx")
    b_big32 = b_ssb[0] if causal else b_sx
    bsp_h = phi[0][0:1, 0:D]
    bsp_l = phi[0][0:1, D:2 * D]
    b_bsp_ali = b_phi[0]
    b_otq = B("otq"); b_wst = B("wst")
    b_wB = B("wB"); b_bsp = B("bsp"); b_bqkv = B("bqkv")
    b_ps = B("ps"); b_tr0 = B("tr0"); b_tr1 = B("tr1")
    b_po = B("po"); b_yw = B("yw")
    sdump = big32
    b_sdump = b_big32

    ppw = _prep_ffn_prepass_work(c, est) if prepass else []
    for tt in range(NT):
        _emit_ln_tile(c, tt, xn_hi, xn_lo, b_xn, tt, T, ps, b_psb[0],
                      sdump, b_sdump)

    S.dma("sp", lambda e: e.dma_start(
        out=bqkv[:, 0:8], in_=dram[f"{pfx}_bq"][:]), writes=[b_bqkv], sem="dbias")
    S.dma("sp", lambda e: e.dma_start(
        out=bqkv[:, 8:16], in_=dram[f"{pfx}_bk"][:]), writes=[b_bqkv], sem="dbias")
    S.dma("sp", lambda e: e.dma_start(
        out=bqkv[:, 16:24], in_=dram[f"{pfx}_bv"][:]), writes=[b_bqkv], sem="dbias")
    S.dma("sp", lambda e: e.dma_start(out=big32[0:1, 0:D], in_=dram[f"{pfx}_bo"][:]),
          reads=[b_big32], writes=[b_big32], sem="dbias2")
    S.emit("act", lambda e: e.activation(out=bsp_h, in_=big32[0:1, 0:D],
                                         func=ACTF.Copy),
           reads=[b_big32], writes=[b_bsp_ali])
    S.emit("dve", lambda e: e.tensor_sub(bsp_l, big32[0:1, 0:D], bsp_h),
           reads=[b_big32, b_bsp_ali], writes=[b_bsp_ali])

    ones1 = c.ones2[0:1, :]
    for tt in range(NT):
        for dob in range(2):
            S.emit("pe", [
                (lambda dob: lambda e: e.matmul(
                    yw[:, :], ones1, bsp_h[0:1, dob * 512:(dob + 1) * 512],
                    start=True, stop=False))(dob),
                (lambda dob: lambda e: e.matmul(
                    yw[:, :], ones1, bsp_l[0:1, dob * 512:(dob + 1) * 512],
                    start=False, stop=True))(dob),
            ], reads=[c.b_consts, b_bsp_ali], writes=[b_yw])
            xs = c.X[:, tt * D + dob * 512: tt * D + (dob + 1) * 512]
            S.emit("dve", (lambda xs: lambda e: e.tensor_add(xs, xs, yw[:, :]))(xs),
                   reads=[b_yw, c.b_x[tt]], writes=[c.b_x[tt]])

    def load_split_w(hbm_rows, whi, wlo, b_w, semn, wscale=1.0):
        for h2 in range(2):
            cols = slice(h2 * 512, (h2 + 1) * 512)
            S.dma("sp", (lambda hbm_rows, cols: lambda e: e.dma_start(
                out=wst[:, :], in_=hbm_rows[:, cols]))(hbm_rows, cols),
                reads=[b_wst], writes=[b_wst], sem=semn)
            S.emit("act", (lambda whi, cols, wscale: lambda e: e.activation(
                out=whi[:, cols], in_=wst[:, :], func=ACTF.Copy,
                scale=wscale))(whi, cols, wscale),
                reads=[b_wst], writes=[b_w])
            S.emit("dve", (lambda whi, wlo, cols, wscale: lambda e:
                   e.scalar_tensor_tensor(
                wlo[:, cols], in0=wst[:, :], scalar=wscale, in1=whi[:, cols],
                op0=ALU.mult, op1=ALU.subtract))(whi, wlo, cols, wscale),
                reads=[b_wst, b_w], writes=[b_w])

    NKC = 16

    for pair in range(NPAIR):
        for mat, dsth, dstl, b_dst, bias_off in (
                ("wq", qt_hi, qt_lo, b_qt, 0),
                ("wk", kt_hi, kt_lo, b_kt, 8),
                ("wv", trb_hi, trb_lo, b_trb, 16)):
            whi, wlo, b_w = wB_hi, wB_lo, b_wB
            load_split_w(dram[f"{pfx}_{mat}"][pair * 128:(pair + 1) * 128, :],
                         whi, wlo, b_w, "dwp",
                         wscale=0.125 if mat == "wq" else 1.0)
            bias_col = bqkv[:, bias_off + pair:bias_off + pair + 1]
            if bias_off == 0:
                bsc = c.cols[:, 11:12]
                S.emit("dve", (lambda bias_col, bsc: lambda e: e.tensor_scalar(
                    bsc, bias_col, 0.125, None, op0=ALU.mult))(bias_col, bsc),
                    reads=[b_bqkv], writes=[c.b_cols])
                bias_col = bsc
            for tb in range(4):
                fns = []
                rhs_hi = [xn_hi[:, dc * T + tb * 512:dc * T + (tb + 1) * 512]
                          for dc in range(NC)]
                rhs_lo = [xn_lo[:, dc * T + tb * 512:dc * T + (tb + 1) * 512]
                          for dc in range(NC)]
                dst = ps[:, tb * 512:(tb + 1) * 512]
                for dc in range(NC):
                    lh = whi[:, dc * 128:(dc + 1) * 128]
                    ll = wlo[:, dc * 128:(dc + 1) * 128]
                    fns.append((lambda dst, lh, rh, st: lambda e: e.matmul(
                        dst, lh, rh, start=st, stop=False))(
                        dst, lh, rhs_hi[dc], dc == 0))
                    fns.append((lambda dst, lh, rh: lambda e: e.matmul(
                        dst, lh, rh, start=False, stop=False))(dst, lh, rhs_lo[dc]))
                    fns.append((lambda dst, ll, rh, sp: lambda e: e.matmul(
                        dst, ll, rh, start=False, stop=sp))(
                        dst, ll, rhs_hi[dc], dc == NC - 1))
                S.emit("pe", fns, reads=[b_w, b_xn], writes=[b_psb[tb]])
                src_ = ps[:, tb * 512:(tb + 1) * 512]
                dh = dsth[:, tb * 512:(tb + 1) * 512]
                dl = dstl[:, tb * 512:(tb + 1) * 512]
                S.emit("act", (lambda dh, src, bias_col: lambda e: e.activation(
                    out=dh, in_=src, func=ACTF.Identity, bias=bias_col))(dh, src_, bias_col),
                    reads=[b_psb[tb], b_bqkv, c.b_cols], writes=[b_dst])
                S.emit("dve", (lambda dl, src, dh, bias_col: lambda e:
                       e.scalar_tensor_tensor(dl, in0=src, scalar=bias_col, in1=dh,
                                              op0=ALU.add, op1=ALU.subtract))(
                       dl, src_, dh, bias_col),
                       reads=[b_psb[tb], b_bqkv, b_dst, c.b_cols], writes=[b_dst])

        for half in range(2):
            for src, dstt, b_pt in ((trb_hi, v_hi, b_v), (trb_lo, v_lo, b_v)):
                trp = tr0 if src is trb_hi else tr1
                b_trp = b_tr0 if src is trb_hi else b_tr1
                fns = [(lambda kc, src, trp: lambda e: e.transpose(
                    trp[:, (kc % 8) * 128:((kc % 8) + 1) * 128],
                    src[:, kc * 128:(kc + 1) * 128], c.idn_bf[:]))(kc, src, trp)
                    for kc in range(half * 8, half * 8 + 8)]
                S.emit("pe", fns, reads=[b_trb, c.b_consts], writes=[b_trp])
                dst = dstt[:, half * 1024:(half + 1) * 1024]
                S.emit("act", (lambda dst, trp: lambda e: e.activation(
                    out=dst, in_=trp[:, :], func=ACTF.Copy))(dst, trp),
                    reads=[b_trp], writes=[b_pt])

        load_split_w(dram[f"{pfx}_wo"][pair * 128:(pair + 1) * 128, :],
                     wB_hi, wB_lo, b_wB, "dwp")
        m_, nm_, z_, rz_, t64_ = (c.cols[:, i:i + 1] for i in range(6, 11))
        zp = c.cols[:, 12:16]

        def emit_A(hl, qt, par):
            nkb = (qt // 4 + 1) if causal else 4
            for kb in range(nkb):
                fns = []
                dst = ps[:, kb * 512:(kb + 1) * 512]
                first = True
                if causal and kb == qt // 4:
                    mv = qt % 4
                    fns.append((lambda dst, mv: lambda e: e.matmul(
                        dst, c.idn_bf[:],
                        c.masks[:, 384 - 128 * mv:896 - 128 * mv],
                        start=True, stop=False))(dst, mv))
                    first = False
                lq_hi = qt_hi[hl * 64:hl * 64 + 64, qt * 128:(qt + 1) * 128]
                lq_lo = qt_lo[hl * 64:hl * 64 + 64, qt * 128:(qt + 1) * 128]
                rk_hi = kt_hi[hl * 64:hl * 64 + 64, kb * 512:(kb + 1) * 512]
                rk_lo = kt_lo[hl * 64:hl * 64 + 64, kb * 512:(kb + 1) * 512]
                tp = (hl * 64, 0)
                fns.append((lambda dst, lq, rk, st, tp: lambda e: e.matmul(
                    dst, lq, rk, start=st, stop=False, tile_position=tp))(
                    dst, lq_hi, rk_hi, first, tp))
                fns.append((lambda dst, lq, rk, tp: lambda e: e.matmul(
                    dst, lq, rk, start=False, stop=False, tile_position=tp))(
                    dst, lq_hi, rk_lo, tp))
                fns.append((lambda dst, lq, rk, tp: lambda e: e.matmul(
                    dst, lq, rk, start=False, stop=True, tile_position=tp))(
                    dst, lq_lo, rk_hi, tp))
                S.emit("pe", fns, reads=[b_qt, b_kt, c.b_consts],
                       writes=[b_psb[kb]])
                S.emit("act", (lambda kb, par: lambda e: e.activation(
                    out=ssb[par][:, kb * 512:(kb + 1) * 512],
                    in_=ps[:, kb * 512:(kb + 1) * 512],
                    func=ACTF.Copy))(kb, par),
                    reads=[b_psb[kb]], writes=[b_ssb[par]])

        def emit_B_stats(hl, qt, par):
            nkb = qt // 4 + 1
            sb_t = ssb[par]
            S.emit("dve", (lambda nkb, sb_t: lambda e: e.reduce_max(
                m_, sb_t[:, 0:nkb * 512], axis=AX))(nkb, sb_t),
                reads=[b_ssb[par]], writes=[c.b_cols])
            S.emit("dve", lambda e: e.tensor_scalar(
                nm_, m_, -1.0, None, op0=ALU.mult),
                reads=[c.b_cols], writes=[c.b_cols])
            S.emit("act", (lambda nkb, sb_t: lambda e: e.activation(
                out=sb_t[:, 0:nkb * 512], in_=sb_t[:, 0:nkb * 512],
                func=ACTF.Exp, bias=nm_,
                accum_out=z_))(nkb, sb_t),
                reads=[b_ssb[par], c.b_cols], writes=[b_ssb[par], c.b_cols])

        def emit_B_rounds(hl, qt, par, r0, r1):
            sb_t = ssb[par]
            for r in range(r0, r1):
                src_t, src_b = (sb_t, b_ssb[par]) if r == 0 else (sx, b_sx)
                S.emit("dve", (lambda src_t: lambda e: e.max(
                    out=c.m8[:], in_=src_t[:]))(src_t),
                    reads=[src_b], writes=[c.b_m8])
                if r == 0:
                    S.emit("dve", lambda e: e.tensor_scalar(
                        nm_, c.m8[:, 0:1], -1.0, 0.0,
                        op0=ALU.mult, op1=ALU.min),
                        reads=[c.b_m8], writes=[c.b_cols])
                S.emit("dve", (lambda src_t: lambda e: e.match_replace(
                    out=sx[:], in_to_replace=c.m8[:], in_values=src_t[:],
                    imm_value=NEG))(src_t),
                    reads=[src_b, c.b_m8], writes=[b_sx])

        def emit_B_mask(hl, qt, par):
            sb_t = ssb[par]
            S.emit("dve", lambda e: e.tensor_copy(t64_, c.m8[:, 7:8]),
                   reads=[c.b_m8], writes=[c.b_cols])
            S.emit("dve", (lambda sb_t: lambda e: e.scalar_tensor_tensor(
                sx[:], in0=sb_t[:], scalar=t64_, in1=sb_t[:],
                op0=ALU.is_ge, op1=ALU.mult))(sb_t),
                reads=[b_ssb[par], c.b_cols], writes=[b_sx])
            S.emit("act", (lambda sb_t: lambda e: e.activation(
                out=sb_t[:], in_=sx[:], func=ACTF.Exp,
                bias=nm_, accum_out=z_))(sb_t),
                reads=[b_sx, c.b_cols], writes=[b_ssb[par], c.b_cols])

        def emit_B_tail(hl, qt, par):
            psrc, b_psrc = ssb[par], b_ssb[par]
            nkb = (qt // 4 + 1) if causal else 4
            vw = nkb * 512
            S.emit("dve", lambda e: e.reciprocal(rz_, z_),
                   reads=[c.b_cols], writes=[c.b_cols])
            S.emit("dve", (lambda hl, vw, psrc: lambda e: e.tensor_scalar(
                phi[hl][:, 0:vw], psrc[:, 0:vw], rz_, None,
                op0=ALU.mult))(hl, vw, psrc),
                reads=[b_psrc, c.b_cols], writes=[b_phi[hl]])
            S.emit("dve", (lambda hl, vw, psrc: lambda e: e.scalar_tensor_tensor(
                plo[hl][:, 0:vw], in0=psrc[:, 0:vw], scalar=rz_,
                in1=phi[hl][:, 0:vw],
                op0=ALU.mult, op1=ALU.subtract))(hl, vw, psrc),
                reads=[b_psrc, c.b_cols, b_phi[hl]], writes=[b_phi[hl]])

        def emit_C1(hl, qt):
            nkb = (qt // 4 + 1) if causal else 4
            nkc = 4 * nkb
            pod = po[0:64, 0:128] if hl == 0 else yw[64:128, 0:128]
            b_pod = b_po if hl == 0 else b_yw
            tp = (0, hl * 64)
            pvfirst = True
            for g in range((nkc + 7) // 8):
                kcs = list(range(g * 8, min(nkc, g * 8 + 8)))
                for srcs, trp, b_trp, dstt in ((phi, tr0, b_tr0, trb_hi),
                                               (plo, tr1, b_tr1, trb_lo)):
                    fns = [(lambda kc, src, trp: lambda e: e.transpose(
                        trp[:, (kc % 8) * 128:((kc % 8) + 1) * 128],
                        src[:, kc * 128:(kc + 1) * 128], c.idn_bf[:]))(
                        kc, srcs[hl], trp) for kc in kcs]
                    S.emit("pe", fns, reads=[b_phi[hl], c.b_consts],
                           writes=[b_trp])
                    dst = dstt[:, hl * 1024:hl * 1024 + len(kcs) * 128]
                    S.emit("act", (lambda dst, trp, n: lambda e: e.activation(
                        out=dst, in_=trp[:, 0:n * 128], func=ACTF.Copy))(
                        dst, trp, len(kcs)),
                        reads=[b_trp], writes=[b_trb])
                fns = []
                for kc in kcs:
                    lv_hi = v_hi[:, kc * 128 + hl * 64:kc * 128 + hl * 64 + 64]
                    lv_lo = v_lo[:, kc * 128 + hl * 64:kc * 128 + hl * 64 + 64]
                    col = hl * 1024 + (kc % 8) * 128
                    rp_hi = trb_hi[:, col:col + 128]
                    rp_lo = trb_lo[:, col:col + 128]
                    last = (g == (nkc + 7) // 8 - 1) and kc == kcs[-1]
                    fns.append((lambda pod, tp, lv, rp, st: lambda e: e.matmul(
                        pod, lv, rp, start=st, stop=False, tile_position=tp))(
                        pod, tp, lv_hi, rp_hi, pvfirst))
                    pvfirst = False
                    fns.append((lambda pod, tp, lv, rp: lambda e: e.matmul(
                        pod, lv, rp, start=False, stop=False,
                        tile_position=tp))(pod, tp, lv_hi, rp_lo))
                    fns.append((lambda pod, tp, lv, rp, sp: lambda e: e.matmul(
                        pod, lv, rp, start=False, stop=sp,
                        tile_position=tp))(pod, tp, lv_lo, rp_hi, last))
                S.emit("pe", fns, reads=[b_v, b_trb], writes=[b_pod])
            oh = otq_hi[hl * 64:hl * 64 + 64, :]
            S.emit("act", (lambda oh, pod: lambda e: e.activation(
                out=oh, in_=pod, func=ACTF.Copy))(oh, pod),
                reads=[b_pod], writes=[b_otq])

        def emit_C_otqlo(hl, qt):
            pod = po[0:64, 0:128] if hl == 0 else yw[64:128, 0:128]
            b_pod = b_po if hl == 0 else b_yw
            oh = otq_hi[hl * 64:hl * 64 + 64, :]
            ol = otq_lo[hl * 64:hl * 64 + 64, :]
            S.emit("dve", (lambda ol, pod, oh: lambda e: e.tensor_sub(
                ol, pod, oh))(ol, pod, oh),
                reads=[b_pod, b_otq], writes=[b_otq])

        def emit_C_wo(hl, qt):
            oh = otq_hi[hl * 64:hl * 64 + 64, :]
            ol = otq_lo[hl * 64:hl * 64 + 64, :]
            for dob in range(2):
                wdst = po[:, 0:512] if dob == 0 else yw[:, 0:512]
                b_wdst = b_po if dob == 0 else b_yw
                fns = []
                for lh, rh, st, sp in ((oh, wB_hi, True, False),
                                       (oh, wB_lo, False, False),
                                       (ol, wB_hi, False, True)):
                    rhs = rh[hl * 64:hl * 64 + 64, dob * 512:(dob + 1) * 512]
                    fns.append((lambda lh, rhs, st, sp, rp, wdst:
                                lambda e: e.matmul(
                        wdst, lh, rhs, start=st, stop=sp,
                        tile_position=(rp, 0)))(lh, rhs, st, sp,
                                                hl * 64, wdst))
                S.emit("pe", fns, reads=[b_otq, b_wB], writes=[b_wdst])

        def emit_C_xadd(hl, qt):
            for dob in range(2):
                wdst = po[:, 0:512] if dob == 0 else yw[:, 0:512]
                b_wdst = b_po if dob == 0 else b_yw
                xs = c.X[:, qt * D + dob * 512:qt * D + (dob + 1) * 512]
                S.emit("dve", (lambda xs, wdst: lambda e: e.tensor_add(
                    xs, xs, wdst))(xs, wdst),
                    reads=[b_wdst, c.b_x[qt]], writes=[c.b_x[qt]])

        units = [(hl, qt) for qt in range(NT) for hl in range(2)]
        n_u = len(units)
        for i in range(n_u + 2):
            for _ in range(3):
                if ppw:
                    ppw.pop(0)()
            u0 = units[i] if i < n_u else None
            u1 = units[i - 1] if 1 <= i <= n_u else None
            u2 = units[i - 2] if i >= 2 else None
            if causal:
                if u0:
                    emit_A(*u0, i % 2)
                if u2:
                    emit_C1(*u2)
                if u1:
                    emit_B_stats(*u1, (i - 1) % 2)
                if u2:
                    emit_C_otqlo(*u2)
                    emit_C_wo(*u2)
                    emit_C_xadd(*u2)
                if u1:
                    emit_B_tail(*u1, (i - 1) % 2)
            else:
                if u0:
                    emit_A(*u0, i % 2)
                if u2:
                    emit_C1(*u2)
                if u1:
                    emit_B_rounds(*u1, (i - 1) % 2, 0, 6)
                if u2:
                    emit_C_otqlo(*u2)
                    emit_C_wo(*u2)
                if u1:
                    emit_B_rounds(*u1, (i - 1) % 2, 6, 8)
                    emit_B_mask(*u1, (i - 1) % 2)
                if u2:
                    emit_C_xadd(*u2)
                if u1:
                    emit_B_tail(*u1, (i - 1) % 2)
    est.close()



def _prep_ffn_prepass_work(c, est):
    S, nc, dram = c.S, c.nc, c.dram
    W4 = 256
    st_ = [est.enter_context(nc.sbuf_tensor(f"pp_st{i}", [128, W4], F32))
           for i in range(2)]
    hi_ = [est.enter_context(nc.sbuf_tensor(f"pp_h{i}", [128, W4], BF16))
           for i in range(2)]
    lo_ = [est.enter_context(nc.sbuf_tensor(f"pp_l{i}", [128, W4], BF16))
           for i in range(2)]
    b_st = [S.buf("pp_st0"), S.buf("pp_st1")]
    b_hl = [S.buf("pp_hl0"), S.buf("pp_hl1")]

    def srcs(fam):
        if fam == "up":
            return dram["up_w"], dram["c_up_h"], dram["c_up_l"], NE
        if fam == "dn":
            return dram["down_w"], dram["c_dn_h"], dram["c_dn_l"], NE
        if fam == "shu":
            return dram["sh_up_w"], dram["c_shu_h"], dram["c_shu_l"], 1
        return dram["sh_down_w"], dram["c_shd_h"], dram["c_shd_l"], 1

    work = []

    def bias_job_k(k, p):
        cs = slice(k * W4, (k + 1) * W4)
        S.dma("pool", (lambda cs, p: lambda e_: e_.dma_start(
            out=st_[p][0:NE, 0:W4], in_=dram["up_b"][:, cs]))(cs, p),
            reads=[b_st[p]], writes=[b_st[p]], sem=f"dppi{p}")
        S.emit("act", (lambda p: lambda e_: e_.activation(
            out=hi_[p][0:NE, :], in_=st_[p][0:NE, :], func=ACTF.Copy))(p),
            reads=[b_st[p]], writes=[b_hl[p]])
        S.emit("dve", (lambda p: lambda e_: e_.tensor_sub(
            lo_[p][0:NE, :], st_[p][0:NE, :], hi_[p][0:NE, :]))(p),
            reads=[b_st[p], b_hl[p]], writes=[b_hl[p]])
        S.dma("pool", [
            (lambda p, cs: lambda e_: e_.dma_start(
                out=dram["c_ub_h"][:, cs], in_=hi_[p][0:NE, :]))(p, cs),
            (lambda p, cs: lambda e_: e_.dma_start(
                out=dram["c_ub_l"][:, cs], in_=lo_[p][0:NE, :]))(p, cs),
        ], reads=[b_hl[p]], writes=[b_hl[p], c.b_cache], sem=f"dppo{p}")

    def shb_job(p):
        fns = []
        for r, key in enumerate(("sh_up_b", "sh_down_b")):
            for k in range(4):
                fns.append((lambda r, k, key: lambda e_: e_.dma_start(
                    out=st_[p][4 * r + k:4 * r + k + 1, 0:W4],
                    in_=dram[key][:, k * W4:(k + 1) * W4]))(r, k, key))
        S.dma("pool", fns, reads=[b_st[p]], writes=[b_st[p]], sem=f"dppi{p}")
        S.emit("act", lambda e_: e_.activation(
            out=hi_[p][0:8, :], in_=st_[p][0:8, :], func=ACTF.Copy),
            reads=[b_st[p]], writes=[b_hl[p]])
        S.emit("dve", lambda e_: e_.tensor_sub(
            lo_[p][0:8, :], st_[p][0:8, :], hi_[p][0:8, :]),
            reads=[b_st[p], b_hl[p]], writes=[b_hl[p]])
        outs = [(dram["c_sub_h"], hi_[p]), (dram["c_sub_l"], lo_[p])]
        fns = []
        for dst, srct in outs:
            for r in range(2):
                for k in range(4):
                    fns.append((lambda dst, srct, r, k: lambda e_: e_.dma_start(
                        out=dst[r:r + 1, k * W4:(k + 1) * W4],
                        in_=srct[4 * r + k:4 * r + k + 1, :]))(dst, srct, r, k))
        S.dma("pool", fns, reads=[b_hl[p]],
              writes=[b_hl[p], c.b_cache], sem=f"dppo{p}")

    for k in range(4):
        work.append((lambda k: lambda: bias_job_k(k, k % 2))(k))
    work.append(lambda: shb_job(0))
    for fam in ("up", "dn", "shu", "shd"):
        srct, dh, dl, n = srcs(fam)
        for e in range(n):
            for dc in range(NC):
                for hf in range(4):
                    def job(srct=srct, dh=dh, dl=dl, e=e, dc=dc, hf=hf,
                            p=[len(work) % 2]):
                        p = p[0]
                        rows = srct[(e * NC + dc) * 128:(e * NC + dc + 1) * 128,
                                    hf * W4:(hf + 1) * W4]
                        S.dma("pool", (lambda rows, p: lambda e_: e_.dma_start(
                            out=st_[p][:], in_=rows))(rows, p),
                            reads=[b_st[p]], writes=[b_st[p]], sem=f"dppi{p}")
                        S.emit("act", (lambda p: lambda e_: e_.activation(
                            out=hi_[p][:], in_=st_[p][:], func=ACTF.Copy))(p),
                            reads=[b_st[p]], writes=[b_hl[p]])
                        S.emit("dve", (lambda p: lambda e_: e_.tensor_sub(
                            lo_[p][:], st_[p][:], hi_[p][:]))(p),
                            reads=[b_st[p], b_hl[p]], writes=[b_hl[p]])
                        ohi = dh[e * 128:(e + 1) * 128,
                                 dc * D + hf * W4:dc * D + (hf + 1) * W4]
                        olo = dl[e * 128:(e + 1) * 128,
                                 dc * D + hf * W4:dc * D + (hf + 1) * W4]
                        S.dma("pool", [
                            (lambda ohi, p: lambda e_: e_.dma_start(
                                out=ohi, in_=hi_[p][:]))(ohi, p),
                            (lambda olo, p: lambda e_: e_.dma_start(
                                out=olo, in_=lo_[p][:]))(olo, p),
                        ], reads=[b_hl[p]],
                            writes=[b_hl[p], c.b_cache], sem=f"dppo{p}")
                    work.append(job)
    return work



def _emit_ffn(c):
    c.ffn_n = getattr(c, "ffn_n", 0) + 1
    pfx = f"f{c.ffn_n}"
    S, nc, dram = c.S, c.nc, c.dram
    B = S.buf
    est = contextlib.ExitStack()

    def sb(name, shape, dt=F32):
        return est.enter_context(nc.sbuf_tensor(f"{pfx}_{name}", list(shape), dt))

    QT_TOK = 512
    QT_TILES = 4

    xn_hi = [sb("xnh0", (128, NC * QT_TOK), BF16),
             sb("xnh1", (128, NC * QT_TOK), BF16)]
    xn_lo = [sb("xnl0", (128, NC * QT_TOK), BF16),
             sb("xnl1", (128, NC * QT_TOK), BF16)]
    upw_hi = sb("uwh", (128, NC * D), BF16); upw_lo = sb("uwl", (128, NC * D), BF16)
    dnw_hi = sb("dwh", (128, NC * D), BF16); dnw_lo = sb("dwl", (128, NC * D), BF16)
    hp_hi = sb("hph", (128, QT_TILES * D), BF16)
    hp_lo = sb("hpl", (128, QT_TILES * D), BF16)
    ht_hi = [sb("hth0", (128, D), BF16), sb("hth1", (128, D), BF16)]
    ht_lo = [sb("htl0", (128, D), BF16), sb("htl1", (128, D), BF16)]
    sdump = sb("sdump", (128, D))
    gw_hi = sb("gwh", (128, 64), BF16); gw_lo = sb("gwl", (128, 64), BF16)
    gst = sb("gst", (128, 64))
    comb = [sb("comb0", (128, QT_TILES * 8)), sb("comb1", (128, QT_TILES * 8))]
    smn = sb("smn", (128, 8))
    ct_hi = [sb("cth0", (8, QT_TILES * 128), BF16),
             sb("cth1", (8, QT_TILES * 128), BF16)]
    ct_lo = [sb("ctl0", (8, QT_TILES * 128), BF16),
             sb("ctl1", (8, QT_TILES * 128), BF16)]
    db_hi = sb("dbh", (NE, D), BF16); db_lo = sb("dbl", (NE, D), BF16)
    ubs_h = sb("ubsh", (1, D), BF16); ubs_l = sb("ubsl", (1, D), BF16)

    ups = [est.enter_context(nc.psum_tensor(f"{pfx}_up{i}", [128, 1024], F32))
           for i in range(2)]
    trf = est.enter_context(nc.psum_tensor(f"{pfx}_trf", [128, 1024], BF16))
    gps = est.enter_context(nc.psum_tensor(f"{pfx}_gps", [128, 512], F32))
    yps = est.enter_context(nc.psum_tensor(f"{pfx}_yps", [128, 1024], F32))

    b_xn = [B("fxn0"), B("fxn1")]; b_upw = B("fupw"); b_dnw = B("fdnw")
    b_hp = B("fhp"); b_ht = [B("fht0"), B("fht1")]
    b_gw = B("fgw"); b_comb = [B("fcomb0"), B("fcomb1")]
    b_ct = [B("fct0"), B("fct1")]
    b_bias = B("fbias"); b_ubs = B("fubs")
    b_ups = [B("fup0"), B("fup1")]
    b_trf = B("ftrf"); b_gps = B("fgps"); b_yps = B("fyps")
    b_sdump = B("fsdump"); b_smn = B("fsmn")

    S.dma("sp", lambda e: e.dma_start(out=gst[:], in_=dram["gate_sw"][:]),
          writes=[b_gw], sem="dgw")
    S.emit("act", lambda e: e.activation(out=gw_hi[:], in_=gst[:], func=ACTF.Copy),
           reads=[b_gw], writes=[b_gw])
    S.emit("dve", lambda e: e.tensor_sub(gw_lo[:], gst[:], gw_hi[:]),
           reads=[b_gw], writes=[b_gw])
    def split_bias(src_hbm, n_rows, out_hi, out_lo, semn):
        S.dma("sp", (lambda: lambda e: e.dma_start(
            out=sdump[0:n_rows, :], in_=src_hbm))(),
            reads=[b_sdump], writes=[b_sdump], sem=semn)
        S.emit("act", (lambda: lambda e: e.activation(
            out=out_hi[0:n_rows, :], in_=sdump[0:n_rows, :], func=ACTF.Copy))(),
            reads=[b_sdump], writes=[b_bias])
        S.emit("dve", (lambda: lambda e: e.tensor_sub(
            out_lo[0:n_rows, :], sdump[0:n_rows, :], out_hi[0:n_rows, :]))(),
            reads=[b_sdump, b_bias], writes=[b_bias])

    split_bias(dram["down_b"][:], NE, db_hi, db_lo, "dfb")

    def load_w(cache_h, cache_l, erow, whi, wlo, b_w, semn):
        S.dma("sp", [
            (lambda: lambda e: e.dma_start(
                out=whi[:], in_=cache_h[erow * 128:(erow + 1) * 128, :]))(),
            (lambda: lambda e: e.dma_start(
                out=wlo[:], in_=cache_l[erow * 128:(erow + 1) * 128, :]))(),
        ], reads=[b_w, c.b_cache], writes=[b_w], sem=semn)

    m_, nm_, z_, rz_, b2_, s2_, r2_ = (c.cols[:, i:i + 1] for i in range(16, 23))

    def emit_ln_gate_tile(qtr, par, i):
        tt = qtr * QT_TILES + i
        _emit_ln_tile(c, tt, xn_hi[par], xn_lo[par], b_xn[par], i, QT_TOK,
                      gps, b_gps, sdump, b_sdump)
        fns = []
        for dc in range(NC):
            lh_hi = xn_hi[par][:, dc * QT_TOK + i * 128:dc * QT_TOK + (i + 1) * 128]
            lh_lo = xn_lo[par][:, dc * QT_TOK + i * 128:dc * QT_TOK + (i + 1) * 128]
            rh_hi = gw_hi[:, dc * 8:(dc + 1) * 8]
            rh_lo = gw_lo[:, dc * 8:(dc + 1) * 8]
            fns.append((lambda lh, rh, st: lambda e: e.matmul(
                gps[:, 0:8], lh, rh, start=st, stop=False))(lh_hi, rh_hi, dc == 0))
            fns.append((lambda lh, rh: lambda e: e.matmul(
                gps[:, 0:8], lh, rh, start=False, stop=False))(lh_hi, rh_lo))
            fns.append((lambda lh, rh, sp: lambda e: e.matmul(
                gps[:, 0:8], lh, rh, start=False, stop=sp))(
                lh_lo, rh_hi, dc == NC - 1))
        S.emit("pe", fns, reads=[b_xn[par], b_gw], writes=[b_gps])
        S.emit("dve", lambda e: e.reduce_max(m_, gps[:, 0:8], axis=AX),
               reads=[b_gps], writes=[c.b_cols])
        S.emit("dve", lambda e: e.tensor_scalar(nm_, m_, -1.0, None,
                                                op0=ALU.mult),
               reads=[c.b_cols], writes=[c.b_cols])
        S.emit("act", lambda e: e.activation(out=smn[:], in_=gps[:, 0:8],
                                             func=ACTF.Exp, bias=nm_,
                                             accum_out=z_),
               reads=[b_gps, c.b_cols], writes=[b_smn, c.b_cols])
        S.emit("dve", lambda e: e.reciprocal(rz_, z_),
               reads=[c.b_cols], writes=[c.b_cols])
        S.emit("act", lambda e: e.activation(out=smn[:], in_=smn[:],
                                             func=ACTF.Copy, scale=rz_),
               reads=[b_smn, c.b_cols], writes=[b_smn])
        S.emit("dve", lambda e: e.max(out=c.m8[:], in_=smn[:]),
               reads=[b_smn], writes=[c.b_m8])
        S.emit("dve", lambda e: e.tensor_add(s2_, c.m8[:, 0:1], c.m8[:, 1:2]),
               reads=[c.b_m8], writes=[c.b_cols])
        S.emit("dve", lambda e: e.tensor_scalar(s2_, s2_, 1e-20, None,
                                                op0=ALU.add),
               reads=[c.b_cols], writes=[c.b_cols])
        S.emit("dve", lambda e: e.reciprocal(r2_, s2_),
               reads=[c.b_cols], writes=[c.b_cols])
        cmb = comb[par][:, i * 8:(i + 1) * 8]
        S.emit("dve", (lambda cmb: lambda e: e.scalar_tensor_tensor(
            cmb, in0=smn[:], scalar=c.m8[:, 1:2], in1=smn[:],
            op0=ALU.is_ge, op1=ALU.mult))(cmb),
            reads=[b_smn, c.b_m8], writes=[b_comb[par]])
        S.emit("dve", (lambda cmb: lambda e: e.tensor_scalar(
            cmb, cmb, r2_, None, op0=ALU.mult))(cmb),
            reads=[b_comb[par], c.b_cols], writes=[b_comb[par]])

    def emit_combT(qtr, par):
        fns = [(lambda i, par: lambda e: e.transpose(
            gps[0:8, i * 128:(i + 1) * 128],
            comb[par][:, i * 8:(i + 1) * 8], c.idn_f32[:]))(i, par)
            for i in range(QT_TILES)]
        S.emit("pe", fns, reads=[b_comb[par], c.b_consts], writes=[b_gps])
        S.emit("act", (lambda par: lambda e: e.activation(
            out=ct_hi[par][:], in_=gps[0:8, 0:512], func=ACTF.Copy))(par),
            reads=[b_gps], writes=[b_ct[par]])
        S.emit("dve", (lambda par: lambda e: e.tensor_sub(
            ct_lo[par][:], gps[0:8, 0:512], ct_hi[par][:]))(par),
            reads=[b_gps, b_ct[par]], writes=[b_ct[par]])

    def emit_refill(e_next):
        if e_next == NE:
            hsrc, lsrc = dram["c_sub_h"][0:1, :], dram["c_sub_l"][0:1, :]
        elif e_next == NE + 1:
            hsrc, lsrc = dram["c_sub_h"][1:2, :], dram["c_sub_l"][1:2, :]
        else:
            hsrc = dram["c_ub_h"][e_next:e_next + 1, :]
            lsrc = dram["c_ub_l"][e_next:e_next + 1, :]
        S.dma("sp", [
            (lambda hsrc: lambda e_: e_.dma_start(
                out=ubs_h[0:1, :], in_=hsrc))(hsrc),
            (lambda lsrc: lambda e_: e_.dma_start(
                out=ubs_l[0:1, :], in_=lsrc))(lsrc),
        ], reads=[b_ubs, c.b_cache], writes=[b_ubs], sem="dub")

    for i in range(QT_TILES):
        emit_ln_gate_tile(0, 0, i)
    emit_combT(0, 0)

    for qtr in range(4):
        par = qtr % 2
        tts = range(qtr * QT_TILES, (qtr + 1) * QT_TILES)
        load_w(dram["c_up_h"], dram["c_up_l"], 0, upw_hi, upw_lo, b_upw, "dfu")
        emit_refill(0)
        for e in range(NE + 1):
            shared = e == NE
            if shared:
                load_w(dram["c_shd_h"], dram["c_shd_l"], 0, dnw_hi, dnw_lo,
                       b_dnw, "dfd")
            else:
                load_w(dram["c_dn_h"], dram["c_dn_l"], e, dnw_hi, dnw_lo,
                       b_dnw, "dfd")
            for i, tt in enumerate(tts):
                up = ups[i % 2]
                b_up = b_ups[i % 2]
                h32 = yps[:, 0:D]
                b_h32 = b_yps
                for dhb in range(2):
                    fns = []
                    dst = up[:, dhb * 512:(dhb + 1) * 512]
                    fns.append((lambda dst, dhb: lambda e_: e_.matmul(
                        dst, c.ones2[0:1, :], ubs_h[0:1, dhb * 512:(dhb + 1) * 512],
                        start=True, stop=False))(dst, dhb))
                    fns.append((lambda dst, dhb: lambda e_: e_.matmul(
                        dst, c.ones2[0:1, :], ubs_l[0:1, dhb * 512:(dhb + 1) * 512],
                        start=False, stop=False))(dst, dhb))
                    for dc in range(NC):
                        lh_hi = xn_hi[par][:, dc * QT_TOK + i * 128:
                                           dc * QT_TOK + (i + 1) * 128]
                        lh_lo = xn_lo[par][:, dc * QT_TOK + i * 128:
                                           dc * QT_TOK + (i + 1) * 128]
                        rh_hi = upw_hi[:, dc * D + dhb * 512:dc * D + (dhb + 1) * 512]
                        rh_lo = upw_lo[:, dc * D + dhb * 512:dc * D + (dhb + 1) * 512]
                        fns.append((lambda dst, lh, rh: lambda e_: e_.matmul(
                            dst, lh, rh, start=False, stop=False))(dst, lh_hi, rh_hi))
                        fns.append((lambda dst, lh, rh: lambda e_: e_.matmul(
                            dst, lh, rh, start=False, stop=False))(dst, lh_hi, rh_lo))
                        fns.append((lambda dst, lh, rh, sp: lambda e_: e_.matmul(
                            dst, lh, rh, start=False, stop=sp))(
                            dst, lh_lo, rh_hi, dc == NC - 1))
                    S.emit("pe", fns, reads=[b_xn[par], b_upw, b_ubs, c.b_consts],
                           writes=[b_up])
                    S.emit("act", (lambda up, dhb: lambda e_: e_.activation(
                        out=h32[:, dhb * 512:(dhb + 1) * 512],
                        in_=up[:, dhb * 512:(dhb + 1) * 512],
                        func=ACTF.Silu))(up, dhb),
                        reads=[b_up], writes=[b_h32])
                hh = hp_hi[:, i * D:(i + 1) * D]
                hl = hp_lo[:, i * D:(i + 1) * D]
                if shared:
                    S.emit("act", (lambda hh: lambda e_: e_.activation(
                        out=hh, in_=h32[:], func=ACTF.Copy))(hh),
                        reads=[b_h32], writes=[b_hp])
                    S.emit("dve", (lambda hl, hh: lambda e_: e_.tensor_sub(
                        hl, h32[:], hh))(hl, hh),
                        reads=[b_h32, b_hp], writes=[b_hp])
                else:
                    cc = comb[par][:, i * 8 + e:i * 8 + e + 1]
                    S.emit("act", (lambda hh, cc: lambda e_: e_.activation(
                        out=hh, in_=h32[:], func=ACTF.Copy, scale=cc))(hh, cc),
                        reads=[b_h32, b_comb[par]], writes=[b_hp])
                    S.emit("dve", (lambda hl, hh, cc: lambda e_:
                           e_.scalar_tensor_tensor(hl, in0=h32[:], scalar=cc,
                                                   in1=hh, op0=ALU.mult,
                                                   op1=ALU.subtract))(hl, hh, cc),
                           reads=[b_h32, b_comb[par], b_hp], writes=[b_hp])
            if e < NE:
                if e == NE - 1:
                    load_w(dram["c_shu_h"], dram["c_shu_l"], 0, upw_hi, upw_lo,
                           b_upw, "dfu")
                else:
                    load_w(dram["c_up_h"], dram["c_up_l"], e + 1, upw_hi, upw_lo,
                           b_upw, "dfu")
                emit_refill(e + 1)
            else:
                emit_refill(NE + 1)
            if qtr < 3 and 4 <= e <= 7:
                emit_ln_gate_tile(qtr + 1, (qtr + 1) % 2, e - 4)
            if qtr < 3 and e == NE:
                emit_combT(qtr + 1, (qtr + 1) % 2)
            def emit_tr(i):
                hh = hp_hi[:, i * D:(i + 1) * D]
                hl = hp_lo[:, i * D:(i + 1) * D]
                for src, dst_t in ((hh, ht_hi[i % 2]), (hl, ht_lo[i % 2])):
                    fns = [(lambda dc, src: lambda e_: e_.transpose(
                        trf[:, dc * 128:(dc + 1) * 128],
                        src[:, dc * 128:(dc + 1) * 128], c.idn_bf[:]))(dc, src)
                        for dc in range(NC)]
                    S.emit("pe", fns, reads=[b_hp, c.b_consts], writes=[b_trf])
                    S.emit("act", (lambda dst_t: lambda e_: e_.activation(
                        out=dst_t[:], in_=trf[:], func=ACTF.Copy))(dst_t),
                        reads=[b_trf], writes=[b_ht[i % 2]])

            def emit_down(i, tt):
                fns = []
                for dob in range(2):
                    dst = yps[:, dob * 512:(dob + 1) * 512]
                    first = True
                    if shared:
                        fns.append((lambda dst, dob: lambda e_: e_.matmul(
                            dst, c.ones2[0:1, :],
                            ubs_h[0:1, dob * 512:(dob + 1) * 512],
                            start=True, stop=False))(dst, dob))
                        fns.append((lambda dst, dob: lambda e_: e_.matmul(
                            dst, c.ones2[0:1, :],
                            ubs_l[0:1, dob * 512:(dob + 1) * 512],
                            start=False, stop=False))(dst, dob))
                        first = False
                        cth = ct_hi[par][:, i * 128:(i + 1) * 128]
                        ctl = ct_lo[par][:, i * 128:(i + 1) * 128]
                        dbh = db_hi[:, dob * 512:(dob + 1) * 512]
                        dbl = db_lo[:, dob * 512:(dob + 1) * 512]
                        for lh, rh in ((cth, dbh), (cth, dbl), (ctl, dbh)):
                            fns.append((lambda dst, lh, rh: lambda e_: e_.matmul(
                                dst, lh, rh, start=False, stop=False))(dst, lh, rh))
                    for dc in range(NC):
                        lh_hi = ht_hi[i % 2][:, dc * 128:(dc + 1) * 128]
                        lh_lo = ht_lo[i % 2][:, dc * 128:(dc + 1) * 128]
                        rh_hi = dnw_hi[:, dc * D + dob * 512:dc * D + (dob + 1) * 512]
                        rh_lo = dnw_lo[:, dc * D + dob * 512:dc * D + (dob + 1) * 512]
                        fns.append((lambda dst, lh, rh, st: lambda e_: e_.matmul(
                            dst, lh, rh, start=st, stop=False))(
                            dst, lh_hi, rh_hi, first and dc == 0))
                        fns.append((lambda dst, lh, rh: lambda e_: e_.matmul(
                            dst, lh, rh, start=False, stop=False))(dst, lh_hi, rh_lo))
                        fns.append((lambda dst, lh, rh, sp: lambda e_: e_.matmul(
                            dst, lh, rh, start=False, stop=sp))(
                            dst, lh_lo, rh_hi, dc == NC - 1))
                S.emit("pe", fns, reads=[b_ht[i % 2], b_dnw, b_ct[par], b_bias,
                                         b_ubs, c.b_consts],
                       writes=[b_yps])
                for dob in range(2):
                    xs = c.X[:, tt * D + dob * 512:tt * D + (dob + 1) * 512]
                    S.emit("dve", (lambda xs, dob: lambda e_: e_.tensor_add(
                        xs, xs, yps[:, dob * 512:(dob + 1) * 512]))(xs, dob),
                        reads=[b_yps, c.b_x[tt]], writes=[c.b_x[tt]])

            tl = list(tts)
            for i, tt in enumerate(tl):
                emit_tr(i)
                if i >= 1:
                    emit_down(i - 1, tl[i - 1])
            emit_down(len(tl) - 1, tl[-1])
    est.close()



def _prep_shared(inputs, pfx_map):
    f32 = np.float32
    bf = ml_dtypes.bfloat16
    d = {}
    for pfx, nm in pfx_map.items():
        wq = np.asarray(inputs[f"{nm}_Wq"], f32)
        wk = np.asarray(inputs[f"{nm}_Wk"], f32)
        wv = np.asarray(inputs[f"{nm}_Wv"], f32)
        wo = np.asarray(inputs[f"{nm}_Wo"], f32)

        def pan(w):
            a = w.T.reshape(8, 128, 8, 128).transpose(2, 1, 0, 3)
            return np.ascontiguousarray(a.reshape(D, D))

        d[f"{pfx}_wq"] = pan(wq)
        d[f"{pfx}_wk"] = pan(wk)
        d[f"{pfx}_wv"] = pan(wv)
        d[f"{pfx}_wo"] = np.ascontiguousarray(wo.T.reshape(D, D))
        d[f"{pfx}_bq"] = np.ascontiguousarray(
            np.asarray(inputs[f"{nm}_bq"], f32).reshape(8, 128).T)
        d[f"{pfx}_bk"] = np.ascontiguousarray(
            np.asarray(inputs[f"{nm}_bk"], f32).reshape(8, 128).T)
        d[f"{pfx}_bv"] = np.ascontiguousarray(
            np.asarray(inputs[f"{nm}_bv"], f32).reshape(8, 128).T)
        d[f"{pfx}_bo"] = np.asarray(inputs[f"{nm}_bo"], f32).reshape(1, D)
    d["up_w"] = np.ascontiguousarray(
        np.asarray(inputs["L0_ffn_up_w"], f32).reshape(NE * D, D))
    d["down_w"] = np.ascontiguousarray(
        np.asarray(inputs["L0_ffn_down_w"], f32).reshape(NE * D, D))
    d["sh_up_w"] = np.ascontiguousarray(np.asarray(inputs["L0_ffn_sh_up_w"], f32))
    d["sh_down_w"] = np.ascontiguousarray(np.asarray(inputs["L0_ffn_sh_down_w"], f32))
    d["up_b"] = np.ascontiguousarray(np.asarray(inputs["L0_ffn_up_b"], f32))
    d["down_b"] = np.ascontiguousarray(np.asarray(inputs["L0_ffn_down_b"], f32))
    d["sh_up_b"] = np.asarray(inputs["L0_ffn_sh_up_b"], f32).reshape(1, D)
    d["sh_down_b"] = np.asarray(inputs["L0_ffn_sh_down_b"], f32).reshape(1, D)
    gt = np.asarray(inputs["L0_ffn_gate_w"], f32).T
    d["gate_sw"] = np.ascontiguousarray(
        gt.reshape(8, 128, 8).transpose(1, 0, 2).reshape(128, 64))
    d["idn_bf"] = np.eye(128, dtype=bf)
    d["idn_f32"] = np.eye(128, dtype=f32)
    d["ones2"] = np.ones((2, 128), dtype=bf)
    d["masks4"] = _mask_variants()
    return d


def kernel(**inputs):
    pfx_map = {"a": "L0_attn_a", "b": "L0_attn_b"}
    shared = _prep_shared(inputs, pfx_map)
    x = np.ascontiguousarray(np.asarray(inputs["L0_attn_a_x"], np.float32))
    nc = build_program()
    in_maps = []
    for core in range(8):
        m = dict(shared)
        m["x"] = np.ascontiguousarray(x[core])
        in_maps.append(m)
    res = run_bass_kernel_spmd(nc, in_maps, core_ids=list(range(8)))
    out = np.stack([np.asarray(r["out"], np.float32) for r in res.results], axis=0)
    return out


# version 18
# speedup vs baseline: 1.0705x; 1.0705x over previous
import contextlib
import numpy as np
import ml_dtypes
import concourse.bass as bass
import concourse.mybir as mybir
from concourse.bass_utils import run_bass_kernel_spmd

F32 = mybir.dt.float32
BF16 = mybir.dt.bfloat16
AX = mybir.AxisListType.X
ALU = mybir.AluOpType
ACTF = mybir.ActivationFunctionType

T = 2048
D = 1024
NT = 16
NC = 8
H = 16
DH = 64
NPAIR = 8
NE = 8
TOPK = 64
NEG = -1.0e30

N_SUBLAYERS = 4


ENGINES = ("pe", "act", "dve", "pool", "sp")
_ENG_ATTR = {"pe": "tensor", "act": "scalar", "dve": "vector", "pool": "gpsimd",
             "sp": "sync"}


class Buf:
    __slots__ = ("name", "last_w", "readers")

    def __init__(self, name):
        self.name = name
        self.last_w = None
        self.readers = {}


class Sched:
    def __init__(self, nc):
        self.nc = nc
        self.sems = {}
        self.counts = {e: 0 for e in ENGINES}
        self.dma_counts = {}
        self.known = {e: {} for e in ENGINES}
        self.streams = {e: [] for e in ENGINES}
        self._sem_names = set()

    def buf(self, name):
        return Buf(name)

    def _need_waits(self, eng, reads, writes):
        req = {}
        for b in reads:
            if b.last_w is not None:
                s, v = b.last_w
                if req.get(s, -1) < v:
                    req[s] = v
        for b in writes:
            if b.last_w is not None:
                s, v = b.last_w
                if req.get(s, -1) < v:
                    req[s] = v
            for s, v in b.readers.items():
                if req.get(s, -1) < v:
                    req[s] = v
        out = []
        kn = self.known[eng]
        own = "s_" + eng
        for s, v in sorted(req.items()):
            if s == own and kn.get(s, -1) >= v:
                continue
            if kn.get(s, -1) < v:
                out.append((s, v))
                kn[s] = v
        return out

    def _record(self, sem, val, reads, writes):
        for b in reads:
            if b.readers.get(sem, -1) < val:
                b.readers[sem] = val
        for b in writes:
            b.last_w = (sem, val)
            b.readers = {}

    def emit(self, eng, fns, reads=(), writes=()):
        if callable(fns):
            fns = [fns]
        waits = self._need_waits(eng, reads, writes)
        sem = "s_" + eng
        self._sem_names.add(sem)
        self.counts[eng] += 1
        val = self.counts[eng]
        self.streams[eng].append(("op", fns, waits, sem, 1))
        self._record(sem, val, reads, writes)
        return val

    def dma(self, eng, fns, reads=(), writes=(), sem=None):
        if callable(fns):
            fns = [fns]
        waits = self._need_waits(eng, reads, writes)
        sem = sem if sem is not None else ("d_" + eng)
        self._sem_names.add(sem)
        self.dma_counts[sem] = self.dma_counts.get(sem, 0) + 16 * len(fns)
        val = self.dma_counts[sem]
        self.streams[eng].append(("dma", fns, waits, sem, 16))
        self._record(sem, val, reads, writes)
        return val

    def fence(self):
        snap_e = {e: self.counts[e] for e in ENGINES}
        snap_d = dict(self.dma_counts)
        for e in ENGINES:
            ws = [("s_" + o, snap_e[o]) for o in ENGINES
                  if o != e and snap_e[o] > 0]
            ws += [(s, v) for s, v in snap_d.items() if v > 0]
            for s, _ in ws:
                self._sem_names.add(s)
            if ws:
                self.streams[e].append(("waitonly", None, ws, None, 0))
                kn = self.known[e]
                for s, v in ws:
                    if kn.get(s, -1) < v:
                        kn[s] = v

    def drain_all(self, eng="pool"):
        ws = [("s_" + e, self.counts[e]) for e in ENGINES
              if e != eng and self.counts[e] > 0]
        ws += [(s, v) for s, v in self.dma_counts.items() if v > 0]
        for s, _ in ws:
            self._sem_names.add(s)
        self.streams[eng].append(("waitonly", None, ws, None, 0))

    def finalize(self, block, exit_stack):
        for s in sorted(self._sem_names):
            self.sems[s] = exit_stack.enter_context(self.nc.semaphore(s))

        def run_stream(name):
            def body(eng):
                for kind, fns, waits, sem, _amt in self.streams[name]:
                    for ws, wv in waits:
                        eng.wait_ge(self.sems[ws], wv)
                    if kind == "waitonly":
                        continue
                    insts = [fn(eng) for fn in fns]
                    if kind == "op":
                        insts[-1].then_inc(self.sems[sem], 1)
                    else:
                        for inst in insts:
                            inst.then_inc(self.sems[sem], 16)
            return body

        for e in ENGINES:
            if self.streams[e]:
                getattr(block, _ENG_ATTR[e])(run_stream(e))



def _mask_variants():
    r = np.arange(128)[:, None]
    j = np.arange(896)[None, :]
    return np.ascontiguousarray(
        np.where(j <= r + 384, 0.0, NEG)).astype(ml_dtypes.bfloat16)


class Ctx:
    pass


def build_program(n_sub=None):
    n_sub = N_SUBLAYERS if n_sub is None else n_sub
    nc = bass.Bass("TRN2", target_bir_lowering=False)
    c = Ctx()
    c.nc = nc
    S = Sched(nc)
    c.S = S
    est = contextlib.ExitStack()
    c.est = est

    dram = {}

    def din(name, shape, dt=F32):
        dram[name] = nc.dram_tensor(name, list(shape), dt, kind="ExternalInput").ap()
        return dram[name]

    din("x", (T, D))
    for p in ("a", "b"):
        din(f"{p}_wq", (D, D)); din(f"{p}_wk", (D, D)); din(f"{p}_wv", (D, D))
        din(f"{p}_wo", (D, D))
        din(f"{p}_bq", (128, 8)); din(f"{p}_bk", (128, 8)); din(f"{p}_bv", (128, 8))
        din(f"{p}_bo", (1, D))
    din("up_w", (NE * D, D)); din("down_w", (NE * D, D))
    din("sh_up_w", (D, D)); din("sh_down_w", (D, D))
    din("up_b", (NE, D)); din("down_b", (NE, D))
    din("sh_up_b", (1, D)); din("sh_down_b", (1, D))
    din("gate_sw", (128, 64))
    din("idn_bf", (128, 128), BF16)
    din("idn_f32", (128, 128))
    din("ones2", (2, 128), BF16)
    din("masks4", (128, 896), BF16)
    out_h = nc.dram_tensor("out", [T, D], F32, kind="ExternalOutput").ap()

    def dsc(name, shape, dt=BF16):
        t = nc.dram_tensor(name, list(shape), dt, kind="Internal")
        dram[name] = t.ap()
        return dram[name]

    dsc("c_up_h", (NE * 128, NC * D)); dsc("c_up_l", (NE * 128, NC * D))
    dsc("c_dn_h", (NE * 128, NC * D)); dsc("c_dn_l", (NE * 128, NC * D))
    dsc("c_shu_h", (128, NC * D)); dsc("c_shu_l", (128, NC * D))
    dsc("c_shd_h", (128, NC * D)); dsc("c_shd_l", (128, NC * D))
    dsc("c_ub_h", (NE, D)); dsc("c_ub_l", (NE, D))
    dsc("c_sub_h", (2, D)); dsc("c_sub_l", (2, D))
    c.dram = dram

    def sb(name, shape, dt=F32):
        return est.enter_context(nc.sbuf_tensor(name, list(shape), dt))

    c.X = sb("X", (128, NT * D))
    c.idn_bf = sb("idn_bf_t", (128, 128), BF16)
    c.idn_f32 = sb("idn_f32_t", (128, 128))
    c.ones2 = sb("ones2_t", (2, 128), BF16)
    c.cols = sb("cols", (128, 32))
    c.m8 = sb("m8t", (128, 8))

    B = S.buf
    c.b_x = [B(f"x{t}") for t in range(NT)]
    c.b_consts = B("consts")
    c.b_cols = B("cols")
    c.b_m8 = B("m8")
    c.b_out = [B(f"out{t}") for t in range(NT)]
    c.b_cache = B("wcache")

    for tt in range(NT):
        S.dma("sp", (lambda tt: lambda e: e.dma_start(
            out=c.X[:, tt * D:(tt + 1) * D],
            in_=dram["x"][tt * 128:(tt + 1) * 128, :]))(tt),
            writes=[c.b_x[tt]], sem=f"dx{tt}")
    S.dma("sp", [
        lambda e: e.dma_start(out=c.idn_bf[:], in_=dram["idn_bf"][:]),
        lambda e: e.dma_start(out=c.idn_f32[:], in_=dram["idn_f32"][:]),
        lambda e: e.dma_start(out=c.ones2[:], in_=dram["ones2"][:]),
    ], writes=[c.b_consts], sem="dconst")

    if n_sub >= 1:
        _emit_attn(c, "a", causal=True, prepass=(n_sub >= 2))
    if n_sub >= 2:
        S.fence()
        _emit_ffn(c)
    if n_sub >= 3:
        S.fence()
        _emit_attn(c, "b", causal=False, prepass=False)
    if n_sub >= 4:
        S.fence()
        _emit_ffn(c)

    for tt in range(NT):
        S.dma("sp", (lambda tt: lambda e: e.dma_start(
            out=out_h[tt * 128:(tt + 1) * 128, :],
            in_=c.X[:, tt * D:(tt + 1) * D]))(tt),
            reads=[c.b_x[tt]], writes=[c.b_out[tt]], sem=f"do{tt}")
    S.drain_all("pool")

    block = est.enter_context(nc.Block())
    S.finalize(block, est)
    est.close()
    return nc



def _emit_ln_tile(c, tt, xn_hi, xn_lo, b_xn, tloc, t_span, ps, b_ps, sdump, b_sdump):
    S, nc, X = c.S, c.nc, c.X
    xs = X[:, tt * D:(tt + 1) * D]
    su, sq, mn, nv, rs, nm = (c.cols[:, i:i + 1] for i in range(6))
    S.emit("dve", (lambda xs: lambda e: e.reduce_sum(su, xs, axis=AX))(xs),
           reads=[c.b_x[tt]], writes=[c.b_cols])
    S.emit("act", (lambda xs: lambda e: e.activation(
        out=sdump[:, 0:D], in_=xs, func=ACTF.Square, accum_out=sq))(xs),
        reads=[c.b_x[tt]], writes=[c.b_cols, b_sdump])
    S.emit("dve", lambda e: e.tensor_scalar(mn, su, 1.0 / D, None, op0=ALU.mult),
           reads=[c.b_cols], writes=[c.b_cols])
    S.emit("dve", lambda e: e.tensor_scalar(nv, sq, -1.0 / D, -1e-5,
                                            op0=ALU.mult, op1=ALU.add),
           reads=[c.b_cols], writes=[c.b_cols])
    S.emit("dve", lambda e: e.scalar_tensor_tensor(
        nv, in0=mn, scalar=mn, in1=nv, op0=ALU.mult, op1=ALU.add),
        reads=[c.b_cols], writes=[c.b_cols])
    S.emit("act", lambda e: e.activation(out=rs, in_=nv, func=ACTF.Sqrt,
                                         scale=-1.0),
           reads=[c.b_cols], writes=[c.b_cols])
    S.emit("dve", lambda e: e.reciprocal(rs, rs),
           reads=[c.b_cols], writes=[c.b_cols])
    S.emit("dve", lambda e: e.tensor_scalar(nm, mn, rs, -1.0,
                                            op0=ALU.mult, op1=ALU.mult),
           reads=[c.b_cols], writes=[c.b_cols])
    S.emit("act", (lambda xs: lambda e: e.activation(
        out=sdump[:, 0:D], in_=xs, func=ACTF.Identity, scale=rs, bias=nm))(xs),
        reads=[c.b_x[tt], c.b_cols], writes=[b_sdump])
    for half in range(2):
        fns = []
        for q in range(4):
            dc = half * 4 + q
            fns.append((lambda dc, q: lambda e: e.transpose(
                ps[:, q * 128:(q + 1) * 128],
                sdump[:, dc * 128:(dc + 1) * 128], c.idn_f32[:]))(dc, q))
        S.emit("pe", fns, reads=[b_sdump, c.b_consts], writes=[b_ps])
        dst_hi = xn_hi[:, :].rearrange("p (cc t) -> p cc t", t=t_span)[
            :, half * 4:half * 4 + 4, tloc * 128:(tloc + 1) * 128]
        dst_lo = xn_lo[:, :].rearrange("p (cc t) -> p cc t", t=t_span)[
            :, half * 4:half * 4 + 4, tloc * 128:(tloc + 1) * 128]
        src = ps[:, 0:512]
        S.emit("act", (lambda dst_hi, src: lambda e: e.activation(
            out=dst_hi, in_=src, func=ACTF.Copy))(dst_hi, src),
            reads=[b_ps], writes=[b_xn])
        S.emit("dve", (lambda dst_lo, src, dst_hi: lambda e: e.tensor_sub(
            dst_lo, src, dst_hi))(dst_lo, src, dst_hi),
            reads=[b_ps, b_xn], writes=[b_xn])



def _emit_attn(c, pfx, causal, prepass):
    S, nc, dram = c.S, c.nc, c.dram
    B = S.buf
    est = contextlib.ExitStack()

    def sb(name, shape, dt=F32):
        return est.enter_context(nc.sbuf_tensor(f"{pfx}_{name}", list(shape), dt))

    xn_hi = sb("xnh", (128, NC * T), BF16)
    xn_lo = sb("xnl", (128, NC * T), BF16)
    qt_hi = sb("qth", (128, T), BF16); qt_lo = sb("qtl", (128, T), BF16)
    kt_hi = sb("kth", (128, T), BF16); kt_lo = sb("ktl", (128, T), BF16)
    v_hi = sb("vh", (128, T), BF16); v_lo = sb("vl", (128, T), BF16)
    trb_hi = sb("trbh", (128, T), BF16); trb_lo = sb("trbl", (128, T), BF16)
    phi = [sb("phi0", (128, T), BF16), sb("phi1", (128, T), BF16)]
    plo = [sb("plo0", (128, T), BF16), sb("plo1", (128, T), BF16)]
    ssb = [sb("ssb0", (128, T)), sb("ssb1", (128, T))]
    if not causal:
        sx = sb("sx", (128, T))
    otq_hi = sb("otqh", (128, 128), BF16); otq_lo = sb("otql", (128, 128), BF16)
    if causal:
        c.masks = sb("masks", (128, 896), BF16)
        S.dma("sp", lambda e: e.dma_start(out=c.masks[:], in_=dram["masks4"][:]),
              writes=[c.b_consts], sem="dmask")
    big32 = ssb[0] if causal else sx
    b_big32 = None
    wst = sb("wst", (128, 512))
    wB_hi = sb("wBh", (128, D), BF16); wB_lo = sb("wBl", (128, D), BF16)
    bqkv = sb("bqkv", (128, 24))

    ps = est.enter_context(nc.psum_tensor(f"{pfx}_ps", [128, 2048], F32))
    tr0 = est.enter_context(nc.psum_tensor(f"{pfx}_tr0", [128, 1024], BF16))
    tr1 = est.enter_context(nc.psum_tensor(f"{pfx}_tr1", [128, 1024], BF16))
    po = est.enter_context(nc.psum_tensor(f"{pfx}_po", [128, 512], F32))
    yw = est.enter_context(nc.psum_tensor(f"{pfx}_yw", [128, 512], F32))

    b_xn = B("xn"); b_qt = B("qt"); b_kt = B("kt"); b_v = B("v")
    b_trb = B("trb"); b_phi = [B("phi0"), B("phi1")]
    b_psb = [B("psb0"), B("psb1"), B("psb2"), B("psb3")]
    b_ssb = [B("ssb0"), B("ssb1")]; b_sx = B("sx")
    b_big32 = b_ssb[0] if causal else b_sx
    bsp_h = phi[0][0:1, 0:D]
    bsp_l = phi[0][0:1, D:2 * D]
    b_bsp_ali = b_phi[0]
    b_otq = B("otq"); b_wst = B("wst")
    b_wB = B("wB"); b_bsp = B("bsp"); b_bqkv = B("bqkv")
    b_ps = B("ps"); b_tr0 = B("tr0"); b_tr1 = B("tr1")
    b_po = B("po"); b_yw = B("yw")
    sdump = big32
    b_sdump = b_big32

    ppw = _prep_ffn_prepass_work(c, est) if prepass else []
    for tt in range(NT):
        _emit_ln_tile(c, tt, xn_hi, xn_lo, b_xn, tt, T, ps, b_psb[0],
                      sdump, b_sdump)

    S.dma("sp", lambda e: e.dma_start(
        out=bqkv[:, 0:8], in_=dram[f"{pfx}_bq"][:]), writes=[b_bqkv], sem="dbias")
    S.dma("sp", lambda e: e.dma_start(
        out=bqkv[:, 8:16], in_=dram[f"{pfx}_bk"][:]), writes=[b_bqkv], sem="dbias")
    S.dma("sp", lambda e: e.dma_start(
        out=bqkv[:, 16:24], in_=dram[f"{pfx}_bv"][:]), writes=[b_bqkv], sem="dbias")
    S.dma("sp", lambda e: e.dma_start(out=big32[0:1, 0:D], in_=dram[f"{pfx}_bo"][:]),
          reads=[b_big32], writes=[b_big32], sem="dbias2")
    S.emit("act", lambda e: e.activation(out=bsp_h, in_=big32[0:1, 0:D],
                                         func=ACTF.Copy),
           reads=[b_big32], writes=[b_bsp_ali])
    S.emit("dve", lambda e: e.tensor_sub(bsp_l, big32[0:1, 0:D], bsp_h),
           reads=[b_big32, b_bsp_ali], writes=[b_bsp_ali])

    ones1 = c.ones2[0:1, :]
    for tt in range(NT):
        for dob in range(2):
            S.emit("pe", [
                (lambda dob: lambda e: e.matmul(
                    yw[:, :], ones1, bsp_h[0:1, dob * 512:(dob + 1) * 512],
                    start=True, stop=False))(dob),
                (lambda dob: lambda e: e.matmul(
                    yw[:, :], ones1, bsp_l[0:1, dob * 512:(dob + 1) * 512],
                    start=False, stop=True))(dob),
            ], reads=[c.b_consts, b_bsp_ali], writes=[b_yw])
            xs = c.X[:, tt * D + dob * 512: tt * D + (dob + 1) * 512]
            S.emit("dve", (lambda xs: lambda e: e.tensor_add(xs, xs, yw[:, :]))(xs),
                   reads=[b_yw, c.b_x[tt]], writes=[c.b_x[tt]])

    def load_split_w(hbm_rows, whi, wlo, b_w, semn, wscale=1.0):
        for h2 in range(2):
            cols = slice(h2 * 512, (h2 + 1) * 512)
            S.dma("sp", (lambda hbm_rows, cols: lambda e: e.dma_start(
                out=wst[:, :], in_=hbm_rows[:, cols]))(hbm_rows, cols),
                reads=[b_wst], writes=[b_wst], sem=semn)
            S.emit("act", (lambda whi, cols, wscale: lambda e: e.activation(
                out=whi[:, cols], in_=wst[:, :], func=ACTF.Copy,
                scale=wscale))(whi, cols, wscale),
                reads=[b_wst], writes=[b_w])
            S.emit("dve", (lambda whi, wlo, cols, wscale: lambda e:
                   e.scalar_tensor_tensor(
                wlo[:, cols], in0=wst[:, :], scalar=wscale, in1=whi[:, cols],
                op0=ALU.mult, op1=ALU.subtract))(whi, wlo, cols, wscale),
                reads=[b_wst, b_w], writes=[b_w])

    NKC = 16

    for pair in range(NPAIR):
        for mat, dsth, dstl, b_dst, bias_off in (
                ("wq", qt_hi, qt_lo, b_qt, 0),
                ("wk", kt_hi, kt_lo, b_kt, 8),
                ("wv", trb_hi, trb_lo, b_trb, 16)):
            whi, wlo, b_w = wB_hi, wB_lo, b_wB
            load_split_w(dram[f"{pfx}_{mat}"][pair * 128:(pair + 1) * 128, :],
                         whi, wlo, b_w, "dwp",
                         wscale=0.125 if mat == "wq" else 1.0)
            bias_col = bqkv[:, bias_off + pair:bias_off + pair + 1]
            if bias_off == 0:
                bsc = c.cols[:, 11:12]
                S.emit("dve", (lambda bias_col, bsc: lambda e: e.tensor_scalar(
                    bsc, bias_col, 0.125, None, op0=ALU.mult))(bias_col, bsc),
                    reads=[b_bqkv], writes=[c.b_cols])
                bias_col = bsc
            for tb in range(4):
                fns = []
                rhs_hi = [xn_hi[:, dc * T + tb * 512:dc * T + (tb + 1) * 512]
                          for dc in range(NC)]
                rhs_lo = [xn_lo[:, dc * T + tb * 512:dc * T + (tb + 1) * 512]
                          for dc in range(NC)]
                dst = ps[:, tb * 512:(tb + 1) * 512]
                for dc in range(NC):
                    lh = whi[:, dc * 128:(dc + 1) * 128]
                    ll = wlo[:, dc * 128:(dc + 1) * 128]
                    fns.append((lambda dst, lh, rh, st: lambda e: e.matmul(
                        dst, lh, rh, start=st, stop=False))(
                        dst, lh, rhs_hi[dc], dc == 0))
                    fns.append((lambda dst, lh, rh: lambda e: e.matmul(
                        dst, lh, rh, start=False, stop=False))(dst, lh, rhs_lo[dc]))
                    fns.append((lambda dst, ll, rh, sp: lambda e: e.matmul(
                        dst, ll, rh, start=False, stop=sp))(
                        dst, ll, rhs_hi[dc], dc == NC - 1))
                S.emit("pe", fns, reads=[b_w, b_xn], writes=[b_psb[tb]])
                src_ = ps[:, tb * 512:(tb + 1) * 512]
                dh = dsth[:, tb * 512:(tb + 1) * 512]
                dl = dstl[:, tb * 512:(tb + 1) * 512]
                S.emit("act", (lambda dh, src, bias_col: lambda e: e.activation(
                    out=dh, in_=src, func=ACTF.Identity, bias=bias_col))(dh, src_, bias_col),
                    reads=[b_psb[tb], b_bqkv, c.b_cols], writes=[b_dst])
                S.emit("dve", (lambda dl, src, dh, bias_col: lambda e:
                       e.scalar_tensor_tensor(dl, in0=src, scalar=bias_col, in1=dh,
                                              op0=ALU.add, op1=ALU.subtract))(
                       dl, src_, dh, bias_col),
                       reads=[b_psb[tb], b_bqkv, b_dst, c.b_cols], writes=[b_dst])

        for half in range(2):
            for src, dstt, b_pt in ((trb_hi, v_hi, b_v), (trb_lo, v_lo, b_v)):
                trp = tr0 if src is trb_hi else tr1
                b_trp = b_tr0 if src is trb_hi else b_tr1
                fns = [(lambda kc, src, trp: lambda e: e.transpose(
                    trp[:, (kc % 8) * 128:((kc % 8) + 1) * 128],
                    src[:, kc * 128:(kc + 1) * 128], c.idn_bf[:]))(kc, src, trp)
                    for kc in range(half * 8, half * 8 + 8)]
                S.emit("pe", fns, reads=[b_trb, c.b_consts], writes=[b_trp])
                dst = dstt[:, half * 1024:(half + 1) * 1024]
                S.emit("act", (lambda dst, trp: lambda e: e.activation(
                    out=dst, in_=trp[:, :], func=ACTF.Copy))(dst, trp),
                    reads=[b_trp], writes=[b_pt])

        load_split_w(dram[f"{pfx}_wo"][pair * 128:(pair + 1) * 128, :],
                     wB_hi, wB_lo, b_wB, "dwp")
        m_, nm_, z_, rz_, t64_ = (c.cols[:, i:i + 1] for i in range(6, 11))
        zp = c.cols[:, 12:16]

        def emit_A(hl, qt, par):
            nkb = (qt // 4 + 1) if causal else 4
            for kb in range(nkb):
                fns = []
                dst = ps[:, kb * 512:(kb + 1) * 512]
                first = True
                if causal and kb == qt // 4:
                    mv = qt % 4
                    fns.append((lambda dst, mv: lambda e: e.matmul(
                        dst, c.idn_bf[:],
                        c.masks[:, 384 - 128 * mv:896 - 128 * mv],
                        start=True, stop=False))(dst, mv))
                    first = False
                lq_hi = qt_hi[hl * 64:hl * 64 + 64, qt * 128:(qt + 1) * 128]
                lq_lo = qt_lo[hl * 64:hl * 64 + 64, qt * 128:(qt + 1) * 128]
                rk_hi = kt_hi[hl * 64:hl * 64 + 64, kb * 512:(kb + 1) * 512]
                rk_lo = kt_lo[hl * 64:hl * 64 + 64, kb * 512:(kb + 1) * 512]
                tp = (hl * 64, 0)
                fns.append((lambda dst, lq, rk, st, tp: lambda e: e.matmul(
                    dst, lq, rk, start=st, stop=False, tile_position=tp))(
                    dst, lq_hi, rk_hi, first, tp))
                fns.append((lambda dst, lq, rk, tp: lambda e: e.matmul(
                    dst, lq, rk, start=False, stop=False, tile_position=tp))(
                    dst, lq_hi, rk_lo, tp))
                fns.append((lambda dst, lq, rk, tp: lambda e: e.matmul(
                    dst, lq, rk, start=False, stop=True, tile_position=tp))(
                    dst, lq_lo, rk_hi, tp))
                S.emit("pe", fns, reads=[b_qt, b_kt, c.b_consts],
                       writes=[b_psb[kb]])
                S.emit("act", (lambda kb, par: lambda e: e.activation(
                    out=ssb[par][:, kb * 512:(kb + 1) * 512],
                    in_=ps[:, kb * 512:(kb + 1) * 512],
                    func=ACTF.Copy))(kb, par),
                    reads=[b_psb[kb]], writes=[b_ssb[par]])

        def emit_B_stats(hl, qt, par):
            nkb = qt // 4 + 1
            sb_t = ssb[par]
            S.emit("dve", (lambda nkb, sb_t: lambda e: e.reduce_max(
                m_, sb_t[:, 0:nkb * 512], axis=AX))(nkb, sb_t),
                reads=[b_ssb[par]], writes=[c.b_cols])
            S.emit("dve", lambda e: e.tensor_scalar(
                nm_, m_, -1.0, None, op0=ALU.mult),
                reads=[c.b_cols], writes=[c.b_cols])
            S.emit("act", (lambda nkb, sb_t: lambda e: e.activation(
                out=sb_t[:, 0:nkb * 512], in_=sb_t[:, 0:nkb * 512],
                func=ACTF.Exp, bias=nm_,
                accum_out=z_))(nkb, sb_t),
                reads=[b_ssb[par], c.b_cols], writes=[b_ssb[par], c.b_cols])

        def emit_B_rounds(hl, qt, par, r0, r1):
            sb_t = ssb[par]
            for r in range(r0, r1):
                src_t, src_b = (sb_t, b_ssb[par]) if r == 0 else (sx, b_sx)
                S.emit("dve", (lambda src_t: lambda e: e.max(
                    out=c.m8[:], in_=src_t[:]))(src_t),
                    reads=[src_b], writes=[c.b_m8])
                if r == 0:
                    S.emit("dve", lambda e: e.tensor_scalar(
                        nm_, c.m8[:, 0:1], -1.0, 0.0,
                        op0=ALU.mult, op1=ALU.min),
                        reads=[c.b_m8], writes=[c.b_cols])
                if r == 7:
                    continue
                S.emit("dve", (lambda src_t: lambda e: e.match_replace(
                    out=sx[:], in_to_replace=c.m8[:], in_values=src_t[:],
                    imm_value=NEG))(src_t),
                    reads=[src_b, c.b_m8], writes=[b_sx])

        def emit_B_mask(hl, qt, par):
            sb_t = ssb[par]
            S.emit("dve", lambda e: e.tensor_copy(t64_, c.m8[:, 7:8]),
                   reads=[c.b_m8], writes=[c.b_cols])
            S.emit("dve", (lambda sb_t: lambda e: e.scalar_tensor_tensor(
                sx[:], in0=sb_t[:], scalar=t64_, in1=sb_t[:],
                op0=ALU.is_ge, op1=ALU.mult))(sb_t),
                reads=[b_ssb[par], c.b_cols], writes=[b_sx])
            S.emit("act", (lambda sb_t: lambda e: e.activation(
                out=sb_t[:], in_=sx[:], func=ACTF.Exp,
                bias=nm_, accum_out=z_))(sb_t),
                reads=[b_sx, c.b_cols], writes=[b_ssb[par], c.b_cols])

        def emit_B_tail(hl, qt, par):
            psrc, b_psrc = ssb[par], b_ssb[par]
            nkb = (qt // 4 + 1) if causal else 4
            vw = nkb * 512
            S.emit("dve", lambda e: e.reciprocal(rz_, z_),
                   reads=[c.b_cols], writes=[c.b_cols])
            S.emit("dve", (lambda hl, vw, psrc: lambda e: e.tensor_scalar(
                phi[hl][:, 0:vw], psrc[:, 0:vw], rz_, None,
                op0=ALU.mult))(hl, vw, psrc),
                reads=[b_psrc, c.b_cols], writes=[b_phi[hl]])
            S.emit("dve", (lambda hl, vw, psrc: lambda e: e.scalar_tensor_tensor(
                plo[hl][:, 0:vw], in0=psrc[:, 0:vw], scalar=rz_,
                in1=phi[hl][:, 0:vw],
                op0=ALU.mult, op1=ALU.subtract))(hl, vw, psrc),
                reads=[b_psrc, c.b_cols, b_phi[hl]], writes=[b_phi[hl]])

        def emit_C1(hl, qt):
            nkb = (qt // 4 + 1) if causal else 4
            nkc = 4 * nkb
            pod = po[0:64, 0:128] if hl == 0 else yw[64:128, 0:128]
            b_pod = b_po if hl == 0 else b_yw
            tp = (0, hl * 64)
            pvfirst = True
            for g in range((nkc + 7) // 8):
                kcs = list(range(g * 8, min(nkc, g * 8 + 8)))
                for srcs, trp, b_trp, dstt in ((phi, tr0, b_tr0, trb_hi),
                                               (plo, tr1, b_tr1, trb_lo)):
                    fns = [(lambda kc, src, trp: lambda e: e.transpose(
                        trp[:, (kc % 8) * 128:((kc % 8) + 1) * 128],
                        src[:, kc * 128:(kc + 1) * 128], c.idn_bf[:]))(
                        kc, srcs[hl], trp) for kc in kcs]
                    S.emit("pe", fns, reads=[b_phi[hl], c.b_consts],
                           writes=[b_trp])
                    dst = dstt[:, hl * 1024:hl * 1024 + len(kcs) * 128]
                    S.emit("act", (lambda dst, trp, n: lambda e: e.activation(
                        out=dst, in_=trp[:, 0:n * 128], func=ACTF.Copy))(
                        dst, trp, len(kcs)),
                        reads=[b_trp], writes=[b_trb])
                fns = []
                for kc in kcs:
                    lv_hi = v_hi[:, kc * 128 + hl * 64:kc * 128 + hl * 64 + 64]
                    lv_lo = v_lo[:, kc * 128 + hl * 64:kc * 128 + hl * 64 + 64]
                    col = hl * 1024 + (kc % 8) * 128
                    rp_hi = trb_hi[:, col:col + 128]
                    rp_lo = trb_lo[:, col:col + 128]
                    last = (g == (nkc + 7) // 8 - 1) and kc == kcs[-1]
                    fns.append((lambda pod, tp, lv, rp, st: lambda e: e.matmul(
                        pod, lv, rp, start=st, stop=False, tile_position=tp))(
                        pod, tp, lv_hi, rp_hi, pvfirst))
                    pvfirst = False
                    fns.append((lambda pod, tp, lv, rp: lambda e: e.matmul(
                        pod, lv, rp, start=False, stop=False,
                        tile_position=tp))(pod, tp, lv_hi, rp_lo))
                    fns.append((lambda pod, tp, lv, rp, sp: lambda e: e.matmul(
                        pod, lv, rp, start=False, stop=sp,
                        tile_position=tp))(pod, tp, lv_lo, rp_hi, last))
                S.emit("pe", fns, reads=[b_v, b_trb], writes=[b_pod])
            oh = otq_hi[hl * 64:hl * 64 + 64, :]
            S.emit("act", (lambda oh, pod: lambda e: e.activation(
                out=oh, in_=pod, func=ACTF.Copy))(oh, pod),
                reads=[b_pod], writes=[b_otq])

        def emit_C_otqlo(hl, qt):
            pod = po[0:64, 0:128] if hl == 0 else yw[64:128, 0:128]
            b_pod = b_po if hl == 0 else b_yw
            oh = otq_hi[hl * 64:hl * 64 + 64, :]
            ol = otq_lo[hl * 64:hl * 64 + 64, :]
            S.emit("dve", (lambda ol, pod, oh: lambda e: e.tensor_sub(
                ol, pod, oh))(ol, pod, oh),
                reads=[b_pod, b_otq], writes=[b_otq])

        def emit_C_wo(hl, qt):
            oh = otq_hi[hl * 64:hl * 64 + 64, :]
            ol = otq_lo[hl * 64:hl * 64 + 64, :]
            for dob in range(2):
                wdst = po[:, 0:512] if dob == 0 else yw[:, 0:512]
                b_wdst = b_po if dob == 0 else b_yw
                fns = []
                for lh, rh, st, sp in ((oh, wB_hi, True, False),
                                       (oh, wB_lo, False, False),
                                       (ol, wB_hi, False, True)):
                    rhs = rh[hl * 64:hl * 64 + 64, dob * 512:(dob + 1) * 512]
                    fns.append((lambda lh, rhs, st, sp, rp, wdst:
                                lambda e: e.matmul(
                        wdst, lh, rhs, start=st, stop=sp,
                        tile_position=(rp, 0)))(lh, rhs, st, sp,
                                                hl * 64, wdst))
                S.emit("pe", fns, reads=[b_otq, b_wB], writes=[b_wdst])

        def emit_C_xadd(hl, qt):
            for dob in range(2):
                wdst = po[:, 0:512] if dob == 0 else yw[:, 0:512]
                b_wdst = b_po if dob == 0 else b_yw
                xs = c.X[:, qt * D + dob * 512:qt * D + (dob + 1) * 512]
                S.emit("dve", (lambda xs, wdst: lambda e: e.tensor_add(
                    xs, xs, wdst))(xs, wdst),
                    reads=[b_wdst, c.b_x[qt]], writes=[c.b_x[qt]])

        units = [(hl, qt) for qt in range(NT) for hl in range(2)]
        n_u = len(units)
        for i in range(n_u + 2):
            for _ in range(2):
                if ppw:
                    ppw.pop(0)()
            u0 = units[i] if i < n_u else None
            u1 = units[i - 1] if 1 <= i <= n_u else None
            u2 = units[i - 2] if i >= 2 else None
            if causal:
                if u0:
                    emit_A(*u0, i % 2)
                if u2:
                    emit_C1(*u2)
                if u1:
                    emit_B_stats(*u1, (i - 1) % 2)
                if u2:
                    emit_C_otqlo(*u2)
                    emit_C_wo(*u2)
                    emit_C_xadd(*u2)
                if u1:
                    emit_B_tail(*u1, (i - 1) % 2)
            else:
                if u0:
                    emit_A(*u0, i % 2)
                if u2:
                    emit_C1(*u2)
                if u1:
                    emit_B_rounds(*u1, (i - 1) % 2, 0, 6)
                if u2:
                    emit_C_otqlo(*u2)
                    emit_C_wo(*u2)
                if u1:
                    emit_B_rounds(*u1, (i - 1) % 2, 6, 8)
                    emit_B_mask(*u1, (i - 1) % 2)
                if u2:
                    emit_C_xadd(*u2)
                if u1:
                    emit_B_tail(*u1, (i - 1) % 2)
    est.close()



def _prep_ffn_prepass_work(c, est):
    S, nc, dram = c.S, c.nc, c.dram
    W4 = 512
    st0 = est.enter_context(nc.sbuf_tensor("pp_st0", [128, W4], F32))
    st_ = [st0, st0]
    hi_ = [est.enter_context(nc.sbuf_tensor(f"pp_h{i}", [128, W4], BF16))
           for i in range(2)]
    lo_ = [est.enter_context(nc.sbuf_tensor(f"pp_l{i}", [128, W4], BF16))
           for i in range(2)]
    b_st = S.buf("pp_st")
    b_hl = [S.buf("pp_hl0"), S.buf("pp_hl1")]

    def srcs(fam):
        if fam == "up":
            return dram["up_w"], dram["c_up_h"], dram["c_up_l"], NE
        if fam == "dn":
            return dram["down_w"], dram["c_dn_h"], dram["c_dn_l"], NE
        if fam == "shu":
            return dram["sh_up_w"], dram["c_shu_h"], dram["c_shu_l"], 1
        return dram["sh_down_w"], dram["c_shd_h"], dram["c_shd_l"], 1

    work = []

    def bias_job():
        for p in range(2):
            cs = slice(p * (D // 2), (p + 1) * (D // 2))
            S.dma("pool", (lambda cs: lambda e_: e_.dma_start(
                out=st_[0][0:NE, 0:D // 2], in_=dram["up_b"][:, cs]))(cs),
                reads=[b_st], writes=[b_st], sem="dppi")
            S.emit("act", (lambda p: lambda e_: e_.activation(
                out=hi_[p][0:NE, :], in_=st_[p][0:NE, :], func=ACTF.Copy))(p),
                reads=[b_st], writes=[b_hl[p]])
            S.emit("dve", (lambda p: lambda e_: e_.tensor_sub(
                lo_[p][0:NE, :], st_[p][0:NE, :], hi_[p][0:NE, :]))(p),
                reads=[b_st, b_hl[p]], writes=[b_hl[p]])
            S.dma("pool", [
                (lambda p, cs: lambda e_: e_.dma_start(
                    out=dram["c_ub_h"][:, cs], in_=hi_[p][0:NE, :]))(p, cs),
                (lambda p, cs: lambda e_: e_.dma_start(
                    out=dram["c_ub_l"][:, cs], in_=lo_[p][0:NE, :]))(p, cs),
            ], reads=[b_hl[p]], writes=[b_hl[p], c.b_cache], sem=f"dppo{p}")

    def shb_job():
        S.dma("pool", [
            lambda e_: e_.dma_start(out=st_[0][0:1, 0:D // 2],
                                    in_=dram["sh_up_b"][:, 0:D // 2]),
            lambda e_: e_.dma_start(out=st_[0][1:2, 0:D // 2],
                                    in_=dram["sh_up_b"][:, D // 2:D]),
            lambda e_: e_.dma_start(out=st_[0][2:3, 0:D // 2],
                                    in_=dram["sh_down_b"][:, 0:D // 2]),
            lambda e_: e_.dma_start(out=st_[0][3:4, 0:D // 2],
                                    in_=dram["sh_down_b"][:, D // 2:D]),
        ], reads=[b_st], writes=[b_st], sem="dppi")
        S.emit("act", lambda e_: e_.activation(
            out=hi_[0][0:4, :], in_=st_[0][0:4, :], func=ACTF.Copy),
            reads=[b_st], writes=[b_hl[0]])
        S.emit("dve", lambda e_: e_.tensor_sub(
            lo_[0][0:4, :], st_[0][0:4, :], hi_[0][0:4, :]),
            reads=[b_st, b_hl[0]], writes=[b_hl[0]])
        outs = [(dram["c_sub_h"], hi_[0]), (dram["c_sub_l"], lo_[0])]
        fns = []
        for dst, srct in outs:
            for r in range(2):
                for hf in range(2):
                    fns.append((lambda dst, srct, r, hf: lambda e_: e_.dma_start(
                        out=dst[r:r + 1, hf * (D // 2):(hf + 1) * (D // 2)],
                        in_=srct[2 * r + hf:2 * r + hf + 1, :]))(
                        dst, srct, r, hf))
        S.dma("pool", fns, reads=[b_hl[0]],
              writes=[b_hl[0], c.b_cache], sem="dppo0")

    work.append(bias_job)
    work.append(shb_job)
    for fam in ("up", "dn", "shu", "shd"):
        srct, dh, dl, n = srcs(fam)
        for e in range(n):
            for dc in range(NC):
                for hf in range(2):
                    def job(srct=srct, dh=dh, dl=dl, e=e, dc=dc, hf=hf,
                            p=[len(work) % 2]):
                        p = p[0]
                        rows = srct[(e * NC + dc) * 128:(e * NC + dc + 1) * 128,
                                    hf * W4:(hf + 1) * W4]
                        S.dma("pool", (lambda rows, p: lambda e_: e_.dma_start(
                            out=st_[p][:], in_=rows))(rows, p),
                            reads=[b_st], writes=[b_st], sem="dppi")
                        S.emit("act", (lambda p: lambda e_: e_.activation(
                            out=hi_[p][:], in_=st_[p][:], func=ACTF.Copy))(p),
                            reads=[b_st], writes=[b_hl[p]])
                        S.emit("dve", (lambda p: lambda e_: e_.tensor_sub(
                            lo_[p][:], st_[p][:], hi_[p][:]))(p),
                            reads=[b_st, b_hl[p]], writes=[b_hl[p]])
                        ohi = dh[e * 128:(e + 1) * 128,
                                 dc * D + hf * W4:dc * D + (hf + 1) * W4]
                        olo = dl[e * 128:(e + 1) * 128,
                                 dc * D + hf * W4:dc * D + (hf + 1) * W4]
                        S.dma("pool", [
                            (lambda ohi, p: lambda e_: e_.dma_start(
                                out=ohi, in_=hi_[p][:]))(ohi, p),
                            (lambda olo, p: lambda e_: e_.dma_start(
                                out=olo, in_=lo_[p][:]))(olo, p),
                        ], reads=[b_hl[p]],
                            writes=[b_hl[p], c.b_cache], sem=f"dppo{p}")
                    work.append(job)
    return work



def _emit_ffn(c):
    c.ffn_n = getattr(c, "ffn_n", 0) + 1
    pfx = f"f{c.ffn_n}"
    S, nc, dram = c.S, c.nc, c.dram
    B = S.buf
    est = contextlib.ExitStack()

    def sb(name, shape, dt=F32):
        return est.enter_context(nc.sbuf_tensor(f"{pfx}_{name}", list(shape), dt))

    QT_TOK = 512
    QT_TILES = 4

    xn_hi = [sb("xnh0", (128, NC * QT_TOK), BF16),
             sb("xnh1", (128, NC * QT_TOK), BF16)]
    xn_lo = [sb("xnl0", (128, NC * QT_TOK), BF16),
             sb("xnl1", (128, NC * QT_TOK), BF16)]
    upw_hi = sb("uwh", (128, NC * D), BF16); upw_lo = sb("uwl", (128, NC * D), BF16)
    dnw_hi = sb("dwh", (128, NC * D), BF16); dnw_lo = sb("dwl", (128, NC * D), BF16)
    hp_hi = sb("hph", (128, QT_TILES * D), BF16)
    hp_lo = sb("hpl", (128, QT_TILES * D), BF16)
    ht_hi = [sb("hth0", (128, D), BF16), sb("hth1", (128, D), BF16)]
    ht_lo = [sb("htl0", (128, D), BF16), sb("htl1", (128, D), BF16)]
    sdump = sb("sdump", (128, D))
    gw_hi = sb("gwh", (128, 64), BF16); gw_lo = sb("gwl", (128, 64), BF16)
    gst = sb("gst", (128, 64))
    comb = [sb("comb0", (128, QT_TILES * 8)), sb("comb1", (128, QT_TILES * 8))]
    smn = sb("smn", (128, 8))
    ct_hi = [sb("cth0", (8, QT_TILES * 128), BF16),
             sb("cth1", (8, QT_TILES * 128), BF16)]
    ct_lo = [sb("ctl0", (8, QT_TILES * 128), BF16),
             sb("ctl1", (8, QT_TILES * 128), BF16)]
    db_hi = sb("dbh", (NE, D), BF16); db_lo = sb("dbl", (NE, D), BF16)
    ubs_h = sb("ubsh", (1, D), BF16); ubs_l = sb("ubsl", (1, D), BF16)

    ups = [est.enter_context(nc.psum_tensor(f"{pfx}_up{i}", [128, 1024], F32))
           for i in range(2)]
    trf = est.enter_context(nc.psum_tensor(f"{pfx}_trf", [128, 1024], BF16))
    gps = est.enter_context(nc.psum_tensor(f"{pfx}_gps", [128, 512], F32))
    yps = est.enter_context(nc.psum_tensor(f"{pfx}_yps", [128, 1024], F32))

    b_xn = [B("fxn0"), B("fxn1")]; b_upw = B("fupw"); b_dnw = B("fdnw")
    b_hp = B("fhp"); b_ht = [B("fht0"), B("fht1")]
    b_gw = B("fgw"); b_comb = [B("fcomb0"), B("fcomb1")]
    b_ct = [B("fct0"), B("fct1")]
    b_bias = B("fbias"); b_ubs = B("fubs")
    b_ups = [B("fup0"), B("fup1")]
    b_trf = B("ftrf"); b_gps = B("fgps"); b_yps = B("fyps")
    b_sdump = B("fsdump"); b_smn = B("fsmn")

    S.dma("sp", lambda e: e.dma_start(out=gst[:], in_=dram["gate_sw"][:]),
          writes=[b_gw], sem="dgw")
    S.emit("act", lambda e: e.activation(out=gw_hi[:], in_=gst[:], func=ACTF.Copy),
           reads=[b_gw], writes=[b_gw])
    S.emit("dve", lambda e: e.tensor_sub(gw_lo[:], gst[:], gw_hi[:]),
           reads=[b_gw], writes=[b_gw])
    def split_bias(src_hbm, n_rows, out_hi, out_lo, semn):
        S.dma("sp", (lambda: lambda e: e.dma_start(
            out=sdump[0:n_rows, :], in_=src_hbm))(),
            reads=[b_sdump], writes=[b_sdump], sem=semn)
        S.emit("act", (lambda: lambda e: e.activation(
            out=out_hi[0:n_rows, :], in_=sdump[0:n_rows, :], func=ACTF.Copy))(),
            reads=[b_sdump], writes=[b_bias])
        S.emit("dve", (lambda: lambda e: e.tensor_sub(
            out_lo[0:n_rows, :], sdump[0:n_rows, :], out_hi[0:n_rows, :]))(),
            reads=[b_sdump, b_bias], writes=[b_bias])

    split_bias(dram["down_b"][:], NE, db_hi, db_lo, "dfb")

    def load_w(cache_h, cache_l, erow, whi, wlo, b_w, semn):
        S.dma("sp", [
            (lambda: lambda e: e.dma_start(
                out=whi[:], in_=cache_h[erow * 128:(erow + 1) * 128, :]))(),
            (lambda: lambda e: e.dma_start(
                out=wlo[:], in_=cache_l[erow * 128:(erow + 1) * 128, :]))(),
        ], reads=[b_w, c.b_cache], writes=[b_w], sem=semn)

    m_, nm_, z_, rz_, b2_, s2_, r2_ = (c.cols[:, i:i + 1] for i in range(16, 23))

    def emit_ln_gate_tile(qtr, par, i):
        tt = qtr * QT_TILES + i
        _emit_ln_tile(c, tt, xn_hi[par], xn_lo[par], b_xn[par], i, QT_TOK,
                      gps, b_gps, sdump, b_sdump)
        fns = []
        for dc in range(NC):
            lh_hi = xn_hi[par][:, dc * QT_TOK + i * 128:dc * QT_TOK + (i + 1) * 128]
            lh_lo = xn_lo[par][:, dc * QT_TOK + i * 128:dc * QT_TOK + (i + 1) * 128]
            rh_hi = gw_hi[:, dc * 8:(dc + 1) * 8]
            rh_lo = gw_lo[:, dc * 8:(dc + 1) * 8]
            fns.append((lambda lh, rh, st: lambda e: e.matmul(
                gps[:, 0:8], lh, rh, start=st, stop=False))(lh_hi, rh_hi, dc == 0))
            fns.append((lambda lh, rh: lambda e: e.matmul(
                gps[:, 0:8], lh, rh, start=False, stop=False))(lh_hi, rh_lo))
            fns.append((lambda lh, rh, sp: lambda e: e.matmul(
                gps[:, 0:8], lh, rh, start=False, stop=sp))(
                lh_lo, rh_hi, dc == NC - 1))
        S.emit("pe", fns, reads=[b_xn[par], b_gw], writes=[b_gps])
        S.emit("dve", lambda e: e.reduce_max(m_, gps[:, 0:8], axis=AX),
               reads=[b_gps], writes=[c.b_cols])
        S.emit("dve", lambda e: e.tensor_scalar(nm_, m_, -1.0, None,
                                                op0=ALU.mult),
               reads=[c.b_cols], writes=[c.b_cols])
        S.emit("act", lambda e: e.activation(out=smn[:], in_=gps[:, 0:8],
                                             func=ACTF.Exp, bias=nm_,
                                             accum_out=z_),
               reads=[b_gps, c.b_cols], writes=[b_smn, c.b_cols])
        S.emit("dve", lambda e: e.reciprocal(rz_, z_),
               reads=[c.b_cols], writes=[c.b_cols])
        S.emit("act", lambda e: e.activation(out=smn[:], in_=smn[:],
                                             func=ACTF.Copy, scale=rz_),
               reads=[b_smn, c.b_cols], writes=[b_smn])
        S.emit("dve", lambda e: e.max(out=c.m8[:], in_=smn[:]),
               reads=[b_smn], writes=[c.b_m8])
        S.emit("dve", lambda e: e.tensor_add(s2_, c.m8[:, 0:1], c.m8[:, 1:2]),
               reads=[c.b_m8], writes=[c.b_cols])
        S.emit("dve", lambda e: e.tensor_scalar(s2_, s2_, 1e-20, None,
                                                op0=ALU.add),
               reads=[c.b_cols], writes=[c.b_cols])
        S.emit("dve", lambda e: e.reciprocal(r2_, s2_),
               reads=[c.b_cols], writes=[c.b_cols])
        cmb = comb[par][:, i * 8:(i + 1) * 8]
        S.emit("dve", (lambda cmb: lambda e: e.scalar_tensor_tensor(
            cmb, in0=smn[:], scalar=c.m8[:, 1:2], in1=smn[:],
            op0=ALU.is_ge, op1=ALU.mult))(cmb),
            reads=[b_smn, c.b_m8], writes=[b_comb[par]])
        S.emit("dve", (lambda cmb: lambda e: e.tensor_scalar(
            cmb, cmb, r2_, None, op0=ALU.mult))(cmb),
            reads=[b_comb[par], c.b_cols], writes=[b_comb[par]])

    def emit_combT(qtr, par):
        fns = [(lambda i, par: lambda e: e.transpose(
            gps[0:8, i * 128:(i + 1) * 128],
            comb[par][:, i * 8:(i + 1) * 8], c.idn_f32[:]))(i, par)
            for i in range(QT_TILES)]
        S.emit("pe", fns, reads=[b_comb[par], c.b_consts], writes=[b_gps])
        S.emit("act", (lambda par: lambda e: e.activation(
            out=ct_hi[par][:], in_=gps[0:8, 0:512], func=ACTF.Copy))(par),
            reads=[b_gps], writes=[b_ct[par]])
        S.emit("dve", (lambda par: lambda e: e.tensor_sub(
            ct_lo[par][:], gps[0:8, 0:512], ct_hi[par][:]))(par),
            reads=[b_gps, b_ct[par]], writes=[b_ct[par]])

    def emit_refill(e_next):
        if e_next == NE:
            hsrc, lsrc = dram["c_sub_h"][0:1, :], dram["c_sub_l"][0:1, :]
        elif e_next == NE + 1:
            hsrc, lsrc = dram["c_sub_h"][1:2, :], dram["c_sub_l"][1:2, :]
        else:
            hsrc = dram["c_ub_h"][e_next:e_next + 1, :]
            lsrc = dram["c_ub_l"][e_next:e_next + 1, :]
        S.dma("sp", [
            (lambda hsrc: lambda e_: e_.dma_start(
                out=ubs_h[0:1, :], in_=hsrc))(hsrc),
            (lambda lsrc: lambda e_: e_.dma_start(
                out=ubs_l[0:1, :], in_=lsrc))(lsrc),
        ], reads=[b_ubs, c.b_cache], writes=[b_ubs], sem="dub")

    for i in range(QT_TILES):
        emit_ln_gate_tile(0, 0, i)
    emit_combT(0, 0)

    for qtr in range(4):
        par = qtr % 2
        tts = range(qtr * QT_TILES, (qtr + 1) * QT_TILES)
        load_w(dram["c_up_h"], dram["c_up_l"], 0, upw_hi, upw_lo, b_upw, "dfu")
        emit_refill(0)
        for e in range(NE + 1):
            shared = e == NE
            if shared:
                load_w(dram["c_shd_h"], dram["c_shd_l"], 0, dnw_hi, dnw_lo,
                       b_dnw, "dfd")
            else:
                load_w(dram["c_dn_h"], dram["c_dn_l"], e, dnw_hi, dnw_lo,
                       b_dnw, "dfd")
            for i, tt in enumerate(tts):
                up = ups[i % 2]
                b_up = b_ups[i % 2]
                h32 = yps[:, 0:D]
                b_h32 = b_yps
                for dhb in range(2):
                    fns = []
                    dst = up[:, dhb * 512:(dhb + 1) * 512]
                    fns.append((lambda dst, dhb: lambda e_: e_.matmul(
                        dst, c.ones2[0:1, :], ubs_h[0:1, dhb * 512:(dhb + 1) * 512],
                        start=True, stop=False))(dst, dhb))
                    fns.append((lambda dst, dhb: lambda e_: e_.matmul(
                        dst, c.ones2[0:1, :], ubs_l[0:1, dhb * 512:(dhb + 1) * 512],
                        start=False, stop=False))(dst, dhb))
                    for dc in range(NC):
                        lh_hi = xn_hi[par][:, dc * QT_TOK + i * 128:
                                           dc * QT_TOK + (i + 1) * 128]
                        lh_lo = xn_lo[par][:, dc * QT_TOK + i * 128:
                                           dc * QT_TOK + (i + 1) * 128]
                        rh_hi = upw_hi[:, dc * D + dhb * 512:dc * D + (dhb + 1) * 512]
                        rh_lo = upw_lo[:, dc * D + dhb * 512:dc * D + (dhb + 1) * 512]
                        fns.append((lambda dst, lh, rh: lambda e_: e_.matmul(
                            dst, lh, rh, start=False, stop=False))(dst, lh_hi, rh_hi))
                        fns.append((lambda dst, lh, rh: lambda e_: e_.matmul(
                            dst, lh, rh, start=False, stop=False))(dst, lh_hi, rh_lo))
                        fns.append((lambda dst, lh, rh, sp: lambda e_: e_.matmul(
                            dst, lh, rh, start=False, stop=sp))(
                            dst, lh_lo, rh_hi, dc == NC - 1))
                    S.emit("pe", fns, reads=[b_xn[par], b_upw, b_ubs, c.b_consts],
                           writes=[b_up])
                    S.emit("act", (lambda up, dhb: lambda e_: e_.activation(
                        out=h32[:, dhb * 512:(dhb + 1) * 512],
                        in_=up[:, dhb * 512:(dhb + 1) * 512],
                        func=ACTF.Silu))(up, dhb),
                        reads=[b_up], writes=[b_h32])
                hh = hp_hi[:, i * D:(i + 1) * D]
                hl = hp_lo[:, i * D:(i + 1) * D]
                if shared:
                    S.emit("act", (lambda hh: lambda e_: e_.activation(
                        out=hh, in_=h32[:], func=ACTF.Copy))(hh),
                        reads=[b_h32], writes=[b_hp])
                    S.emit("dve", (lambda hl, hh: lambda e_: e_.tensor_sub(
                        hl, h32[:], hh))(hl, hh),
                        reads=[b_h32, b_hp], writes=[b_hp])
                else:
                    cc = comb[par][:, i * 8 + e:i * 8 + e + 1]
                    S.emit("act", (lambda hh, cc: lambda e_: e_.activation(
                        out=hh, in_=h32[:], func=ACTF.Copy, scale=cc))(hh, cc),
                        reads=[b_h32, b_comb[par]], writes=[b_hp])
                    S.emit("dve", (lambda hl, hh, cc: lambda e_:
                           e_.scalar_tensor_tensor(hl, in0=h32[:], scalar=cc,
                                                   in1=hh, op0=ALU.mult,
                                                   op1=ALU.subtract))(hl, hh, cc),
                           reads=[b_h32, b_comb[par], b_hp], writes=[b_hp])
            if e < NE:
                if e == NE - 1:
                    load_w(dram["c_shu_h"], dram["c_shu_l"], 0, upw_hi, upw_lo,
                           b_upw, "dfu")
                else:
                    load_w(dram["c_up_h"], dram["c_up_l"], e + 1, upw_hi, upw_lo,
                           b_upw, "dfu")
                emit_refill(e + 1)
            else:
                emit_refill(NE + 1)
            if qtr < 3 and 4 <= e <= 7:
                emit_ln_gate_tile(qtr + 1, (qtr + 1) % 2, e - 4)
            if qtr < 3 and e == NE:
                emit_combT(qtr + 1, (qtr + 1) % 2)
            def emit_tr(i):
                hh = hp_hi[:, i * D:(i + 1) * D]
                hl = hp_lo[:, i * D:(i + 1) * D]
                for src, dst_t in ((hh, ht_hi[i % 2]), (hl, ht_lo[i % 2])):
                    fns = [(lambda dc, src: lambda e_: e_.transpose(
                        trf[:, dc * 128:(dc + 1) * 128],
                        src[:, dc * 128:(dc + 1) * 128], c.idn_bf[:]))(dc, src)
                        for dc in range(NC)]
                    S.emit("pe", fns, reads=[b_hp, c.b_consts], writes=[b_trf])
                    S.emit("act", (lambda dst_t: lambda e_: e_.activation(
                        out=dst_t[:], in_=trf[:], func=ACTF.Copy))(dst_t),
                        reads=[b_trf], writes=[b_ht[i % 2]])

            def emit_down(i, tt):
                fns = []
                for dob in range(2):
                    dst = yps[:, dob * 512:(dob + 1) * 512]
                    first = True
                    if shared:
                        fns.append((lambda dst, dob: lambda e_: e_.matmul(
                            dst, c.ones2[0:1, :],
                            ubs_h[0:1, dob * 512:(dob + 1) * 512],
                            start=True, stop=False))(dst, dob))
                        fns.append((lambda dst, dob: lambda e_: e_.matmul(
                            dst, c.ones2[0:1, :],
                            ubs_l[0:1, dob * 512:(dob + 1) * 512],
                            start=False, stop=False))(dst, dob))
                        first = False
                        cth = ct_hi[par][:, i * 128:(i + 1) * 128]
                        ctl = ct_lo[par][:, i * 128:(i + 1) * 128]
                        dbh = db_hi[:, dob * 512:(dob + 1) * 512]
                        dbl = db_lo[:, dob * 512:(dob + 1) * 512]
                        for lh, rh in ((cth, dbh), (cth, dbl), (ctl, dbh)):
                            fns.append((lambda dst, lh, rh: lambda e_: e_.matmul(
                                dst, lh, rh, start=False, stop=False))(dst, lh, rh))
                    for dc in range(NC):
                        lh_hi = ht_hi[i % 2][:, dc * 128:(dc + 1) * 128]
                        lh_lo = ht_lo[i % 2][:, dc * 128:(dc + 1) * 128]
                        rh_hi = dnw_hi[:, dc * D + dob * 512:dc * D + (dob + 1) * 512]
                        rh_lo = dnw_lo[:, dc * D + dob * 512:dc * D + (dob + 1) * 512]
                        fns.append((lambda dst, lh, rh, st: lambda e_: e_.matmul(
                            dst, lh, rh, start=st, stop=False))(
                            dst, lh_hi, rh_hi, first and dc == 0))
                        fns.append((lambda dst, lh, rh: lambda e_: e_.matmul(
                            dst, lh, rh, start=False, stop=False))(dst, lh_hi, rh_lo))
                        fns.append((lambda dst, lh, rh, sp: lambda e_: e_.matmul(
                            dst, lh, rh, start=False, stop=sp))(
                            dst, lh_lo, rh_hi, dc == NC - 1))
                S.emit("pe", fns, reads=[b_ht[i % 2], b_dnw, b_ct[par], b_bias,
                                         b_ubs, c.b_consts],
                       writes=[b_yps])
                for dob in range(2):
                    xs = c.X[:, tt * D + dob * 512:tt * D + (dob + 1) * 512]
                    S.emit("dve", (lambda xs, dob: lambda e_: e_.tensor_add(
                        xs, xs, yps[:, dob * 512:(dob + 1) * 512]))(xs, dob),
                        reads=[b_yps, c.b_x[tt]], writes=[c.b_x[tt]])

            tl = list(tts)
            for i, tt in enumerate(tl):
                emit_tr(i)
                if i >= 1:
                    emit_down(i - 1, tl[i - 1])
            emit_down(len(tl) - 1, tl[-1])
    est.close()



def _prep_shared(inputs, pfx_map):
    f32 = np.float32
    bf = ml_dtypes.bfloat16
    d = {}
    for pfx, nm in pfx_map.items():
        wq = np.asarray(inputs[f"{nm}_Wq"], f32)
        wk = np.asarray(inputs[f"{nm}_Wk"], f32)
        wv = np.asarray(inputs[f"{nm}_Wv"], f32)
        wo = np.asarray(inputs[f"{nm}_Wo"], f32)

        def pan(w):
            a = w.T.reshape(8, 128, 8, 128).transpose(2, 1, 0, 3)
            return np.ascontiguousarray(a.reshape(D, D))

        d[f"{pfx}_wq"] = pan(wq)
        d[f"{pfx}_wk"] = pan(wk)
        d[f"{pfx}_wv"] = pan(wv)
        d[f"{pfx}_wo"] = np.ascontiguousarray(wo.T.reshape(D, D))
        d[f"{pfx}_bq"] = np.ascontiguousarray(
            np.asarray(inputs[f"{nm}_bq"], f32).reshape(8, 128).T)
        d[f"{pfx}_bk"] = np.ascontiguousarray(
            np.asarray(inputs[f"{nm}_bk"], f32).reshape(8, 128).T)
        d[f"{pfx}_bv"] = np.ascontiguousarray(
            np.asarray(inputs[f"{nm}_bv"], f32).reshape(8, 128).T)
        d[f"{pfx}_bo"] = np.asarray(inputs[f"{nm}_bo"], f32).reshape(1, D)
    d["up_w"] = np.ascontiguousarray(
        np.asarray(inputs["L0_ffn_up_w"], f32).reshape(NE * D, D))
    d["down_w"] = np.ascontiguousarray(
        np.asarray(inputs["L0_ffn_down_w"], f32).reshape(NE * D, D))
    d["sh_up_w"] = np.ascontiguousarray(np.asarray(inputs["L0_ffn_sh_up_w"], f32))
    d["sh_down_w"] = np.ascontiguousarray(np.asarray(inputs["L0_ffn_sh_down_w"], f32))
    d["up_b"] = np.ascontiguousarray(np.asarray(inputs["L0_ffn_up_b"], f32))
    d["down_b"] = np.ascontiguousarray(np.asarray(inputs["L0_ffn_down_b"], f32))
    d["sh_up_b"] = np.asarray(inputs["L0_ffn_sh_up_b"], f32).reshape(1, D)
    d["sh_down_b"] = np.asarray(inputs["L0_ffn_sh_down_b"], f32).reshape(1, D)
    gt = np.asarray(inputs["L0_ffn_gate_w"], f32).T
    d["gate_sw"] = np.ascontiguousarray(
        gt.reshape(8, 128, 8).transpose(1, 0, 2).reshape(128, 64))
    d["idn_bf"] = np.eye(128, dtype=bf)
    d["idn_f32"] = np.eye(128, dtype=f32)
    d["ones2"] = np.ones((2, 128), dtype=bf)
    d["masks4"] = _mask_variants()
    return d


def kernel(**inputs):
    pfx_map = {"a": "L0_attn_a", "b": "L0_attn_b"}
    shared = _prep_shared(inputs, pfx_map)
    x = np.ascontiguousarray(np.asarray(inputs["L0_attn_a_x"], np.float32))
    nc = build_program()
    in_maps = []
    for core in range(8):
        m = dict(shared)
        m["x"] = np.ascontiguousarray(x[core])
        in_maps.append(m)
    res = run_bass_kernel_spmd(nc, in_maps, core_ids=list(range(8)))
    out = np.stack([np.asarray(r["out"], np.float32) for r in res.results], axis=0)
    return out


# version 19
# speedup vs baseline: 1.0731x; 1.0024x over previous
import contextlib
import numpy as np
import ml_dtypes
import concourse.bass as bass
import concourse.mybir as mybir
from concourse.bass_utils import run_bass_kernel_spmd

F32 = mybir.dt.float32
BF16 = mybir.dt.bfloat16
AX = mybir.AxisListType.X
ALU = mybir.AluOpType
ACTF = mybir.ActivationFunctionType

T = 2048
D = 1024
NT = 16
NC = 8
H = 16
DH = 64
NPAIR = 8
NE = 8
TOPK = 64
NEG = -1.0e30

N_SUBLAYERS = 4


ENGINES = ("pe", "act", "dve", "pool", "sp")
_ENG_ATTR = {"pe": "tensor", "act": "scalar", "dve": "vector", "pool": "gpsimd",
             "sp": "sync"}


class Buf:
    __slots__ = ("name", "last_w", "readers")

    def __init__(self, name):
        self.name = name
        self.last_w = None
        self.readers = {}


class Sched:
    def __init__(self, nc):
        self.nc = nc
        self.sems = {}
        self.counts = {e: 0 for e in ENGINES}
        self.dma_counts = {}
        self.known = {e: {} for e in ENGINES}
        self.streams = {e: [] for e in ENGINES}
        self._sem_names = set()

    def buf(self, name):
        return Buf(name)

    def _need_waits(self, eng, reads, writes):
        req = {}
        for b in reads:
            if b.last_w is not None:
                s, v = b.last_w
                if req.get(s, -1) < v:
                    req[s] = v
        for b in writes:
            if b.last_w is not None:
                s, v = b.last_w
                if req.get(s, -1) < v:
                    req[s] = v
            for s, v in b.readers.items():
                if req.get(s, -1) < v:
                    req[s] = v
        out = []
        kn = self.known[eng]
        own = "s_" + eng
        for s, v in sorted(req.items()):
            if s == own and kn.get(s, -1) >= v:
                continue
            if kn.get(s, -1) < v:
                out.append((s, v))
                kn[s] = v
        return out

    def _record(self, sem, val, reads, writes):
        for b in reads:
            if b.readers.get(sem, -1) < val:
                b.readers[sem] = val
        for b in writes:
            b.last_w = (sem, val)
            b.readers = {}

    def emit(self, eng, fns, reads=(), writes=()):
        if callable(fns):
            fns = [fns]
        waits = self._need_waits(eng, reads, writes)
        sem = "s_" + eng
        self._sem_names.add(sem)
        self.counts[eng] += 1
        val = self.counts[eng]
        self.streams[eng].append(("op", fns, waits, sem, 1))
        self._record(sem, val, reads, writes)
        return val

    def dma(self, eng, fns, reads=(), writes=(), sem=None):
        if callable(fns):
            fns = [fns]
        waits = self._need_waits(eng, reads, writes)
        sem = sem if sem is not None else ("d_" + eng)
        self._sem_names.add(sem)
        self.dma_counts[sem] = self.dma_counts.get(sem, 0) + 16 * len(fns)
        val = self.dma_counts[sem]
        self.streams[eng].append(("dma", fns, waits, sem, 16))
        self._record(sem, val, reads, writes)
        return val

    def fence(self):
        snap_e = {e: self.counts[e] for e in ENGINES}
        snap_d = dict(self.dma_counts)
        for e in ENGINES:
            ws = [("s_" + o, snap_e[o]) for o in ENGINES
                  if o != e and snap_e[o] > 0]
            ws += [(s, v) for s, v in snap_d.items() if v > 0]
            for s, _ in ws:
                self._sem_names.add(s)
            if ws:
                self.streams[e].append(("waitonly", None, ws, None, 0))
                kn = self.known[e]
                for s, v in ws:
                    if kn.get(s, -1) < v:
                        kn[s] = v

    def drain_all(self, eng="pool"):
        ws = [("s_" + e, self.counts[e]) for e in ENGINES
              if e != eng and self.counts[e] > 0]
        ws += [(s, v) for s, v in self.dma_counts.items() if v > 0]
        for s, _ in ws:
            self._sem_names.add(s)
        self.streams[eng].append(("waitonly", None, ws, None, 0))

    def finalize(self, block, exit_stack):
        for s in sorted(self._sem_names):
            self.sems[s] = exit_stack.enter_context(self.nc.semaphore(s))

        def run_stream(name):
            def body(eng):
                for kind, fns, waits, sem, _amt in self.streams[name]:
                    for ws, wv in waits:
                        eng.wait_ge(self.sems[ws], wv)
                    if kind == "waitonly":
                        continue
                    insts = [fn(eng) for fn in fns]
                    if kind == "op":
                        insts[-1].then_inc(self.sems[sem], 1)
                    else:
                        for inst in insts:
                            inst.then_inc(self.sems[sem], 16)
            return body

        for e in ENGINES:
            if self.streams[e]:
                getattr(block, _ENG_ATTR[e])(run_stream(e))



def _mask_variants():
    r = np.arange(128)[:, None]
    j = np.arange(896)[None, :]
    return np.ascontiguousarray(
        np.where(j <= r + 384, 0.0, NEG)).astype(ml_dtypes.bfloat16)


class Ctx:
    pass


def build_program(n_sub=None):
    n_sub = N_SUBLAYERS if n_sub is None else n_sub
    nc = bass.Bass("TRN2", target_bir_lowering=False)
    c = Ctx()
    c.nc = nc
    S = Sched(nc)
    c.S = S
    est = contextlib.ExitStack()
    c.est = est

    dram = {}

    def din(name, shape, dt=F32):
        dram[name] = nc.dram_tensor(name, list(shape), dt, kind="ExternalInput").ap()
        return dram[name]

    din("x", (T, D))
    for p in ("a", "b"):
        din(f"{p}_wq", (D, D)); din(f"{p}_wk", (D, D)); din(f"{p}_wv", (D, D))
        din(f"{p}_wo", (D, D))
        din(f"{p}_bq", (128, 8)); din(f"{p}_bk", (128, 8)); din(f"{p}_bv", (128, 8))
        din(f"{p}_bo", (1, D))
    din("up_w", (NE * D, D)); din("down_w", (NE * D, D))
    din("sh_up_w", (D, D)); din("sh_down_w", (D, D))
    din("up_b", (NE, D)); din("down_b", (NE, D))
    din("sh_up_b", (1, D)); din("sh_down_b", (1, D))
    din("gate_sw", (128, 64))
    din("idn_bf", (128, 128), BF16)
    din("idn_f32", (128, 128))
    din("ones2", (2, 128), BF16)
    din("masks4", (128, 896), BF16)
    out_h = nc.dram_tensor("out", [T, D], F32, kind="ExternalOutput").ap()

    def dsc(name, shape, dt=BF16):
        t = nc.dram_tensor(name, list(shape), dt, kind="Internal")
        dram[name] = t.ap()
        return dram[name]

    dsc("c_up_h", (NE * 128, NC * D)); dsc("c_up_l", (NE * 128, NC * D))
    dsc("c_dn_h", (NE * 128, NC * D)); dsc("c_dn_l", (NE * 128, NC * D))
    dsc("c_shu_h", (128, NC * D)); dsc("c_shu_l", (128, NC * D))
    dsc("c_shd_h", (128, NC * D)); dsc("c_shd_l", (128, NC * D))
    dsc("c_ub_h", (NE, D)); dsc("c_ub_l", (NE, D))
    dsc("c_sub_h", (2, D)); dsc("c_sub_l", (2, D))
    c.dram = dram

    def sb(name, shape, dt=F32):
        return est.enter_context(nc.sbuf_tensor(name, list(shape), dt))

    c.X = sb("X", (128, NT * D))
    c.idn_bf = sb("idn_bf_t", (128, 128), BF16)
    c.idn_f32 = sb("idn_f32_t", (128, 128))
    c.ones2 = sb("ones2_t", (2, 128), BF16)
    c.cols = sb("cols", (128, 32))
    c.m8 = sb("m8t", (128, 8))

    B = S.buf
    c.b_x = [B(f"x{t}") for t in range(NT)]
    c.b_consts = B("consts")
    c.b_cols = B("cols")
    c.b_m8 = B("m8")
    c.b_out = [B(f"out{t}") for t in range(NT)]
    c.b_cache = B("wcache")

    for tt in range(NT):
        S.dma("sp", (lambda tt: lambda e: e.dma_start(
            out=c.X[:, tt * D:(tt + 1) * D],
            in_=dram["x"][tt * 128:(tt + 1) * 128, :]))(tt),
            writes=[c.b_x[tt]], sem=f"dx{tt}")
    S.dma("sp", [
        lambda e: e.dma_start(out=c.idn_bf[:], in_=dram["idn_bf"][:]),
        lambda e: e.dma_start(out=c.idn_f32[:], in_=dram["idn_f32"][:]),
        lambda e: e.dma_start(out=c.ones2[:], in_=dram["ones2"][:]),
    ], writes=[c.b_consts], sem="dconst")

    if n_sub >= 1:
        _emit_attn(c, "a", causal=True, prepass=(n_sub >= 2))
    if n_sub >= 2:
        S.fence()
        _emit_ffn(c)
    if n_sub >= 3:
        S.fence()
        _emit_attn(c, "b", causal=False, prepass=False)
    if n_sub >= 4:
        S.fence()
        _emit_ffn(c)

    for tt in range(NT):
        S.dma("sp", (lambda tt: lambda e: e.dma_start(
            out=out_h[tt * 128:(tt + 1) * 128, :],
            in_=c.X[:, tt * D:(tt + 1) * D]))(tt),
            reads=[c.b_x[tt]], writes=[c.b_out[tt]], sem=f"do{tt}")
    S.drain_all("pool")

    block = est.enter_context(nc.Block())
    S.finalize(block, est)
    est.close()
    return nc



def _emit_ln_tile(c, tt, xn_hi, xn_lo, b_xn, tloc, t_span, ps, b_ps, sdump, b_sdump):
    S, nc, X = c.S, c.nc, c.X
    xs = X[:, tt * D:(tt + 1) * D]
    su, sq, mn, nv, rs, nm = (c.cols[:, i:i + 1] for i in range(6))
    S.emit("dve", (lambda xs: lambda e: e.reduce_sum(su, xs, axis=AX))(xs),
           reads=[c.b_x[tt]], writes=[c.b_cols])
    S.emit("act", (lambda xs: lambda e: e.activation(
        out=sdump[:, 0:D], in_=xs, func=ACTF.Square, accum_out=sq))(xs),
        reads=[c.b_x[tt]], writes=[c.b_cols, b_sdump])
    S.emit("dve", lambda e: e.tensor_scalar(mn, su, 1.0 / D, None, op0=ALU.mult),
           reads=[c.b_cols], writes=[c.b_cols])
    S.emit("dve", lambda e: e.tensor_scalar(nv, sq, -1.0 / D, -1e-5,
                                            op0=ALU.mult, op1=ALU.add),
           reads=[c.b_cols], writes=[c.b_cols])
    S.emit("dve", lambda e: e.scalar_tensor_tensor(
        nv, in0=mn, scalar=mn, in1=nv, op0=ALU.mult, op1=ALU.add),
        reads=[c.b_cols], writes=[c.b_cols])
    S.emit("act", lambda e: e.activation(out=rs, in_=nv, func=ACTF.Sqrt,
                                         scale=-1.0),
           reads=[c.b_cols], writes=[c.b_cols])
    S.emit("dve", lambda e: e.reciprocal(rs, rs),
           reads=[c.b_cols], writes=[c.b_cols])
    S.emit("dve", lambda e: e.tensor_scalar(nm, mn, rs, -1.0,
                                            op0=ALU.mult, op1=ALU.mult),
           reads=[c.b_cols], writes=[c.b_cols])
    S.emit("act", (lambda xs: lambda e: e.activation(
        out=sdump[:, 0:D], in_=xs, func=ACTF.Identity, scale=rs, bias=nm))(xs),
        reads=[c.b_x[tt], c.b_cols], writes=[b_sdump])
    for half in range(2):
        fns = []
        for q in range(4):
            dc = half * 4 + q
            fns.append((lambda dc, q: lambda e: e.transpose(
                ps[:, q * 128:(q + 1) * 128],
                sdump[:, dc * 128:(dc + 1) * 128], c.idn_f32[:]))(dc, q))
        S.emit("pe", fns, reads=[b_sdump, c.b_consts], writes=[b_ps])
        dst_hi = xn_hi[:, :].rearrange("p (cc t) -> p cc t", t=t_span)[
            :, half * 4:half * 4 + 4, tloc * 128:(tloc + 1) * 128]
        dst_lo = xn_lo[:, :].rearrange("p (cc t) -> p cc t", t=t_span)[
            :, half * 4:half * 4 + 4, tloc * 128:(tloc + 1) * 128]
        src = ps[:, 0:512]
        S.emit("act", (lambda dst_hi, src: lambda e: e.activation(
            out=dst_hi, in_=src, func=ACTF.Copy))(dst_hi, src),
            reads=[b_ps], writes=[b_xn])
        S.emit("dve", (lambda dst_lo, src, dst_hi: lambda e: e.tensor_sub(
            dst_lo, src, dst_hi))(dst_lo, src, dst_hi),
            reads=[b_ps, b_xn], writes=[b_xn])



def _emit_attn(c, pfx, causal, prepass):
    S, nc, dram = c.S, c.nc, c.dram
    B = S.buf
    est = contextlib.ExitStack()

    def sb(name, shape, dt=F32):
        return est.enter_context(nc.sbuf_tensor(f"{pfx}_{name}", list(shape), dt))

    xn_hi = sb("xnh", (128, NC * T), BF16)
    xn_lo = sb("xnl", (128, NC * T), BF16)
    qt_hi = sb("qth", (128, T), BF16); qt_lo = sb("qtl", (128, T), BF16)
    kt_hi = sb("kth", (128, T), BF16); kt_lo = sb("ktl", (128, T), BF16)
    v_hi = sb("vh", (128, T), BF16); v_lo = sb("vl", (128, T), BF16)
    trb_hi = sb("trbh", (128, T), BF16); trb_lo = sb("trbl", (128, T), BF16)
    phi = [sb("phi0", (128, T), BF16), sb("phi1", (128, T), BF16)]
    plo = [sb("plo0", (128, T), BF16), sb("plo1", (128, T), BF16)]
    ssb = [sb("ssb0", (128, T)), sb("ssb1", (128, T))]
    if not causal:
        sx = sb("sx", (128, T))
    otq_hi = sb("otqh", (128, 128), BF16); otq_lo = sb("otql", (128, 128), BF16)
    if causal:
        c.masks = sb("masks", (128, 896), BF16)
        S.dma("sp", lambda e: e.dma_start(out=c.masks[:], in_=dram["masks4"][:]),
              writes=[c.b_consts], sem="dmask")
    big32 = ssb[0] if causal else sx
    b_big32 = None
    wst = sb("wst", (128, 512))
    wB_hi = sb("wBh", (128, D), BF16); wB_lo = sb("wBl", (128, D), BF16)
    bqkv = sb("bqkv", (128, 24))

    ps = est.enter_context(nc.psum_tensor(f"{pfx}_ps", [128, 2048], F32))
    tr0 = est.enter_context(nc.psum_tensor(f"{pfx}_tr0", [128, 1024], BF16))
    tr1 = est.enter_context(nc.psum_tensor(f"{pfx}_tr1", [128, 1024], BF16))
    po = est.enter_context(nc.psum_tensor(f"{pfx}_po", [128, 512], F32))
    yw = est.enter_context(nc.psum_tensor(f"{pfx}_yw", [128, 512], F32))

    b_xn = B("xn"); b_qt = B("qt"); b_kt = B("kt"); b_v = B("v")
    b_trb = B("trb"); b_phi = [B("phi0"), B("phi1")]
    b_psb = [B("psb0"), B("psb1"), B("psb2"), B("psb3")]
    b_ssb = [B("ssb0"), B("ssb1")]; b_sx = B("sx")
    b_big32 = b_ssb[0] if causal else b_sx
    bsp_h = phi[0][0:1, 0:D]
    bsp_l = phi[0][0:1, D:2 * D]
    b_bsp_ali = b_phi[0]
    b_otq = B("otq"); b_wst = B("wst")
    b_wB = B("wB"); b_bsp = B("bsp"); b_bqkv = B("bqkv")
    b_ps = B("ps"); b_tr0 = B("tr0"); b_tr1 = B("tr1")
    b_po = B("po"); b_yw = B("yw")
    sdump = big32
    b_sdump = b_big32

    ppw = _prep_ffn_prepass_work(c, est) if prepass else []
    for tt in range(NT):
        _emit_ln_tile(c, tt, xn_hi, xn_lo, b_xn, tt, T, ps, b_psb[0],
                      sdump, b_sdump)

    S.dma("sp", lambda e: e.dma_start(
        out=bqkv[:, 0:8], in_=dram[f"{pfx}_bq"][:]), writes=[b_bqkv], sem="dbias")
    S.dma("sp", lambda e: e.dma_start(
        out=bqkv[:, 8:16], in_=dram[f"{pfx}_bk"][:]), writes=[b_bqkv], sem="dbias")
    S.dma("sp", lambda e: e.dma_start(
        out=bqkv[:, 16:24], in_=dram[f"{pfx}_bv"][:]), writes=[b_bqkv], sem="dbias")
    S.dma("sp", lambda e: e.dma_start(out=big32[0:1, 0:D], in_=dram[f"{pfx}_bo"][:]),
          reads=[b_big32], writes=[b_big32], sem="dbias2")
    S.emit("act", lambda e: e.activation(out=bsp_h, in_=big32[0:1, 0:D],
                                         func=ACTF.Copy),
           reads=[b_big32], writes=[b_bsp_ali])
    S.emit("dve", lambda e: e.tensor_sub(bsp_l, big32[0:1, 0:D], bsp_h),
           reads=[b_big32, b_bsp_ali], writes=[b_bsp_ali])

    ones1 = c.ones2[0:1, :]
    for tt in range(NT):
        for dob in range(2):
            S.emit("pe", [
                (lambda dob: lambda e: e.matmul(
                    yw[:, :], ones1, bsp_h[0:1, dob * 512:(dob + 1) * 512],
                    start=True, stop=False))(dob),
                (lambda dob: lambda e: e.matmul(
                    yw[:, :], ones1, bsp_l[0:1, dob * 512:(dob + 1) * 512],
                    start=False, stop=True))(dob),
            ], reads=[c.b_consts, b_bsp_ali], writes=[b_yw])
            xs = c.X[:, tt * D + dob * 512: tt * D + (dob + 1) * 512]
            S.emit("dve", (lambda xs: lambda e: e.tensor_add(xs, xs, yw[:, :]))(xs),
                   reads=[b_yw, c.b_x[tt]], writes=[c.b_x[tt]])

    def load_split_w(hbm_rows, whi, wlo, b_w, semn, wscale=1.0):
        for h2 in range(2):
            cols = slice(h2 * 512, (h2 + 1) * 512)
            S.dma("sp", (lambda hbm_rows, cols: lambda e: e.dma_start(
                out=wst[:, :], in_=hbm_rows[:, cols]))(hbm_rows, cols),
                reads=[b_wst], writes=[b_wst], sem=semn)
            S.emit("act", (lambda whi, cols, wscale: lambda e: e.activation(
                out=whi[:, cols], in_=wst[:, :], func=ACTF.Copy,
                scale=wscale))(whi, cols, wscale),
                reads=[b_wst], writes=[b_w])
            S.emit("dve", (lambda whi, wlo, cols, wscale: lambda e:
                   e.scalar_tensor_tensor(
                wlo[:, cols], in0=wst[:, :], scalar=wscale, in1=whi[:, cols],
                op0=ALU.mult, op1=ALU.subtract))(whi, wlo, cols, wscale),
                reads=[b_wst, b_w], writes=[b_w])

    NKC = 16

    for pair in range(NPAIR):
        for mat, dsth, dstl, b_dst, bias_off in (
                ("wq", qt_hi, qt_lo, b_qt, 0),
                ("wk", kt_hi, kt_lo, b_kt, 8),
                ("wv", trb_hi, trb_lo, b_trb, 16)):
            whi, wlo, b_w = wB_hi, wB_lo, b_wB
            load_split_w(dram[f"{pfx}_{mat}"][pair * 128:(pair + 1) * 128, :],
                         whi, wlo, b_w, "dwp",
                         wscale=0.125 if mat == "wq" else 1.0)
            bias_col = bqkv[:, bias_off + pair:bias_off + pair + 1]
            if bias_off == 0:
                bsc = c.cols[:, 11:12]
                S.emit("dve", (lambda bias_col, bsc: lambda e: e.tensor_scalar(
                    bsc, bias_col, 0.125, None, op0=ALU.mult))(bias_col, bsc),
                    reads=[b_bqkv], writes=[c.b_cols])
                bias_col = bsc
            for tb in range(4):
                fns = []
                rhs_hi = [xn_hi[:, dc * T + tb * 512:dc * T + (tb + 1) * 512]
                          for dc in range(NC)]
                rhs_lo = [xn_lo[:, dc * T + tb * 512:dc * T + (tb + 1) * 512]
                          for dc in range(NC)]
                dst = ps[:, tb * 512:(tb + 1) * 512]
                for dc in range(NC):
                    lh = whi[:, dc * 128:(dc + 1) * 128]
                    ll = wlo[:, dc * 128:(dc + 1) * 128]
                    fns.append((lambda dst, lh, rh, st: lambda e: e.matmul(
                        dst, lh, rh, start=st, stop=False))(
                        dst, lh, rhs_hi[dc], dc == 0))
                    fns.append((lambda dst, lh, rh: lambda e: e.matmul(
                        dst, lh, rh, start=False, stop=False))(dst, lh, rhs_lo[dc]))
                    fns.append((lambda dst, ll, rh, sp: lambda e: e.matmul(
                        dst, ll, rh, start=False, stop=sp))(
                        dst, ll, rhs_hi[dc], dc == NC - 1))
                S.emit("pe", fns, reads=[b_w, b_xn], writes=[b_psb[tb]])
                src_ = ps[:, tb * 512:(tb + 1) * 512]
                dh = dsth[:, tb * 512:(tb + 1) * 512]
                dl = dstl[:, tb * 512:(tb + 1) * 512]
                S.emit("act", (lambda dh, src, bias_col: lambda e: e.activation(
                    out=dh, in_=src, func=ACTF.Identity, bias=bias_col))(dh, src_, bias_col),
                    reads=[b_psb[tb], b_bqkv, c.b_cols], writes=[b_dst])
                S.emit("dve", (lambda dl, src, dh, bias_col: lambda e:
                       e.scalar_tensor_tensor(dl, in0=src, scalar=bias_col, in1=dh,
                                              op0=ALU.add, op1=ALU.subtract))(
                       dl, src_, dh, bias_col),
                       reads=[b_psb[tb], b_bqkv, b_dst, c.b_cols], writes=[b_dst])

        for half in range(2):
            for src, dstt, b_pt in ((trb_hi, v_hi, b_v), (trb_lo, v_lo, b_v)):
                trp = tr0 if src is trb_hi else tr1
                b_trp = b_tr0 if src is trb_hi else b_tr1
                fns = [(lambda kc, src, trp: lambda e: e.transpose(
                    trp[:, (kc % 8) * 128:((kc % 8) + 1) * 128],
                    src[:, kc * 128:(kc + 1) * 128], c.idn_bf[:]))(kc, src, trp)
                    for kc in range(half * 8, half * 8 + 8)]
                S.emit("pe", fns, reads=[b_trb, c.b_consts], writes=[b_trp])
                dst = dstt[:, half * 1024:(half + 1) * 1024]
                S.emit("act", (lambda dst, trp: lambda e: e.activation(
                    out=dst, in_=trp[:, :], func=ACTF.Copy))(dst, trp),
                    reads=[b_trp], writes=[b_pt])

        load_split_w(dram[f"{pfx}_wo"][pair * 128:(pair + 1) * 128, :],
                     wB_hi, wB_lo, b_wB, "dwp")
        m_, nm_, z_, rz_, t64_ = (c.cols[:, i:i + 1] for i in range(6, 11))
        zp = c.cols[:, 12:16]

        def emit_A(hl, qt, par):
            nkb = (qt // 4 + 1) if causal else 4
            for kb in range(nkb):
                fns = []
                dst = ps[:, kb * 512:(kb + 1) * 512]
                first = True
                if causal and kb == qt // 4:
                    mv = qt % 4
                    fns.append((lambda dst, mv: lambda e: e.matmul(
                        dst, c.idn_bf[:],
                        c.masks[:, 384 - 128 * mv:896 - 128 * mv],
                        start=True, stop=False))(dst, mv))
                    first = False
                lq_hi = qt_hi[hl * 64:hl * 64 + 64, qt * 128:(qt + 1) * 128]
                lq_lo = qt_lo[hl * 64:hl * 64 + 64, qt * 128:(qt + 1) * 128]
                rk_hi = kt_hi[hl * 64:hl * 64 + 64, kb * 512:(kb + 1) * 512]
                rk_lo = kt_lo[hl * 64:hl * 64 + 64, kb * 512:(kb + 1) * 512]
                tp = (hl * 64, 0)
                fns.append((lambda dst, lq, rk, st, tp: lambda e: e.matmul(
                    dst, lq, rk, start=st, stop=False, tile_position=tp))(
                    dst, lq_hi, rk_hi, first, tp))
                fns.append((lambda dst, lq, rk, tp: lambda e: e.matmul(
                    dst, lq, rk, start=False, stop=False, tile_position=tp))(
                    dst, lq_hi, rk_lo, tp))
                fns.append((lambda dst, lq, rk, tp: lambda e: e.matmul(
                    dst, lq, rk, start=False, stop=True, tile_position=tp))(
                    dst, lq_lo, rk_hi, tp))
                S.emit("pe", fns, reads=[b_qt, b_kt, c.b_consts],
                       writes=[b_psb[kb]])
                S.emit("act", (lambda kb, par: lambda e: e.activation(
                    out=ssb[par][:, kb * 512:(kb + 1) * 512],
                    in_=ps[:, kb * 512:(kb + 1) * 512],
                    func=ACTF.Copy))(kb, par),
                    reads=[b_psb[kb]], writes=[b_ssb[par]])

        def emit_B_stats(hl, qt, par):
            nkb = qt // 4 + 1
            sb_t = ssb[par]
            S.emit("dve", (lambda nkb, sb_t: lambda e: e.reduce_max(
                m_, sb_t[:, 0:nkb * 512], axis=AX))(nkb, sb_t),
                reads=[b_ssb[par]], writes=[c.b_cols])
            S.emit("dve", lambda e: e.tensor_scalar(
                nm_, m_, -1.0, None, op0=ALU.mult),
                reads=[c.b_cols], writes=[c.b_cols])
            S.emit("act", (lambda nkb, sb_t: lambda e: e.activation(
                out=sb_t[:, 0:nkb * 512], in_=sb_t[:, 0:nkb * 512],
                func=ACTF.Exp, bias=nm_,
                accum_out=z_))(nkb, sb_t),
                reads=[b_ssb[par], c.b_cols], writes=[b_ssb[par], c.b_cols])

        def emit_B_rounds(hl, qt, par, r0, r1):
            sb_t = ssb[par]
            for r in range(r0, r1):
                src_t, src_b = (sb_t, b_ssb[par]) if r == 0 else (sx, b_sx)
                S.emit("dve", (lambda src_t: lambda e: e.max(
                    out=c.m8[:], in_=src_t[:]))(src_t),
                    reads=[src_b], writes=[c.b_m8])
                if r == 0:
                    S.emit("dve", lambda e: e.tensor_scalar(
                        nm_, c.m8[:, 0:1], -1.0, 0.0,
                        op0=ALU.mult, op1=ALU.min),
                        reads=[c.b_m8], writes=[c.b_cols])
                if r == 7:
                    continue
                S.emit("dve", (lambda src_t: lambda e: e.match_replace(
                    out=sx[:], in_to_replace=c.m8[:], in_values=src_t[:],
                    imm_value=NEG))(src_t),
                    reads=[src_b, c.b_m8], writes=[b_sx])

        def emit_B_mask(hl, qt, par):
            sb_t = ssb[par]
            S.emit("dve", (lambda sb_t: lambda e: e.scalar_tensor_tensor(
                sx[:], in0=sb_t[:], scalar=c.m8[:, 7:8], in1=sb_t[:],
                op0=ALU.is_ge, op1=ALU.mult))(sb_t),
                reads=[b_ssb[par], c.b_m8], writes=[b_sx])
            S.emit("act", (lambda sb_t: lambda e: e.activation(
                out=sb_t[:], in_=sx[:], func=ACTF.Exp,
                bias=nm_, accum_out=z_))(sb_t),
                reads=[b_sx, c.b_cols], writes=[b_ssb[par], c.b_cols])

        def emit_B_tail(hl, qt, par):
            psrc, b_psrc = ssb[par], b_ssb[par]
            nkb = (qt // 4 + 1) if causal else 4
            vw = nkb * 512
            S.emit("dve", lambda e: e.reciprocal(rz_, z_),
                   reads=[c.b_cols], writes=[c.b_cols])
            if not causal:
                S.emit("dve", (lambda hl, vw, psrc: lambda e: e.tensor_copy(
                    phi[hl][:, 0:vw], psrc[:, 0:vw]))(hl, vw, psrc),
                    reads=[b_psrc], writes=[b_phi[hl]])
                S.emit("pool", (lambda hl, vw, psrc: lambda e: e.tensor_sub(
                    plo[hl][:, 0:vw], psrc[:, 0:vw],
                    phi[hl][:, 0:vw]))(hl, vw, psrc),
                    reads=[b_psrc, b_phi[hl]], writes=[b_phi[hl]])
                return
            S.emit("dve", (lambda hl, vw, psrc: lambda e: e.tensor_scalar(
                phi[hl][:, 0:vw], psrc[:, 0:vw], rz_, None,
                op0=ALU.mult))(hl, vw, psrc),
                reads=[b_psrc, c.b_cols], writes=[b_phi[hl]])
            S.emit("dve", (lambda hl, vw, psrc: lambda e: e.scalar_tensor_tensor(
                plo[hl][:, 0:vw], in0=psrc[:, 0:vw], scalar=rz_,
                in1=phi[hl][:, 0:vw],
                op0=ALU.mult, op1=ALU.subtract))(hl, vw, psrc),
                reads=[b_psrc, c.b_cols, b_phi[hl]], writes=[b_phi[hl]])

        def emit_C1(hl, qt):
            nkb = (qt // 4 + 1) if causal else 4
            nkc = 4 * nkb
            pod = po[0:64, 0:128] if hl == 0 else yw[64:128, 0:128]
            b_pod = b_po if hl == 0 else b_yw
            tp = (0, hl * 64)
            pvfirst = True
            for g in range((nkc + 7) // 8):
                kcs = list(range(g * 8, min(nkc, g * 8 + 8)))
                for srcs, trp, b_trp, dstt in ((phi, tr0, b_tr0, trb_hi),
                                               (plo, tr1, b_tr1, trb_lo)):
                    fns = [(lambda kc, src, trp: lambda e: e.transpose(
                        trp[:, (kc % 8) * 128:((kc % 8) + 1) * 128],
                        src[:, kc * 128:(kc + 1) * 128], c.idn_bf[:]))(
                        kc, srcs[hl], trp) for kc in kcs]
                    S.emit("pe", fns, reads=[b_phi[hl], c.b_consts],
                           writes=[b_trp])
                    dst = dstt[:, hl * 1024:hl * 1024 + len(kcs) * 128]
                    S.emit("act", (lambda dst, trp, n: lambda e: e.activation(
                        out=dst, in_=trp[:, 0:n * 128], func=ACTF.Copy))(
                        dst, trp, len(kcs)),
                        reads=[b_trp], writes=[b_trb])
                fns = []
                for kc in kcs:
                    lv_hi = v_hi[:, kc * 128 + hl * 64:kc * 128 + hl * 64 + 64]
                    lv_lo = v_lo[:, kc * 128 + hl * 64:kc * 128 + hl * 64 + 64]
                    col = hl * 1024 + (kc % 8) * 128
                    rp_hi = trb_hi[:, col:col + 128]
                    rp_lo = trb_lo[:, col:col + 128]
                    last = (g == (nkc + 7) // 8 - 1) and kc == kcs[-1]
                    fns.append((lambda pod, tp, lv, rp, st: lambda e: e.matmul(
                        pod, lv, rp, start=st, stop=False, tile_position=tp))(
                        pod, tp, lv_hi, rp_hi, pvfirst))
                    pvfirst = False
                    fns.append((lambda pod, tp, lv, rp: lambda e: e.matmul(
                        pod, lv, rp, start=False, stop=False,
                        tile_position=tp))(pod, tp, lv_hi, rp_lo))
                    fns.append((lambda pod, tp, lv, rp, sp: lambda e: e.matmul(
                        pod, lv, rp, start=False, stop=sp,
                        tile_position=tp))(pod, tp, lv_lo, rp_hi, last))
                S.emit("pe", fns, reads=[b_v, b_trb], writes=[b_pod])
            oh = otq_hi[hl * 64:hl * 64 + 64, :]
            S.emit("act", (lambda oh, pod: lambda e: e.activation(
                out=oh, in_=pod, func=ACTF.Copy))(oh, pod),
                reads=[b_pod], writes=[b_otq])

        def emit_C_otqlo(hl, qt):
            pod = po[0:64, 0:128] if hl == 0 else yw[64:128, 0:128]
            b_pod = b_po if hl == 0 else b_yw
            oh = otq_hi[hl * 64:hl * 64 + 64, :]
            ol = otq_lo[hl * 64:hl * 64 + 64, :]
            S.emit("dve", (lambda ol, pod, oh: lambda e: e.tensor_sub(
                ol, pod, oh))(ol, pod, oh),
                reads=[b_pod, b_otq], writes=[b_otq])

        def emit_C_wo(hl, qt):
            oh = otq_hi[hl * 64:hl * 64 + 64, :]
            ol = otq_lo[hl * 64:hl * 64 + 64, :]
            for dob in range(2):
                wdst = po[:, 0:512] if dob == 0 else yw[:, 0:512]
                b_wdst = b_po if dob == 0 else b_yw
                fns = []
                for lh, rh, st, sp in ((oh, wB_hi, True, False),
                                       (oh, wB_lo, False, False),
                                       (ol, wB_hi, False, True)):
                    rhs = rh[hl * 64:hl * 64 + 64, dob * 512:(dob + 1) * 512]
                    fns.append((lambda lh, rhs, st, sp, rp, wdst:
                                lambda e: e.matmul(
                        wdst, lh, rhs, start=st, stop=sp,
                        tile_position=(rp, 0)))(lh, rhs, st, sp,
                                                hl * 64, wdst))
                S.emit("pe", fns, reads=[b_otq, b_wB], writes=[b_wdst])

        def emit_C_xadd(hl, qt):
            for dob in range(2):
                wdst = po[:, 0:512] if dob == 0 else yw[:, 0:512]
                b_wdst = b_po if dob == 0 else b_yw
                xs = c.X[:, qt * D + dob * 512:qt * D + (dob + 1) * 512]
                if causal:
                    S.emit("dve", (lambda xs, wdst: lambda e: e.tensor_add(
                        xs, xs, wdst))(xs, wdst),
                        reads=[b_wdst, c.b_x[qt]], writes=[c.b_x[qt]])
                else:
                    S.emit("dve", (lambda xs, wdst: lambda e:
                           e.scalar_tensor_tensor(
                        xs, in0=wdst, scalar=rz_, in1=xs,
                        op0=ALU.mult, op1=ALU.add))(xs, wdst),
                        reads=[b_wdst, c.b_x[qt], c.b_cols],
                        writes=[c.b_x[qt]])

        units = [(hl, qt) for qt in range(NT) for hl in range(2)]
        n_u = len(units)
        for i in range(n_u + 2):
            for _ in range(2):
                if ppw:
                    ppw.pop(0)()
            u0 = units[i] if i < n_u else None
            u1 = units[i - 1] if 1 <= i <= n_u else None
            u2 = units[i - 2] if i >= 2 else None
            if causal:
                if u0:
                    emit_A(*u0, i % 2)
                if u2:
                    emit_C1(*u2)
                if u1:
                    emit_B_stats(*u1, (i - 1) % 2)
                if u2:
                    emit_C_otqlo(*u2)
                    emit_C_wo(*u2)
                    emit_C_xadd(*u2)
                if u1:
                    emit_B_tail(*u1, (i - 1) % 2)
            else:
                if u0:
                    emit_A(*u0, i % 2)
                if u2:
                    emit_C1(*u2)
                if u1:
                    emit_B_rounds(*u1, (i - 1) % 2, 0, 6)
                if u2:
                    emit_C_otqlo(*u2)
                    emit_C_wo(*u2)
                if u1:
                    emit_B_rounds(*u1, (i - 1) % 2, 6, 8)
                    emit_B_mask(*u1, (i - 1) % 2)
                if u2:
                    emit_C_xadd(*u2)
                if u1:
                    emit_B_tail(*u1, (i - 1) % 2)
    est.close()



def _prep_ffn_prepass_work(c, est):
    S, nc, dram = c.S, c.nc, c.dram
    W4 = 512
    st0 = est.enter_context(nc.sbuf_tensor("pp_st0", [128, W4], F32))
    st_ = [st0, st0]
    hi_ = [est.enter_context(nc.sbuf_tensor(f"pp_h{i}", [128, W4], BF16))
           for i in range(2)]
    lo_ = [est.enter_context(nc.sbuf_tensor(f"pp_l{i}", [128, W4], BF16))
           for i in range(2)]
    b_st = S.buf("pp_st")
    b_hl = [S.buf("pp_hl0"), S.buf("pp_hl1")]

    def srcs(fam):
        if fam == "up":
            return dram["up_w"], dram["c_up_h"], dram["c_up_l"], NE
        if fam == "dn":
            return dram["down_w"], dram["c_dn_h"], dram["c_dn_l"], NE
        if fam == "shu":
            return dram["sh_up_w"], dram["c_shu_h"], dram["c_shu_l"], 1
        return dram["sh_down_w"], dram["c_shd_h"], dram["c_shd_l"], 1

    work = []

    def bias_job():
        for p in range(2):
            cs = slice(p * (D // 2), (p + 1) * (D // 2))
            S.dma("pool", (lambda cs: lambda e_: e_.dma_start(
                out=st_[0][0:NE, 0:D // 2], in_=dram["up_b"][:, cs]))(cs),
                reads=[b_st], writes=[b_st], sem="dppi")
            S.emit("act", (lambda p: lambda e_: e_.activation(
                out=hi_[p][0:NE, :], in_=st_[p][0:NE, :], func=ACTF.Copy))(p),
                reads=[b_st], writes=[b_hl[p]])
            S.emit("dve", (lambda p: lambda e_: e_.tensor_sub(
                lo_[p][0:NE, :], st_[p][0:NE, :], hi_[p][0:NE, :]))(p),
                reads=[b_st, b_hl[p]], writes=[b_hl[p]])
            S.dma("pool", [
                (lambda p, cs: lambda e_: e_.dma_start(
                    out=dram["c_ub_h"][:, cs], in_=hi_[p][0:NE, :]))(p, cs),
                (lambda p, cs: lambda e_: e_.dma_start(
                    out=dram["c_ub_l"][:, cs], in_=lo_[p][0:NE, :]))(p, cs),
            ], reads=[b_hl[p]], writes=[b_hl[p], c.b_cache], sem=f"dppo{p}")

    def shb_job():
        S.dma("pool", [
            lambda e_: e_.dma_start(out=st_[0][0:1, 0:D // 2],
                                    in_=dram["sh_up_b"][:, 0:D // 2]),
            lambda e_: e_.dma_start(out=st_[0][1:2, 0:D // 2],
                                    in_=dram["sh_up_b"][:, D // 2:D]),
            lambda e_: e_.dma_start(out=st_[0][2:3, 0:D // 2],
                                    in_=dram["sh_down_b"][:, 0:D // 2]),
            lambda e_: e_.dma_start(out=st_[0][3:4, 0:D // 2],
                                    in_=dram["sh_down_b"][:, D // 2:D]),
        ], reads=[b_st], writes=[b_st], sem="dppi")
        S.emit("act", lambda e_: e_.activation(
            out=hi_[0][0:4, :], in_=st_[0][0:4, :], func=ACTF.Copy),
            reads=[b_st], writes=[b_hl[0]])
        S.emit("dve", lambda e_: e_.tensor_sub(
            lo_[0][0:4, :], st_[0][0:4, :], hi_[0][0:4, :]),
            reads=[b_st, b_hl[0]], writes=[b_hl[0]])
        outs = [(dram["c_sub_h"], hi_[0]), (dram["c_sub_l"], lo_[0])]
        fns = []
        for dst, srct in outs:
            for r in range(2):
                for hf in range(2):
                    fns.append((lambda dst, srct, r, hf: lambda e_: e_.dma_start(
                        out=dst[r:r + 1, hf * (D // 2):(hf + 1) * (D // 2)],
                        in_=srct[2 * r + hf:2 * r + hf + 1, :]))(
                        dst, srct, r, hf))
        S.dma("pool", fns, reads=[b_hl[0]],
              writes=[b_hl[0], c.b_cache], sem="dppo0")

    work.append(bias_job)
    work.append(shb_job)
    for fam in ("up", "dn", "shu", "shd"):
        srct, dh, dl, n = srcs(fam)
        for e in range(n):
            for dc in range(NC):
                for hf in range(2):
                    def job(srct=srct, dh=dh, dl=dl, e=e, dc=dc, hf=hf,
                            p=[len(work) % 2]):
                        p = p[0]
                        rows = srct[(e * NC + dc) * 128:(e * NC + dc + 1) * 128,
                                    hf * W4:(hf + 1) * W4]
                        S.dma("pool", (lambda rows, p: lambda e_: e_.dma_start(
                            out=st_[p][:], in_=rows))(rows, p),
                            reads=[b_st], writes=[b_st], sem="dppi")
                        S.emit("act", (lambda p: lambda e_: e_.activation(
                            out=hi_[p][:], in_=st_[p][:], func=ACTF.Copy))(p),
                            reads=[b_st], writes=[b_hl[p]])
                        S.emit("dve", (lambda p: lambda e_: e_.tensor_sub(
                            lo_[p][:], st_[p][:], hi_[p][:]))(p),
                            reads=[b_st, b_hl[p]], writes=[b_hl[p]])
                        ohi = dh[e * 128:(e + 1) * 128,
                                 dc * D + hf * W4:dc * D + (hf + 1) * W4]
                        olo = dl[e * 128:(e + 1) * 128,
                                 dc * D + hf * W4:dc * D + (hf + 1) * W4]
                        S.dma("pool", [
                            (lambda ohi, p: lambda e_: e_.dma_start(
                                out=ohi, in_=hi_[p][:]))(ohi, p),
                            (lambda olo, p: lambda e_: e_.dma_start(
                                out=olo, in_=lo_[p][:]))(olo, p),
                        ], reads=[b_hl[p]],
                            writes=[b_hl[p], c.b_cache], sem=f"dppo{p}")
                    work.append(job)
    return work



def _emit_ffn(c):
    c.ffn_n = getattr(c, "ffn_n", 0) + 1
    pfx = f"f{c.ffn_n}"
    S, nc, dram = c.S, c.nc, c.dram
    B = S.buf
    est = contextlib.ExitStack()

    def sb(name, shape, dt=F32):
        return est.enter_context(nc.sbuf_tensor(f"{pfx}_{name}", list(shape), dt))

    QT_TOK = 512
    QT_TILES = 4

    xn_hi = [sb("xnh0", (128, NC * QT_TOK), BF16),
             sb("xnh1", (128, NC * QT_TOK), BF16)]
    xn_lo = [sb("xnl0", (128, NC * QT_TOK), BF16),
             sb("xnl1", (128, NC * QT_TOK), BF16)]
    upw_hi = sb("uwh", (128, NC * D), BF16); upw_lo = sb("uwl", (128, NC * D), BF16)
    dnw_hi = sb("dwh", (128, NC * D), BF16); dnw_lo = sb("dwl", (128, NC * D), BF16)
    hp_hi = sb("hph", (128, QT_TILES * D), BF16)
    hp_lo = sb("hpl", (128, QT_TILES * D), BF16)
    ht_hi = [sb("hth0", (128, D), BF16), sb("hth1", (128, D), BF16)]
    ht_lo = [sb("htl0", (128, D), BF16), sb("htl1", (128, D), BF16)]
    sdump = sb("sdump", (128, D))
    gw_hi = sb("gwh", (128, 64), BF16); gw_lo = sb("gwl", (128, 64), BF16)
    gst = sb("gst", (128, 64))
    comb = [sb("comb0", (128, QT_TILES * 8)), sb("comb1", (128, QT_TILES * 8))]
    smn = sb("smn", (128, 8))
    ct_hi = [sb("cth0", (8, QT_TILES * 128), BF16),
             sb("cth1", (8, QT_TILES * 128), BF16)]
    ct_lo = [sb("ctl0", (8, QT_TILES * 128), BF16),
             sb("ctl1", (8, QT_TILES * 128), BF16)]
    db_hi = sb("dbh", (NE, D), BF16); db_lo = sb("dbl", (NE, D), BF16)
    ubs_h = sb("ubsh", (1, D), BF16); ubs_l = sb("ubsl", (1, D), BF16)

    ups = [est.enter_context(nc.psum_tensor(f"{pfx}_up{i}", [128, 1024], F32))
           for i in range(2)]
    trf = est.enter_context(nc.psum_tensor(f"{pfx}_trf", [128, 1024], BF16))
    gps = est.enter_context(nc.psum_tensor(f"{pfx}_gps", [128, 512], F32))
    yps = est.enter_context(nc.psum_tensor(f"{pfx}_yps", [128, 1024], F32))

    b_xn = [B("fxn0"), B("fxn1")]; b_upw = B("fupw"); b_dnw = B("fdnw")
    b_hp = B("fhp"); b_ht = [B("fht0"), B("fht1")]
    b_gw = B("fgw"); b_comb = [B("fcomb0"), B("fcomb1")]
    b_ct = [B("fct0"), B("fct1")]
    b_bias = B("fbias"); b_ubs = B("fubs")
    b_ups = [B("fup0"), B("fup1")]
    b_trf = B("ftrf"); b_gps = B("fgps"); b_yps = B("fyps")
    b_sdump = B("fsdump"); b_smn = B("fsmn")

    S.dma("sp", lambda e: e.dma_start(out=gst[:], in_=dram["gate_sw"][:]),
          writes=[b_gw], sem="dgw")
    S.emit("act", lambda e: e.activation(out=gw_hi[:], in_=gst[:], func=ACTF.Copy),
           reads=[b_gw], writes=[b_gw])
    S.emit("dve", lambda e: e.tensor_sub(gw_lo[:], gst[:], gw_hi[:]),
           reads=[b_gw], writes=[b_gw])
    def split_bias(src_hbm, n_rows, out_hi, out_lo, semn):
        S.dma("sp", (lambda: lambda e: e.dma_start(
            out=sdump[0:n_rows, :], in_=src_hbm))(),
            reads=[b_sdump], writes=[b_sdump], sem=semn)
        S.emit("act", (lambda: lambda e: e.activation(
            out=out_hi[0:n_rows, :], in_=sdump[0:n_rows, :], func=ACTF.Copy))(),
            reads=[b_sdump], writes=[b_bias])
        S.emit("dve", (lambda: lambda e: e.tensor_sub(
            out_lo[0:n_rows, :], sdump[0:n_rows, :], out_hi[0:n_rows, :]))(),
            reads=[b_sdump, b_bias], writes=[b_bias])

    split_bias(dram["down_b"][:], NE, db_hi, db_lo, "dfb")

    def load_w(cache_h, cache_l, erow, whi, wlo, b_w, semn):
        S.dma("sp", [
            (lambda: lambda e: e.dma_start(
                out=whi[:], in_=cache_h[erow * 128:(erow + 1) * 128, :]))(),
            (lambda: lambda e: e.dma_start(
                out=wlo[:], in_=cache_l[erow * 128:(erow + 1) * 128, :]))(),
        ], reads=[b_w, c.b_cache], writes=[b_w], sem=semn)

    m_, nm_, z_, rz_, b2_, s2_, r2_ = (c.cols[:, i:i + 1] for i in range(16, 23))

    def emit_ln_gate_tile(qtr, par, i):
        tt = qtr * QT_TILES + i
        _emit_ln_tile(c, tt, xn_hi[par], xn_lo[par], b_xn[par], i, QT_TOK,
                      gps, b_gps, sdump, b_sdump)
        fns = []
        for dc in range(NC):
            lh_hi = xn_hi[par][:, dc * QT_TOK + i * 128:dc * QT_TOK + (i + 1) * 128]
            lh_lo = xn_lo[par][:, dc * QT_TOK + i * 128:dc * QT_TOK + (i + 1) * 128]
            rh_hi = gw_hi[:, dc * 8:(dc + 1) * 8]
            rh_lo = gw_lo[:, dc * 8:(dc + 1) * 8]
            fns.append((lambda lh, rh, st: lambda e: e.matmul(
                gps[:, 0:8], lh, rh, start=st, stop=False))(lh_hi, rh_hi, dc == 0))
            fns.append((lambda lh, rh: lambda e: e.matmul(
                gps[:, 0:8], lh, rh, start=False, stop=False))(lh_hi, rh_lo))
            fns.append((lambda lh, rh, sp: lambda e: e.matmul(
                gps[:, 0:8], lh, rh, start=False, stop=sp))(
                lh_lo, rh_hi, dc == NC - 1))
        S.emit("pe", fns, reads=[b_xn[par], b_gw], writes=[b_gps])
        S.emit("dve", lambda e: e.reduce_max(m_, gps[:, 0:8], axis=AX),
               reads=[b_gps], writes=[c.b_cols])
        S.emit("dve", lambda e: e.tensor_scalar(nm_, m_, -1.0, None,
                                                op0=ALU.mult),
               reads=[c.b_cols], writes=[c.b_cols])
        S.emit("act", lambda e: e.activation(out=smn[:], in_=gps[:, 0:8],
                                             func=ACTF.Exp, bias=nm_,
                                             accum_out=z_),
               reads=[b_gps, c.b_cols], writes=[b_smn, c.b_cols])
        S.emit("dve", lambda e: e.reciprocal(rz_, z_),
               reads=[c.b_cols], writes=[c.b_cols])
        S.emit("act", lambda e: e.activation(out=smn[:], in_=smn[:],
                                             func=ACTF.Copy, scale=rz_),
               reads=[b_smn, c.b_cols], writes=[b_smn])
        S.emit("dve", lambda e: e.max(out=c.m8[:], in_=smn[:]),
               reads=[b_smn], writes=[c.b_m8])
        S.emit("dve", lambda e: e.tensor_add(s2_, c.m8[:, 0:1], c.m8[:, 1:2]),
               reads=[c.b_m8], writes=[c.b_cols])
        S.emit("dve", lambda e: e.tensor_scalar(s2_, s2_, 1e-20, None,
                                                op0=ALU.add),
               reads=[c.b_cols], writes=[c.b_cols])
        S.emit("dve", lambda e: e.reciprocal(r2_, s2_),
               reads=[c.b_cols], writes=[c.b_cols])
        cmb = comb[par][:, i * 8:(i + 1) * 8]
        S.emit("dve", (lambda cmb: lambda e: e.scalar_tensor_tensor(
            cmb, in0=smn[:], scalar=c.m8[:, 1:2], in1=smn[:],
            op0=ALU.is_ge, op1=ALU.mult))(cmb),
            reads=[b_smn, c.b_m8], writes=[b_comb[par]])
        S.emit("dve", (lambda cmb: lambda e: e.tensor_scalar(
            cmb, cmb, r2_, None, op0=ALU.mult))(cmb),
            reads=[b_comb[par], c.b_cols], writes=[b_comb[par]])

    def emit_combT(qtr, par):
        fns = [(lambda i, par: lambda e: e.transpose(
            gps[0:8, i * 128:(i + 1) * 128],
            comb[par][:, i * 8:(i + 1) * 8], c.idn_f32[:]))(i, par)
            for i in range(QT_TILES)]
        S.emit("pe", fns, reads=[b_comb[par], c.b_consts], writes=[b_gps])
        S.emit("act", (lambda par: lambda e: e.activation(
            out=ct_hi[par][:], in_=gps[0:8, 0:512], func=ACTF.Copy))(par),
            reads=[b_gps], writes=[b_ct[par]])
        S.emit("dve", (lambda par: lambda e: e.tensor_sub(
            ct_lo[par][:], gps[0:8, 0:512], ct_hi[par][:]))(par),
            reads=[b_gps, b_ct[par]], writes=[b_ct[par]])

    def emit_refill(e_next):
        if e_next == NE:
            hsrc, lsrc = dram["c_sub_h"][0:1, :], dram["c_sub_l"][0:1, :]
        elif e_next == NE + 1:
            hsrc, lsrc = dram["c_sub_h"][1:2, :], dram["c_sub_l"][1:2, :]
        else:
            hsrc = dram["c_ub_h"][e_next:e_next + 1, :]
            lsrc = dram["c_ub_l"][e_next:e_next + 1, :]
        S.dma("sp", [
            (lambda hsrc: lambda e_: e_.dma_start(
                out=ubs_h[0:1, :], in_=hsrc))(hsrc),
            (lambda lsrc: lambda e_: e_.dma_start(
                out=ubs_l[0:1, :], in_=lsrc))(lsrc),
        ], reads=[b_ubs, c.b_cache], writes=[b_ubs], sem="dub")

    for i in range(QT_TILES):
        emit_ln_gate_tile(0, 0, i)
    emit_combT(0, 0)

    for qtr in range(4):
        par = qtr % 2
        tts = range(qtr * QT_TILES, (qtr + 1) * QT_TILES)
        load_w(dram["c_up_h"], dram["c_up_l"], 0, upw_hi, upw_lo, b_upw, "dfu")
        emit_refill(0)
        for e in range(NE + 1):
            shared = e == NE
            if shared:
                load_w(dram["c_shd_h"], dram["c_shd_l"], 0, dnw_hi, dnw_lo,
                       b_dnw, "dfd")
            else:
                load_w(dram["c_dn_h"], dram["c_dn_l"], e, dnw_hi, dnw_lo,
                       b_dnw, "dfd")
            for i, tt in enumerate(tts):
                up = ups[i % 2]
                b_up = b_ups[i % 2]
                h32 = yps[:, 0:D]
                b_h32 = b_yps
                for dhb in range(2):
                    fns = []
                    dst = up[:, dhb * 512:(dhb + 1) * 512]
                    fns.append((lambda dst, dhb: lambda e_: e_.matmul(
                        dst, c.ones2[0:1, :], ubs_h[0:1, dhb * 512:(dhb + 1) * 512],
                        start=True, stop=False))(dst, dhb))
                    fns.append((lambda dst, dhb: lambda e_: e_.matmul(
                        dst, c.ones2[0:1, :], ubs_l[0:1, dhb * 512:(dhb + 1) * 512],
                        start=False, stop=False))(dst, dhb))
                    for dc in range(NC):
                        lh_hi = xn_hi[par][:, dc * QT_TOK + i * 128:
                                           dc * QT_TOK + (i + 1) * 128]
                        lh_lo = xn_lo[par][:, dc * QT_TOK + i * 128:
                                           dc * QT_TOK + (i + 1) * 128]
                        rh_hi = upw_hi[:, dc * D + dhb * 512:dc * D + (dhb + 1) * 512]
                        rh_lo = upw_lo[:, dc * D + dhb * 512:dc * D + (dhb + 1) * 512]
                        fns.append((lambda dst, lh, rh: lambda e_: e_.matmul(
                            dst, lh, rh, start=False, stop=False))(dst, lh_hi, rh_hi))
                        fns.append((lambda dst, lh, rh: lambda e_: e_.matmul(
                            dst, lh, rh, start=False, stop=False))(dst, lh_hi, rh_lo))
                        fns.append((lambda dst, lh, rh, sp: lambda e_: e_.matmul(
                            dst, lh, rh, start=False, stop=sp))(
                            dst, lh_lo, rh_hi, dc == NC - 1))
                    S.emit("pe", fns, reads=[b_xn[par], b_upw, b_ubs, c.b_consts],
                           writes=[b_up])
                    S.emit("act", (lambda up, dhb: lambda e_: e_.activation(
                        out=h32[:, dhb * 512:(dhb + 1) * 512],
                        in_=up[:, dhb * 512:(dhb + 1) * 512],
                        func=ACTF.Silu))(up, dhb),
                        reads=[b_up], writes=[b_h32])
                hh = hp_hi[:, i * D:(i + 1) * D]
                hl = hp_lo[:, i * D:(i + 1) * D]
                if shared:
                    S.emit("act", (lambda hh: lambda e_: e_.activation(
                        out=hh, in_=h32[:], func=ACTF.Copy))(hh),
                        reads=[b_h32], writes=[b_hp])
                    S.emit("dve", (lambda hl, hh: lambda e_: e_.tensor_sub(
                        hl, h32[:], hh))(hl, hh),
                        reads=[b_h32, b_hp], writes=[b_hp])
                else:
                    cc = comb[par][:, i * 8 + e:i * 8 + e + 1]
                    S.emit("act", (lambda hh, cc: lambda e_: e_.activation(
                        out=hh, in_=h32[:], func=ACTF.Copy, scale=cc))(hh, cc),
                        reads=[b_h32, b_comb[par]], writes=[b_hp])
                    S.emit("dve", (lambda hl, hh, cc: lambda e_:
                           e_.scalar_tensor_tensor(hl, in0=h32[:], scalar=cc,
                                                   in1=hh, op0=ALU.mult,
                                                   op1=ALU.subtract))(hl, hh, cc),
                           reads=[b_h32, b_comb[par], b_hp], writes=[b_hp])
            if e < NE:
                if e == NE - 1:
                    load_w(dram["c_shu_h"], dram["c_shu_l"], 0, upw_hi, upw_lo,
                           b_upw, "dfu")
                else:
                    load_w(dram["c_up_h"], dram["c_up_l"], e + 1, upw_hi, upw_lo,
                           b_upw, "dfu")
                emit_refill(e + 1)
            else:
                emit_refill(NE + 1)
            if qtr < 3 and 4 <= e <= 7:
                emit_ln_gate_tile(qtr + 1, (qtr + 1) % 2, e - 4)
            if qtr < 3 and e == NE:
                emit_combT(qtr + 1, (qtr + 1) % 2)
            def emit_tr(i):
                hh = hp_hi[:, i * D:(i + 1) * D]
                hl = hp_lo[:, i * D:(i + 1) * D]
                for src, dst_t in ((hh, ht_hi[i % 2]), (hl, ht_lo[i % 2])):
                    fns = [(lambda dc, src: lambda e_: e_.transpose(
                        trf[:, dc * 128:(dc + 1) * 128],
                        src[:, dc * 128:(dc + 1) * 128], c.idn_bf[:]))(dc, src)
                        for dc in range(NC)]
                    S.emit("pe", fns, reads=[b_hp, c.b_consts], writes=[b_trf])
                    S.emit("act", (lambda dst_t: lambda e_: e_.activation(
                        out=dst_t[:], in_=trf[:], func=ACTF.Copy))(dst_t),
                        reads=[b_trf], writes=[b_ht[i % 2]])

            def emit_down(i, tt):
                fns = []
                for dob in range(2):
                    dst = yps[:, dob * 512:(dob + 1) * 512]
                    first = True
                    if shared:
                        fns.append((lambda dst, dob: lambda e_: e_.matmul(
                            dst, c.ones2[0:1, :],
                            ubs_h[0:1, dob * 512:(dob + 1) * 512],
                            start=True, stop=False))(dst, dob))
                        fns.append((lambda dst, dob: lambda e_: e_.matmul(
                            dst, c.ones2[0:1, :],
                            ubs_l[0:1, dob * 512:(dob + 1) * 512],
                            start=False, stop=False))(dst, dob))
                        first = False
                        cth = ct_hi[par][:, i * 128:(i + 1) * 128]
                        ctl = ct_lo[par][:, i * 128:(i + 1) * 128]
                        dbh = db_hi[:, dob * 512:(dob + 1) * 512]
                        dbl = db_lo[:, dob * 512:(dob + 1) * 512]
                        for lh, rh in ((cth, dbh), (cth, dbl), (ctl, dbh)):
                            fns.append((lambda dst, lh, rh: lambda e_: e_.matmul(
                                dst, lh, rh, start=False, stop=False))(dst, lh, rh))
                    for dc in range(NC):
                        lh_hi = ht_hi[i % 2][:, dc * 128:(dc + 1) * 128]
                        lh_lo = ht_lo[i % 2][:, dc * 128:(dc + 1) * 128]
                        rh_hi = dnw_hi[:, dc * D + dob * 512:dc * D + (dob + 1) * 512]
                        rh_lo = dnw_lo[:, dc * D + dob * 512:dc * D + (dob + 1) * 512]
                        fns.append((lambda dst, lh, rh, st: lambda e_: e_.matmul(
                            dst, lh, rh, start=st, stop=False))(
                            dst, lh_hi, rh_hi, first and dc == 0))
                        fns.append((lambda dst, lh, rh: lambda e_: e_.matmul(
                            dst, lh, rh, start=False, stop=False))(dst, lh_hi, rh_lo))
                        fns.append((lambda dst, lh, rh, sp: lambda e_: e_.matmul(
                            dst, lh, rh, start=False, stop=sp))(
                            dst, lh_lo, rh_hi, dc == NC - 1))
                S.emit("pe", fns, reads=[b_ht[i % 2], b_dnw, b_ct[par], b_bias,
                                         b_ubs, c.b_consts],
                       writes=[b_yps])
                for dob in range(2):
                    xs = c.X[:, tt * D + dob * 512:tt * D + (dob + 1) * 512]
                    S.emit("dve", (lambda xs, dob: lambda e_: e_.tensor_add(
                        xs, xs, yps[:, dob * 512:(dob + 1) * 512]))(xs, dob),
                        reads=[b_yps, c.b_x[tt]], writes=[c.b_x[tt]])

            tl = list(tts)
            for i, tt in enumerate(tl):
                emit_tr(i)
                if i >= 1:
                    emit_down(i - 1, tl[i - 1])
            emit_down(len(tl) - 1, tl[-1])
    est.close()



def _prep_shared(inputs, pfx_map):
    f32 = np.float32
    bf = ml_dtypes.bfloat16
    d = {}
    for pfx, nm in pfx_map.items():
        wq = np.asarray(inputs[f"{nm}_Wq"], f32)
        wk = np.asarray(inputs[f"{nm}_Wk"], f32)
        wv = np.asarray(inputs[f"{nm}_Wv"], f32)
        wo = np.asarray(inputs[f"{nm}_Wo"], f32)

        def pan(w):
            a = w.T.reshape(8, 128, 8, 128).transpose(2, 1, 0, 3)
            return np.ascontiguousarray(a.reshape(D, D))

        d[f"{pfx}_wq"] = pan(wq)
        d[f"{pfx}_wk"] = pan(wk)
        d[f"{pfx}_wv"] = pan(wv)
        d[f"{pfx}_wo"] = np.ascontiguousarray(wo.T.reshape(D, D))
        d[f"{pfx}_bq"] = np.ascontiguousarray(
            np.asarray(inputs[f"{nm}_bq"], f32).reshape(8, 128).T)
        d[f"{pfx}_bk"] = np.ascontiguousarray(
            np.asarray(inputs[f"{nm}_bk"], f32).reshape(8, 128).T)
        d[f"{pfx}_bv"] = np.ascontiguousarray(
            np.asarray(inputs[f"{nm}_bv"], f32).reshape(8, 128).T)
        d[f"{pfx}_bo"] = np.asarray(inputs[f"{nm}_bo"], f32).reshape(1, D)
    d["up_w"] = np.ascontiguousarray(
        np.asarray(inputs["L0_ffn_up_w"], f32).reshape(NE * D, D))
    d["down_w"] = np.ascontiguousarray(
        np.asarray(inputs["L0_ffn_down_w"], f32).reshape(NE * D, D))
    d["sh_up_w"] = np.ascontiguousarray(np.asarray(inputs["L0_ffn_sh_up_w"], f32))
    d["sh_down_w"] = np.ascontiguousarray(np.asarray(inputs["L0_ffn_sh_down_w"], f32))
    d["up_b"] = np.ascontiguousarray(np.asarray(inputs["L0_ffn_up_b"], f32))
    d["down_b"] = np.ascontiguousarray(np.asarray(inputs["L0_ffn_down_b"], f32))
    d["sh_up_b"] = np.asarray(inputs["L0_ffn_sh_up_b"], f32).reshape(1, D)
    d["sh_down_b"] = np.asarray(inputs["L0_ffn_sh_down_b"], f32).reshape(1, D)
    gt = np.asarray(inputs["L0_ffn_gate_w"], f32).T
    d["gate_sw"] = np.ascontiguousarray(
        gt.reshape(8, 128, 8).transpose(1, 0, 2).reshape(128, 64))
    d["idn_bf"] = np.eye(128, dtype=bf)
    d["idn_f32"] = np.eye(128, dtype=f32)
    d["ones2"] = np.ones((2, 128), dtype=bf)
    d["masks4"] = _mask_variants()
    return d


def kernel(**inputs):
    pfx_map = {"a": "L0_attn_a", "b": "L0_attn_b"}
    shared = _prep_shared(inputs, pfx_map)
    x = np.ascontiguousarray(np.asarray(inputs["L0_attn_a_x"], np.float32))
    nc = build_program()
    in_maps = []
    for core in range(8):
        m = dict(shared)
        m["x"] = np.ascontiguousarray(x[core])
        in_maps.append(m)
    res = run_bass_kernel_spmd(nc, in_maps, core_ids=list(range(8)))
    out = np.stack([np.asarray(r["out"], np.float32) for r in res.results], axis=0)
    return out


# version 20 (final)
# speedup vs baseline: 1.0879x; 1.0137x over previous
import contextlib
import numpy as np
import ml_dtypes
import concourse.bass as bass
import concourse.mybir as mybir
from concourse.bass_utils import run_bass_kernel_spmd

F32 = mybir.dt.float32
BF16 = mybir.dt.bfloat16
AX = mybir.AxisListType.X
ALU = mybir.AluOpType
ACTF = mybir.ActivationFunctionType

T = 2048
D = 1024
NT = 16
NC = 8
H = 16
DH = 64
NPAIR = 8
NE = 8
TOPK = 64
NEG = -1.0e30

N_SUBLAYERS = 4


ENGINES = ("pe", "act", "dve", "pool", "sp")
_ENG_ATTR = {"pe": "tensor", "act": "scalar", "dve": "vector", "pool": "gpsimd",
             "sp": "sync"}


class Buf:
    __slots__ = ("name", "last_w", "readers")

    def __init__(self, name):
        self.name = name
        self.last_w = None
        self.readers = {}


class Sched:
    def __init__(self, nc):
        self.nc = nc
        self.sems = {}
        self.counts = {e: 0 for e in ENGINES}
        self.dma_counts = {}
        self.known = {e: {} for e in ENGINES}
        self.streams = {e: [] for e in ENGINES}
        self._sem_names = set()

    def buf(self, name):
        return Buf(name)

    def _need_waits(self, eng, reads, writes):
        req = {}
        for b in reads:
            if b.last_w is not None:
                s, v = b.last_w
                if req.get(s, -1) < v:
                    req[s] = v
        for b in writes:
            if b.last_w is not None:
                s, v = b.last_w
                if req.get(s, -1) < v:
                    req[s] = v
            for s, v in b.readers.items():
                if req.get(s, -1) < v:
                    req[s] = v
        out = []
        kn = self.known[eng]
        own = "s_" + eng
        for s, v in sorted(req.items()):
            if s == own and kn.get(s, -1) >= v:
                continue
            if kn.get(s, -1) < v:
                out.append((s, v))
                kn[s] = v
        return out

    def _record(self, sem, val, reads, writes):
        for b in reads:
            if b.readers.get(sem, -1) < val:
                b.readers[sem] = val
        for b in writes:
            b.last_w = (sem, val)
            b.readers = {}

    def emit(self, eng, fns, reads=(), writes=()):
        if callable(fns):
            fns = [fns]
        waits = self._need_waits(eng, reads, writes)
        sem = "s_" + eng
        self._sem_names.add(sem)
        self.counts[eng] += 1
        val = self.counts[eng]
        self.streams[eng].append(("op", fns, waits, sem, 1))
        self._record(sem, val, reads, writes)
        return val

    def dma(self, eng, fns, reads=(), writes=(), sem=None):
        if callable(fns):
            fns = [fns]
        waits = self._need_waits(eng, reads, writes)
        sem = sem if sem is not None else ("d_" + eng)
        self._sem_names.add(sem)
        self.dma_counts[sem] = self.dma_counts.get(sem, 0) + 16 * len(fns)
        val = self.dma_counts[sem]
        self.streams[eng].append(("dma", fns, waits, sem, 16))
        self._record(sem, val, reads, writes)
        return val

    def fence(self):
        snap_e = {e: self.counts[e] for e in ENGINES}
        snap_d = dict(self.dma_counts)
        for e in ENGINES:
            ws = [("s_" + o, snap_e[o]) for o in ENGINES
                  if o != e and snap_e[o] > 0]
            ws += [(s, v) for s, v in snap_d.items() if v > 0]
            for s, _ in ws:
                self._sem_names.add(s)
            if ws:
                self.streams[e].append(("waitonly", None, ws, None, 0))
                kn = self.known[e]
                for s, v in ws:
                    if kn.get(s, -1) < v:
                        kn[s] = v

    def drain_all(self, eng="pool"):
        ws = [("s_" + e, self.counts[e]) for e in ENGINES
              if e != eng and self.counts[e] > 0]
        ws += [(s, v) for s, v in self.dma_counts.items() if v > 0]
        for s, _ in ws:
            self._sem_names.add(s)
        self.streams[eng].append(("waitonly", None, ws, None, 0))

    def finalize(self, block, exit_stack):
        for s in sorted(self._sem_names):
            self.sems[s] = exit_stack.enter_context(self.nc.semaphore(s))

        def run_stream(name):
            def body(eng):
                for kind, fns, waits, sem, _amt in self.streams[name]:
                    for ws, wv in waits:
                        eng.wait_ge(self.sems[ws], wv)
                    if kind == "waitonly":
                        continue
                    insts = [fn(eng) for fn in fns]
                    if kind == "op":
                        insts[-1].then_inc(self.sems[sem], 1)
                    else:
                        for inst in insts:
                            inst.then_inc(self.sems[sem], 16)
            return body

        for e in ENGINES:
            if self.streams[e]:
                getattr(block, _ENG_ATTR[e])(run_stream(e))



def _mask_variants():
    r = np.arange(128)[:, None]
    j = np.arange(896)[None, :]
    return np.ascontiguousarray(
        np.where(j <= r + 384, 0.0, NEG)).astype(ml_dtypes.bfloat16)


class Ctx:
    pass


def build_program(n_sub=None):
    n_sub = N_SUBLAYERS if n_sub is None else n_sub
    nc = bass.Bass("TRN2", target_bir_lowering=False)
    c = Ctx()
    c.nc = nc
    S = Sched(nc)
    c.S = S
    est = contextlib.ExitStack()
    c.est = est

    dram = {}

    def din(name, shape, dt=F32):
        dram[name] = nc.dram_tensor(name, list(shape), dt, kind="ExternalInput").ap()
        return dram[name]

    din("x", (T, D))
    for p in ("a", "b"):
        din(f"{p}_wq", (D, D)); din(f"{p}_wk", (D, D)); din(f"{p}_wv", (D, D))
        din(f"{p}_wo", (D, D))
        din(f"{p}_bq", (128, 8)); din(f"{p}_bk", (128, 8)); din(f"{p}_bv", (128, 8))
        din(f"{p}_bo", (1, D))
    din("up_w", (NE * D, D)); din("down_w", (NE * D, D))
    din("sh_up_w", (D, D)); din("sh_down_w", (D, D))
    din("up_b", (NE, D)); din("down_b", (NE, D))
    din("sh_up_b", (1, D)); din("sh_down_b", (1, D))
    din("gate_sw", (128, 64))
    din("idn_bf", (128, 128), BF16)
    din("idn_f32", (128, 128))
    din("ones2", (2, 128), BF16)
    din("masks4", (128, 896), BF16)
    out_h = nc.dram_tensor("out", [T, D], F32, kind="ExternalOutput").ap()

    def dsc(name, shape, dt=BF16):
        t = nc.dram_tensor(name, list(shape), dt, kind="Internal")
        dram[name] = t.ap()
        return dram[name]

    dsc("c_up_h", (NE * 128, NC * D)); dsc("c_up_l", (NE * 128, NC * D))
    dsc("c_dn_h", (NE * 128, NC * D)); dsc("c_dn_l", (NE * 128, NC * D))
    dsc("c_shu_h", (128, NC * D)); dsc("c_shu_l", (128, NC * D))
    dsc("c_shd_h", (128, NC * D)); dsc("c_shd_l", (128, NC * D))
    dsc("c_ub_h", (NE, D)); dsc("c_ub_l", (NE, D))
    dsc("c_sub_h", (2, D)); dsc("c_sub_l", (2, D))
    c.dram = dram

    def sb(name, shape, dt=F32):
        return est.enter_context(nc.sbuf_tensor(name, list(shape), dt))

    c.X = sb("X", (128, NT * D))
    c.idn_bf = sb("idn_bf_t", (128, 128), BF16)
    c.idn_f32 = sb("idn_f32_t", (128, 128))
    c.ones2 = sb("ones2_t", (2, 128), BF16)
    c.cols = sb("cols", (128, 32))
    c.m8 = sb("m8t", (128, 8))

    B = S.buf
    c.b_x = [B(f"x{t}") for t in range(NT)]
    c.b_consts = B("consts")
    c.b_cols = B("cols")
    c.b_m8 = B("m8")
    c.b_out = [B(f"out{t}") for t in range(NT)]
    c.b_cache = B("wcache")

    for tt in range(NT):
        S.dma("sp", (lambda tt: lambda e: e.dma_start(
            out=c.X[:, tt * D:(tt + 1) * D],
            in_=dram["x"][tt * 128:(tt + 1) * 128, :]))(tt),
            writes=[c.b_x[tt]], sem=f"dx{tt}")
    S.dma("sp", [
        lambda e: e.dma_start(out=c.idn_bf[:], in_=dram["idn_bf"][:]),
        lambda e: e.dma_start(out=c.idn_f32[:], in_=dram["idn_f32"][:]),
        lambda e: e.dma_start(out=c.ones2[:], in_=dram["ones2"][:]),
    ], writes=[c.b_consts], sem="dconst")

    if n_sub >= 1:
        _emit_attn(c, "a", causal=True, prepass=(n_sub >= 2))
    if n_sub >= 2:
        S.fence()
        _emit_ffn(c)
    if n_sub >= 3:
        S.fence()
        _emit_attn(c, "b", causal=False, prepass=False)
    if n_sub >= 4:
        S.fence()
        _emit_ffn(c)

    for tt in range(NT):
        S.dma("sp", (lambda tt: lambda e: e.dma_start(
            out=out_h[tt * 128:(tt + 1) * 128, :],
            in_=c.X[:, tt * D:(tt + 1) * D]))(tt),
            reads=[c.b_x[tt]], writes=[c.b_out[tt]], sem=f"do{tt}")
    S.drain_all("pool")

    block = est.enter_context(nc.Block())
    S.finalize(block, est)
    est.close()
    return nc



def _emit_ln_tile(c, tt, xn_hi, xn_lo, b_xn, tloc, t_span, ps, b_ps, sdump, b_sdump):
    S, nc, X = c.S, c.nc, c.X
    xs = X[:, tt * D:(tt + 1) * D]
    su, sq, mn, nv, rs, nm = (c.cols[:, i:i + 1] for i in range(6))
    S.emit("dve", (lambda xs: lambda e: e.reduce_sum(su, xs, axis=AX))(xs),
           reads=[c.b_x[tt]], writes=[c.b_cols])
    S.emit("act", (lambda xs: lambda e: e.activation(
        out=sdump[:, 0:D], in_=xs, func=ACTF.Square, accum_out=sq))(xs),
        reads=[c.b_x[tt]], writes=[c.b_cols, b_sdump])
    S.emit("dve", lambda e: e.tensor_scalar(mn, su, 1.0 / D, None, op0=ALU.mult),
           reads=[c.b_cols], writes=[c.b_cols])
    S.emit("dve", lambda e: e.tensor_scalar(nv, sq, -1.0 / D, -1e-5,
                                            op0=ALU.mult, op1=ALU.add),
           reads=[c.b_cols], writes=[c.b_cols])
    S.emit("dve", lambda e: e.scalar_tensor_tensor(
        nv, in0=mn, scalar=mn, in1=nv, op0=ALU.mult, op1=ALU.add),
        reads=[c.b_cols], writes=[c.b_cols])
    S.emit("act", lambda e: e.activation(out=rs, in_=nv, func=ACTF.Sqrt,
                                         scale=-1.0),
           reads=[c.b_cols], writes=[c.b_cols])
    S.emit("dve", lambda e: e.reciprocal(rs, rs),
           reads=[c.b_cols], writes=[c.b_cols])
    S.emit("dve", lambda e: e.tensor_scalar(nm, mn, rs, -1.0,
                                            op0=ALU.mult, op1=ALU.mult),
           reads=[c.b_cols], writes=[c.b_cols])
    S.emit("act", (lambda xs: lambda e: e.activation(
        out=sdump[:, 0:D], in_=xs, func=ACTF.Identity, scale=rs, bias=nm))(xs),
        reads=[c.b_x[tt], c.b_cols], writes=[b_sdump])
    for half in range(2):
        fns = []
        for q in range(4):
            dc = half * 4 + q
            fns.append((lambda dc, q: lambda e: e.transpose(
                ps[:, q * 128:(q + 1) * 128],
                sdump[:, dc * 128:(dc + 1) * 128], c.idn_f32[:]))(dc, q))
        S.emit("pe", fns, reads=[b_sdump, c.b_consts], writes=[b_ps])
        dst_hi = xn_hi[:, :].rearrange("p (cc t) -> p cc t", t=t_span)[
            :, half * 4:half * 4 + 4, tloc * 128:(tloc + 1) * 128]
        dst_lo = xn_lo[:, :].rearrange("p (cc t) -> p cc t", t=t_span)[
            :, half * 4:half * 4 + 4, tloc * 128:(tloc + 1) * 128]
        src = ps[:, 0:512]
        S.emit("act", (lambda dst_hi, src: lambda e: e.activation(
            out=dst_hi, in_=src, func=ACTF.Copy))(dst_hi, src),
            reads=[b_ps], writes=[b_xn])
        S.emit("dve", (lambda dst_lo, src, dst_hi: lambda e: e.tensor_sub(
            dst_lo, src, dst_hi))(dst_lo, src, dst_hi),
            reads=[b_ps, b_xn], writes=[b_xn])



def _emit_attn(c, pfx, causal, prepass):
    S, nc, dram = c.S, c.nc, c.dram
    B = S.buf
    est = contextlib.ExitStack()

    def sb(name, shape, dt=F32):
        return est.enter_context(nc.sbuf_tensor(f"{pfx}_{name}", list(shape), dt))

    xn_hi = sb("xnh", (128, NC * T), BF16)
    xn_lo = sb("xnl", (128, NC * T), BF16)
    qt_hi = sb("qth", (128, T), BF16); qt_lo = sb("qtl", (128, T), BF16)
    kt_hi = sb("kth", (128, T), BF16); kt_lo = sb("ktl", (128, T), BF16)
    v_hi = sb("vh", (128, T), BF16); v_lo = sb("vl", (128, T), BF16)
    trb_hi = sb("trbh", (128, T), BF16); trb_lo = sb("trbl", (128, T), BF16)
    phi = [sb("phi0", (128, T), BF16), sb("phi1", (128, T), BF16)]
    plo = [sb("plo0", (128, T), BF16), sb("plo1", (128, T), BF16)]
    ssb = [sb("ssb0", (128, T)), sb("ssb1", (128, T))]
    if not causal:
        sx = sb("sx", (128, T))
    otq_hi = sb("otqh", (128, 128), BF16); otq_lo = sb("otql", (128, 128), BF16)
    if causal:
        c.masks = sb("masks", (128, 896), BF16)
        S.dma("sp", lambda e: e.dma_start(out=c.masks[:], in_=dram["masks4"][:]),
              writes=[c.b_consts], sem="dmask")
    big32 = ssb[0] if causal else sx
    b_big32 = None
    wst = sb("wst", (128, 512))
    wB_hi = sb("wBh", (128, D), BF16); wB_lo = sb("wBl", (128, D), BF16)
    bqkv = sb("bqkv", (128, 24))

    ps = est.enter_context(nc.psum_tensor(f"{pfx}_ps", [128, 2048], F32))
    tr0 = est.enter_context(nc.psum_tensor(f"{pfx}_tr0", [128, 1024], BF16))
    tr1 = est.enter_context(nc.psum_tensor(f"{pfx}_tr1", [128, 1024], BF16))
    po = est.enter_context(nc.psum_tensor(f"{pfx}_po", [128, 512], F32))
    yw = est.enter_context(nc.psum_tensor(f"{pfx}_yw", [128, 512], F32))

    b_xn = B("xn"); b_qt = B("qt"); b_kt = B("kt"); b_v = B("v")
    b_trb = B("trb"); b_phi = [B("phi0"), B("phi1")]
    b_psb = [B("psb0"), B("psb1"), B("psb2"), B("psb3")]
    b_ssb = [B("ssb0"), B("ssb1")]; b_sx = B("sx")
    b_big32 = b_ssb[0] if causal else b_sx
    bsp_h = phi[0][0:1, 0:D]
    bsp_l = phi[0][0:1, D:2 * D]
    b_bsp_ali = b_phi[0]
    b_otq = B("otq"); b_wst = B("wst")
    b_wB = B("wB"); b_bsp = B("bsp"); b_bqkv = B("bqkv")
    b_ps = B("ps"); b_tr0 = B("tr0"); b_tr1 = B("tr1")
    b_po = B("po"); b_yw = B("yw")
    sdump = big32
    b_sdump = b_big32

    ppw = _prep_ffn_prepass_work(c, est) if prepass else []
    for tt in range(NT):
        _emit_ln_tile(c, tt, xn_hi, xn_lo, b_xn, tt, T, ps, b_psb[0],
                      sdump, b_sdump)

    S.dma("sp", lambda e: e.dma_start(
        out=bqkv[:, 0:8], in_=dram[f"{pfx}_bq"][:]), writes=[b_bqkv], sem="dbias")
    S.dma("sp", lambda e: e.dma_start(
        out=bqkv[:, 8:16], in_=dram[f"{pfx}_bk"][:]), writes=[b_bqkv], sem="dbias")
    S.dma("sp", lambda e: e.dma_start(
        out=bqkv[:, 16:24], in_=dram[f"{pfx}_bv"][:]), writes=[b_bqkv], sem="dbias")
    S.dma("sp", lambda e: e.dma_start(out=big32[0:1, 0:D], in_=dram[f"{pfx}_bo"][:]),
          reads=[b_big32], writes=[b_big32], sem="dbias2")
    S.emit("act", lambda e: e.activation(out=bsp_h, in_=big32[0:1, 0:D],
                                         func=ACTF.Copy),
           reads=[b_big32], writes=[b_bsp_ali])
    S.emit("dve", lambda e: e.tensor_sub(bsp_l, big32[0:1, 0:D], bsp_h),
           reads=[b_big32, b_bsp_ali], writes=[b_bsp_ali])

    ones1 = c.ones2[0:1, :]
    for tt in range(NT):
        for dob in range(2):
            S.emit("pe", [
                (lambda dob: lambda e: e.matmul(
                    yw[:, :], ones1, bsp_h[0:1, dob * 512:(dob + 1) * 512],
                    start=True, stop=False))(dob),
                (lambda dob: lambda e: e.matmul(
                    yw[:, :], ones1, bsp_l[0:1, dob * 512:(dob + 1) * 512],
                    start=False, stop=True))(dob),
            ], reads=[c.b_consts, b_bsp_ali], writes=[b_yw])
            xs = c.X[:, tt * D + dob * 512: tt * D + (dob + 1) * 512]
            S.emit("dve", (lambda xs: lambda e: e.tensor_add(xs, xs, yw[:, :]))(xs),
                   reads=[b_yw, c.b_x[tt]], writes=[c.b_x[tt]])

    def load_split_w(hbm_rows, whi, wlo, b_w, semn, wscale=1.0):
        for h2 in range(2):
            cols = slice(h2 * 512, (h2 + 1) * 512)
            S.dma("sp", (lambda hbm_rows, cols: lambda e: e.dma_start(
                out=wst[:, :], in_=hbm_rows[:, cols]))(hbm_rows, cols),
                reads=[b_wst], writes=[b_wst], sem=semn)
            S.emit("act", (lambda whi, cols, wscale: lambda e: e.activation(
                out=whi[:, cols], in_=wst[:, :], func=ACTF.Copy,
                scale=wscale))(whi, cols, wscale),
                reads=[b_wst], writes=[b_w])
            S.emit("dve", (lambda whi, wlo, cols, wscale: lambda e:
                   e.scalar_tensor_tensor(
                wlo[:, cols], in0=wst[:, :], scalar=wscale, in1=whi[:, cols],
                op0=ALU.mult, op1=ALU.subtract))(whi, wlo, cols, wscale),
                reads=[b_wst, b_w], writes=[b_w])

    NKC = 16

    for pair in range(NPAIR):
        for mat, dsth, dstl, b_dst, bias_off in (
                ("wq", qt_hi, qt_lo, b_qt, 0),
                ("wk", kt_hi, kt_lo, b_kt, 8),
                ("wv", trb_hi, trb_lo, b_trb, 16)):
            whi, wlo, b_w = wB_hi, wB_lo, b_wB
            load_split_w(dram[f"{pfx}_{mat}"][pair * 128:(pair + 1) * 128, :],
                         whi, wlo, b_w, "dwp",
                         wscale=0.125 if mat == "wq" else 1.0)
            bias_col = bqkv[:, bias_off + pair:bias_off + pair + 1]
            if bias_off == 0:
                bsc = c.cols[:, 11:12]
                S.emit("dve", (lambda bias_col, bsc: lambda e: e.tensor_scalar(
                    bsc, bias_col, 0.125, None, op0=ALU.mult))(bias_col, bsc),
                    reads=[b_bqkv], writes=[c.b_cols])
                bias_col = bsc
            for tb in range(4):
                fns = []
                rhs_hi = [xn_hi[:, dc * T + tb * 512:dc * T + (tb + 1) * 512]
                          for dc in range(NC)]
                rhs_lo = [xn_lo[:, dc * T + tb * 512:dc * T + (tb + 1) * 512]
                          for dc in range(NC)]
                dst = ps[:, tb * 512:(tb + 1) * 512]
                for dc in range(NC):
                    lh = whi[:, dc * 128:(dc + 1) * 128]
                    ll = wlo[:, dc * 128:(dc + 1) * 128]
                    fns.append((lambda dst, lh, rh, st: lambda e: e.matmul(
                        dst, lh, rh, start=st, stop=False))(
                        dst, lh, rhs_hi[dc], dc == 0))
                    fns.append((lambda dst, lh, rh: lambda e: e.matmul(
                        dst, lh, rh, start=False, stop=False))(dst, lh, rhs_lo[dc]))
                    fns.append((lambda dst, ll, rh, sp: lambda e: e.matmul(
                        dst, ll, rh, start=False, stop=sp))(
                        dst, ll, rhs_hi[dc], dc == NC - 1))
                S.emit("pe", fns, reads=[b_w, b_xn], writes=[b_psb[tb]])
                src_ = ps[:, tb * 512:(tb + 1) * 512]
                dh = dsth[:, tb * 512:(tb + 1) * 512]
                dl = dstl[:, tb * 512:(tb + 1) * 512]
                S.emit("act", (lambda dh, src, bias_col: lambda e: e.activation(
                    out=dh, in_=src, func=ACTF.Identity, bias=bias_col))(dh, src_, bias_col),
                    reads=[b_psb[tb], b_bqkv, c.b_cols], writes=[b_dst])
                S.emit("dve", (lambda dl, src, dh, bias_col: lambda e:
                       e.scalar_tensor_tensor(dl, in0=src, scalar=bias_col, in1=dh,
                                              op0=ALU.add, op1=ALU.subtract))(
                       dl, src_, dh, bias_col),
                       reads=[b_psb[tb], b_bqkv, b_dst, c.b_cols], writes=[b_dst])

        for half in range(2):
            for src, dstt, b_pt in ((trb_hi, v_hi, b_v), (trb_lo, v_lo, b_v)):
                trp = tr0 if src is trb_hi else tr1
                b_trp = b_tr0 if src is trb_hi else b_tr1
                fns = [(lambda kc, src, trp: lambda e: e.transpose(
                    trp[:, (kc % 8) * 128:((kc % 8) + 1) * 128],
                    src[:, kc * 128:(kc + 1) * 128], c.idn_bf[:]))(kc, src, trp)
                    for kc in range(half * 8, half * 8 + 8)]
                S.emit("pe", fns, reads=[b_trb, c.b_consts], writes=[b_trp])
                dst = dstt[:, half * 1024:(half + 1) * 1024]
                S.emit("act", (lambda dst, trp: lambda e: e.activation(
                    out=dst, in_=trp[:, :], func=ACTF.Copy))(dst, trp),
                    reads=[b_trp], writes=[b_pt])

        load_split_w(dram[f"{pfx}_wo"][pair * 128:(pair + 1) * 128, :],
                     wB_hi, wB_lo, b_wB, "dwp")
        m_, nm_, z_, rz_, t64_ = (c.cols[:, i:i + 1] for i in range(6, 11))
        zp = c.cols[:, 12:16]

        def emit_A(hl, qt, par):
            nkb = (qt // 4 + 1) if causal else 4
            for kb in range(nkb):
                fns = []
                dst = ps[:, kb * 512:(kb + 1) * 512]
                first = True
                if causal and kb == qt // 4:
                    mv = qt % 4
                    fns.append((lambda dst, mv: lambda e: e.matmul(
                        dst, c.idn_bf[:],
                        c.masks[:, 384 - 128 * mv:896 - 128 * mv],
                        start=True, stop=False))(dst, mv))
                    first = False
                lq_hi = qt_hi[hl * 64:hl * 64 + 64, qt * 128:(qt + 1) * 128]
                lq_lo = qt_lo[hl * 64:hl * 64 + 64, qt * 128:(qt + 1) * 128]
                rk_hi = kt_hi[hl * 64:hl * 64 + 64, kb * 512:(kb + 1) * 512]
                rk_lo = kt_lo[hl * 64:hl * 64 + 64, kb * 512:(kb + 1) * 512]
                tp = (hl * 64, 0)
                fns.append((lambda dst, lq, rk, st, tp: lambda e: e.matmul(
                    dst, lq, rk, start=st, stop=False, tile_position=tp))(
                    dst, lq_hi, rk_hi, first, tp))
                fns.append((lambda dst, lq, rk, tp: lambda e: e.matmul(
                    dst, lq, rk, start=False, stop=False, tile_position=tp))(
                    dst, lq_hi, rk_lo, tp))
                fns.append((lambda dst, lq, rk, tp: lambda e: e.matmul(
                    dst, lq, rk, start=False, stop=True, tile_position=tp))(
                    dst, lq_lo, rk_hi, tp))
                S.emit("pe", fns, reads=[b_qt, b_kt, c.b_consts],
                       writes=[b_psb[kb]])
                S.emit("act", (lambda kb, par: lambda e: e.activation(
                    out=ssb[par][:, kb * 512:(kb + 1) * 512],
                    in_=ps[:, kb * 512:(kb + 1) * 512],
                    func=ACTF.Copy))(kb, par),
                    reads=[b_psb[kb]], writes=[b_ssb[par]])

        def emit_B_stats(hl, qt, par):
            nkb = qt // 4 + 1
            sb_t = ssb[par]
            S.emit("dve", (lambda nkb, sb_t: lambda e: e.reduce_max(
                m_, sb_t[:, 0:nkb * 512], axis=AX))(nkb, sb_t),
                reads=[b_ssb[par]], writes=[c.b_cols])
            S.emit("dve", lambda e: e.tensor_scalar(
                nm_, m_, -1.0, None, op0=ALU.mult),
                reads=[c.b_cols], writes=[c.b_cols])
            S.emit("act", (lambda nkb, sb_t: lambda e: e.activation(
                out=sb_t[:, 0:nkb * 512], in_=sb_t[:, 0:nkb * 512],
                func=ACTF.Exp, bias=nm_,
                accum_out=z_))(nkb, sb_t),
                reads=[b_ssb[par], c.b_cols], writes=[b_ssb[par], c.b_cols])

        def emit_B_rounds(hl, qt, par, r0, r1):
            sb_t = ssb[par]
            for r in range(r0, r1):
                src_t, src_b = (sb_t, b_ssb[par]) if r == 0 else (sx, b_sx)
                S.emit("dve", (lambda src_t: lambda e: e.max(
                    out=c.m8[:], in_=src_t[:]))(src_t),
                    reads=[src_b], writes=[c.b_m8])
                if r == 0:
                    S.emit("dve", lambda e: e.tensor_scalar(
                        nm_, c.m8[:, 0:1], -1.0, 0.0,
                        op0=ALU.mult, op1=ALU.min),
                        reads=[c.b_m8], writes=[c.b_cols])
                if r == 7:
                    continue
                S.emit("dve", (lambda src_t: lambda e: e.match_replace(
                    out=sx[:], in_to_replace=c.m8[:], in_values=src_t[:],
                    imm_value=NEG))(src_t),
                    reads=[src_b, c.b_m8], writes=[b_sx])

        def emit_B_mask(hl, qt, par):
            sb_t = ssb[par]
            S.emit("dve", (lambda sb_t: lambda e: e.scalar_tensor_tensor(
                sx[:], in0=sb_t[:], scalar=c.m8[:, 7:8], in1=sb_t[:],
                op0=ALU.is_ge, op1=ALU.mult))(sb_t),
                reads=[b_ssb[par], c.b_m8], writes=[b_sx])
            S.emit("act", (lambda sb_t: lambda e: e.activation(
                out=sb_t[:], in_=sx[:], func=ACTF.Exp,
                bias=nm_, accum_out=z_))(sb_t),
                reads=[b_sx, c.b_cols], writes=[b_ssb[par], c.b_cols])

        def emit_B_tail(hl, qt, par):
            psrc, b_psrc = ssb[par], b_ssb[par]
            nkb = (qt // 4 + 1) if causal else 4
            vw = nkb * 512
            S.emit("dve", lambda e: e.reciprocal(rz_, z_),
                   reads=[c.b_cols], writes=[c.b_cols])
            if not causal:
                S.emit("act", (lambda hl, vw, psrc: lambda e: e.activation(
                    out=phi[hl][:, 0:vw], in_=psrc[:, 0:vw],
                    func=ACTF.Copy))(hl, vw, psrc),
                    reads=[b_psrc], writes=[b_phi[hl]])
                S.emit("pool", (lambda hl, vw, psrc: lambda e: e.tensor_sub(
                    plo[hl][:, 0:vw], psrc[:, 0:vw],
                    phi[hl][:, 0:vw]))(hl, vw, psrc),
                    reads=[b_psrc, b_phi[hl]], writes=[b_phi[hl]])
                return
            S.emit("dve", (lambda hl, vw, psrc: lambda e: e.tensor_scalar(
                phi[hl][:, 0:vw], psrc[:, 0:vw], rz_, None,
                op0=ALU.mult))(hl, vw, psrc),
                reads=[b_psrc, c.b_cols], writes=[b_phi[hl]])
            S.emit("dve", (lambda hl, vw, psrc: lambda e: e.scalar_tensor_tensor(
                plo[hl][:, 0:vw], in0=psrc[:, 0:vw], scalar=rz_,
                in1=phi[hl][:, 0:vw],
                op0=ALU.mult, op1=ALU.subtract))(hl, vw, psrc),
                reads=[b_psrc, c.b_cols, b_phi[hl]], writes=[b_phi[hl]])

        def emit_C1(hl, qt):
            nkb = (qt // 4 + 1) if causal else 4
            nkc = 4 * nkb
            pod = po[0:64, 0:128] if hl == 0 else yw[64:128, 0:128]
            b_pod = b_po if hl == 0 else b_yw
            tp = (0, hl * 64)
            pvfirst = True
            for g in range((nkc + 7) // 8):
                kcs = list(range(g * 8, min(nkc, g * 8 + 8)))
                for srcs, trp, b_trp, dstt in ((phi, tr0, b_tr0, trb_hi),
                                               (plo, tr1, b_tr1, trb_lo)):
                    fns = [(lambda kc, src, trp: lambda e: e.transpose(
                        trp[:, (kc % 8) * 128:((kc % 8) + 1) * 128],
                        src[:, kc * 128:(kc + 1) * 128], c.idn_bf[:]))(
                        kc, srcs[hl], trp) for kc in kcs]
                    S.emit("pe", fns, reads=[b_phi[hl], c.b_consts],
                           writes=[b_trp])
                    dst = dstt[:, hl * 1024:hl * 1024 + len(kcs) * 128]
                    S.emit("act", (lambda dst, trp, n: lambda e: e.activation(
                        out=dst, in_=trp[:, 0:n * 128], func=ACTF.Copy))(
                        dst, trp, len(kcs)),
                        reads=[b_trp], writes=[b_trb])
                fns = []
                for kc in kcs:
                    lv_hi = v_hi[:, kc * 128 + hl * 64:kc * 128 + hl * 64 + 64]
                    lv_lo = v_lo[:, kc * 128 + hl * 64:kc * 128 + hl * 64 + 64]
                    col = hl * 1024 + (kc % 8) * 128
                    rp_hi = trb_hi[:, col:col + 128]
                    rp_lo = trb_lo[:, col:col + 128]
                    last = (g == (nkc + 7) // 8 - 1) and kc == kcs[-1]
                    fns.append((lambda pod, tp, lv, rp, st: lambda e: e.matmul(
                        pod, lv, rp, start=st, stop=False, tile_position=tp))(
                        pod, tp, lv_hi, rp_hi, pvfirst))
                    pvfirst = False
                    fns.append((lambda pod, tp, lv, rp: lambda e: e.matmul(
                        pod, lv, rp, start=False, stop=False,
                        tile_position=tp))(pod, tp, lv_hi, rp_lo))
                    fns.append((lambda pod, tp, lv, rp, sp: lambda e: e.matmul(
                        pod, lv, rp, start=False, stop=sp,
                        tile_position=tp))(pod, tp, lv_lo, rp_hi, last))
                S.emit("pe", fns, reads=[b_v, b_trb], writes=[b_pod])
            oh = otq_hi[hl * 64:hl * 64 + 64, :]
            S.emit("act", (lambda oh, pod: lambda e: e.activation(
                out=oh, in_=pod, func=ACTF.Copy))(oh, pod),
                reads=[b_pod], writes=[b_otq])

        def emit_C_otqlo(hl, qt):
            pod = po[0:64, 0:128] if hl == 0 else yw[64:128, 0:128]
            b_pod = b_po if hl == 0 else b_yw
            oh = otq_hi[hl * 64:hl * 64 + 64, :]
            ol = otq_lo[hl * 64:hl * 64 + 64, :]
            S.emit("dve", (lambda ol, pod, oh: lambda e: e.tensor_sub(
                ol, pod, oh))(ol, pod, oh),
                reads=[b_pod, b_otq], writes=[b_otq])

        def emit_C_wo(hl, qt):
            oh = otq_hi[hl * 64:hl * 64 + 64, :]
            ol = otq_lo[hl * 64:hl * 64 + 64, :]
            for dob in range(2):
                wdst = po[:, 0:512] if dob == 0 else yw[:, 0:512]
                b_wdst = b_po if dob == 0 else b_yw
                fns = []
                for lh, rh, st, sp in ((oh, wB_hi, True, False),
                                       (oh, wB_lo, False, False),
                                       (ol, wB_hi, False, True)):
                    rhs = rh[hl * 64:hl * 64 + 64, dob * 512:(dob + 1) * 512]
                    fns.append((lambda lh, rhs, st, sp, rp, wdst:
                                lambda e: e.matmul(
                        wdst, lh, rhs, start=st, stop=sp,
                        tile_position=(rp, 0)))(lh, rhs, st, sp,
                                                hl * 64, wdst))
                S.emit("pe", fns, reads=[b_otq, b_wB], writes=[b_wdst])

        def emit_C_xadd(hl, qt):
            for dob in range(2):
                wdst = po[:, 0:512] if dob == 0 else yw[:, 0:512]
                b_wdst = b_po if dob == 0 else b_yw
                xs = c.X[:, qt * D + dob * 512:qt * D + (dob + 1) * 512]
                if causal:
                    S.emit("dve", (lambda xs, wdst: lambda e: e.tensor_add(
                        xs, xs, wdst))(xs, wdst),
                        reads=[b_wdst, c.b_x[qt]], writes=[c.b_x[qt]])
                else:
                    S.emit("dve", (lambda xs, wdst: lambda e:
                           e.scalar_tensor_tensor(
                        xs, in0=wdst, scalar=rz_, in1=xs,
                        op0=ALU.mult, op1=ALU.add))(xs, wdst),
                        reads=[b_wdst, c.b_x[qt], c.b_cols],
                        writes=[c.b_x[qt]])

        units = [(hl, qt) for qt in range(NT) for hl in range(2)]
        n_u = len(units)
        for i in range(n_u + 2):
            for _ in range(2):
                if ppw:
                    ppw.pop(0)()
            u0 = units[i] if i < n_u else None
            u1 = units[i - 1] if 1 <= i <= n_u else None
            u2 = units[i - 2] if i >= 2 else None
            if causal:
                if u0:
                    emit_A(*u0, i % 2)
                if u2:
                    emit_C1(*u2)
                if u1:
                    emit_B_stats(*u1, (i - 1) % 2)
                if u2:
                    emit_C_otqlo(*u2)
                    emit_C_wo(*u2)
                    emit_C_xadd(*u2)
                if u1:
                    emit_B_tail(*u1, (i - 1) % 2)
            else:
                if u0:
                    emit_A(*u0, i % 2)
                if u2:
                    emit_C1(*u2)
                if u1:
                    emit_B_rounds(*u1, (i - 1) % 2, 0, 6)
                if u2:
                    emit_C_otqlo(*u2)
                    emit_C_wo(*u2)
                if u1:
                    emit_B_rounds(*u1, (i - 1) % 2, 6, 8)
                    emit_B_mask(*u1, (i - 1) % 2)
                if u2:
                    emit_C_xadd(*u2)
                if u1:
                    emit_B_tail(*u1, (i - 1) % 2)
    est.close()



def _prep_ffn_prepass_work(c, est):
    S, nc, dram = c.S, c.nc, c.dram
    W4 = 512
    st0 = est.enter_context(nc.sbuf_tensor("pp_st0", [128, W4], F32))
    st_ = [st0, st0]
    hi_ = [est.enter_context(nc.sbuf_tensor(f"pp_h{i}", [128, W4], BF16))
           for i in range(2)]
    lo_ = [est.enter_context(nc.sbuf_tensor(f"pp_l{i}", [128, W4], BF16))
           for i in range(2)]
    b_st = S.buf("pp_st")
    b_hl = [S.buf("pp_hl0"), S.buf("pp_hl1")]

    def srcs(fam):
        if fam == "up":
            return dram["up_w"], dram["c_up_h"], dram["c_up_l"], NE
        if fam == "dn":
            return dram["down_w"], dram["c_dn_h"], dram["c_dn_l"], NE
        if fam == "shu":
            return dram["sh_up_w"], dram["c_shu_h"], dram["c_shu_l"], 1
        return dram["sh_down_w"], dram["c_shd_h"], dram["c_shd_l"], 1

    work = []

    def bias_job():
        for p in range(2):
            cs = slice(p * (D // 2), (p + 1) * (D // 2))
            S.dma("pool", (lambda cs: lambda e_: e_.dma_start(
                out=st_[0][0:NE, 0:D // 2], in_=dram["up_b"][:, cs]))(cs),
                reads=[b_st], writes=[b_st], sem="dppi")
            S.emit("act", (lambda p: lambda e_: e_.activation(
                out=hi_[p][0:NE, :], in_=st_[p][0:NE, :], func=ACTF.Copy))(p),
                reads=[b_st], writes=[b_hl[p]])
            S.emit("dve", (lambda p: lambda e_: e_.tensor_sub(
                lo_[p][0:NE, :], st_[p][0:NE, :], hi_[p][0:NE, :]))(p),
                reads=[b_st, b_hl[p]], writes=[b_hl[p]])
            S.dma("pool", [
                (lambda p, cs: lambda e_: e_.dma_start(
                    out=dram["c_ub_h"][:, cs], in_=hi_[p][0:NE, :]))(p, cs),
                (lambda p, cs: lambda e_: e_.dma_start(
                    out=dram["c_ub_l"][:, cs], in_=lo_[p][0:NE, :]))(p, cs),
            ], reads=[b_hl[p]], writes=[b_hl[p], c.b_cache], sem=f"dppo{p}")

    def shb_job():
        S.dma("pool", [
            lambda e_: e_.dma_start(out=st_[0][0:1, 0:D // 2],
                                    in_=dram["sh_up_b"][:, 0:D // 2]),
            lambda e_: e_.dma_start(out=st_[0][1:2, 0:D // 2],
                                    in_=dram["sh_up_b"][:, D // 2:D]),
            lambda e_: e_.dma_start(out=st_[0][2:3, 0:D // 2],
                                    in_=dram["sh_down_b"][:, 0:D // 2]),
            lambda e_: e_.dma_start(out=st_[0][3:4, 0:D // 2],
                                    in_=dram["sh_down_b"][:, D // 2:D]),
        ], reads=[b_st], writes=[b_st], sem="dppi")
        S.emit("act", lambda e_: e_.activation(
            out=hi_[0][0:4, :], in_=st_[0][0:4, :], func=ACTF.Copy),
            reads=[b_st], writes=[b_hl[0]])
        S.emit("dve", lambda e_: e_.tensor_sub(
            lo_[0][0:4, :], st_[0][0:4, :], hi_[0][0:4, :]),
            reads=[b_st, b_hl[0]], writes=[b_hl[0]])
        outs = [(dram["c_sub_h"], hi_[0]), (dram["c_sub_l"], lo_[0])]
        fns = []
        for dst, srct in outs:
            for r in range(2):
                for hf in range(2):
                    fns.append((lambda dst, srct, r, hf: lambda e_: e_.dma_start(
                        out=dst[r:r + 1, hf * (D // 2):(hf + 1) * (D // 2)],
                        in_=srct[2 * r + hf:2 * r + hf + 1, :]))(
                        dst, srct, r, hf))
        S.dma("pool", fns, reads=[b_hl[0]],
              writes=[b_hl[0], c.b_cache], sem="dppo0")

    work.append(bias_job)
    work.append(shb_job)
    for fam in ("up", "dn", "shu", "shd"):
        srct, dh, dl, n = srcs(fam)
        for e in range(n):
            for dc in range(NC):
                for hf in range(2):
                    def job(srct=srct, dh=dh, dl=dl, e=e, dc=dc, hf=hf,
                            p=[len(work) % 2]):
                        p = p[0]
                        rows = srct[(e * NC + dc) * 128:(e * NC + dc + 1) * 128,
                                    hf * W4:(hf + 1) * W4]
                        S.dma("pool", (lambda rows, p: lambda e_: e_.dma_start(
                            out=st_[p][:], in_=rows))(rows, p),
                            reads=[b_st], writes=[b_st], sem="dppi")
                        S.emit("act", (lambda p: lambda e_: e_.activation(
                            out=hi_[p][:], in_=st_[p][:], func=ACTF.Copy))(p),
                            reads=[b_st], writes=[b_hl[p]])
                        S.emit("dve", (lambda p: lambda e_: e_.tensor_sub(
                            lo_[p][:], st_[p][:], hi_[p][:]))(p),
                            reads=[b_st, b_hl[p]], writes=[b_hl[p]])
                        ohi = dh[e * 128:(e + 1) * 128,
                                 dc * D + hf * W4:dc * D + (hf + 1) * W4]
                        olo = dl[e * 128:(e + 1) * 128,
                                 dc * D + hf * W4:dc * D + (hf + 1) * W4]
                        S.dma("pool", [
                            (lambda ohi, p: lambda e_: e_.dma_start(
                                out=ohi, in_=hi_[p][:]))(ohi, p),
                            (lambda olo, p: lambda e_: e_.dma_start(
                                out=olo, in_=lo_[p][:]))(olo, p),
                        ], reads=[b_hl[p]],
                            writes=[b_hl[p], c.b_cache], sem=f"dppo{p}")
                    work.append(job)
    return work



def _emit_ffn(c):
    c.ffn_n = getattr(c, "ffn_n", 0) + 1
    pfx = f"f{c.ffn_n}"
    S, nc, dram = c.S, c.nc, c.dram
    B = S.buf
    est = contextlib.ExitStack()

    def sb(name, shape, dt=F32):
        return est.enter_context(nc.sbuf_tensor(f"{pfx}_{name}", list(shape), dt))

    QT_TOK = 512
    QT_TILES = 4

    xn_hi = [sb("xnh0", (128, NC * QT_TOK), BF16),
             sb("xnh1", (128, NC * QT_TOK), BF16)]
    xn_lo = [sb("xnl0", (128, NC * QT_TOK), BF16),
             sb("xnl1", (128, NC * QT_TOK), BF16)]
    upw_hi = sb("uwh", (128, NC * D), BF16); upw_lo = sb("uwl", (128, NC * D), BF16)
    dnw_hi = sb("dwh", (128, NC * D), BF16); dnw_lo = sb("dwl", (128, NC * D), BF16)
    hp_hi = sb("hph", (128, QT_TILES * D), BF16)
    hp_lo = sb("hpl", (128, QT_TILES * D), BF16)
    ht_hi = [sb("hth0", (128, D), BF16), sb("hth1", (128, D), BF16)]
    ht_lo = [sb("htl0", (128, D), BF16), sb("htl1", (128, D), BF16)]
    sdump = sb("sdump", (128, D))
    gw_hi = sb("gwh", (128, 64), BF16); gw_lo = sb("gwl", (128, 64), BF16)
    gst = sb("gst", (128, 64))
    comb = [sb("comb0", (128, QT_TILES * 8)), sb("comb1", (128, QT_TILES * 8))]
    smn = sb("smn", (128, 8))
    ct_hi = [sb("cth0", (8, QT_TILES * 128), BF16),
             sb("cth1", (8, QT_TILES * 128), BF16)]
    ct_lo = [sb("ctl0", (8, QT_TILES * 128), BF16),
             sb("ctl1", (8, QT_TILES * 128), BF16)]
    db_hi = sb("dbh", (NE, D), BF16); db_lo = sb("dbl", (NE, D), BF16)
    ubs_h = sb("ubsh", (1, D), BF16); ubs_l = sb("ubsl", (1, D), BF16)

    ups = [est.enter_context(nc.psum_tensor(f"{pfx}_up{i}", [128, 1024], F32))
           for i in range(2)]
    trf = est.enter_context(nc.psum_tensor(f"{pfx}_trf", [128, 1024], BF16))
    gps = est.enter_context(nc.psum_tensor(f"{pfx}_gps", [128, 512], F32))
    yps = est.enter_context(nc.psum_tensor(f"{pfx}_yps", [128, 1024], F32))

    b_xn = [B("fxn0"), B("fxn1")]; b_upw = B("fupw"); b_dnw = B("fdnw")
    b_hp = B("fhp"); b_ht = [B("fht0"), B("fht1")]
    b_gw = B("fgw"); b_comb = [B("fcomb0"), B("fcomb1")]
    b_ct = [B("fct0"), B("fct1")]
    b_bias = B("fbias"); b_ubs = B("fubs")
    b_ups = [B("fup0"), B("fup1")]
    b_trf = B("ftrf"); b_gps = B("fgps"); b_yps = B("fyps")
    b_sdump = B("fsdump"); b_smn = B("fsmn")

    S.dma("sp", lambda e: e.dma_start(out=gst[:], in_=dram["gate_sw"][:]),
          writes=[b_gw], sem="dgw")
    S.emit("act", lambda e: e.activation(out=gw_hi[:], in_=gst[:], func=ACTF.Copy),
           reads=[b_gw], writes=[b_gw])
    S.emit("dve", lambda e: e.tensor_sub(gw_lo[:], gst[:], gw_hi[:]),
           reads=[b_gw], writes=[b_gw])
    def split_bias(src_hbm, n_rows, out_hi, out_lo, semn):
        S.dma("sp", (lambda: lambda e: e.dma_start(
            out=sdump[0:n_rows, :], in_=src_hbm))(),
            reads=[b_sdump], writes=[b_sdump], sem=semn)
        S.emit("act", (lambda: lambda e: e.activation(
            out=out_hi[0:n_rows, :], in_=sdump[0:n_rows, :], func=ACTF.Copy))(),
            reads=[b_sdump], writes=[b_bias])
        S.emit("dve", (lambda: lambda e: e.tensor_sub(
            out_lo[0:n_rows, :], sdump[0:n_rows, :], out_hi[0:n_rows, :]))(),
            reads=[b_sdump, b_bias], writes=[b_bias])

    split_bias(dram["down_b"][:], NE, db_hi, db_lo, "dfb")

    def load_w(cache_h, cache_l, erow, whi, wlo, b_w, semn):
        S.dma("sp", [
            (lambda: lambda e: e.dma_start(
                out=whi[:], in_=cache_h[erow * 128:(erow + 1) * 128, :]))(),
            (lambda: lambda e: e.dma_start(
                out=wlo[:], in_=cache_l[erow * 128:(erow + 1) * 128, :]))(),
        ], reads=[b_w, c.b_cache], writes=[b_w], sem=semn)

    m_, nm_, z_, rz_, b2_, s2_, r2_ = (c.cols[:, i:i + 1] for i in range(16, 23))

    def emit_ln_gate_tile(qtr, par, i):
        tt = qtr * QT_TILES + i
        _emit_ln_tile(c, tt, xn_hi[par], xn_lo[par], b_xn[par], i, QT_TOK,
                      gps, b_gps, sdump, b_sdump)
        fns = []
        for dc in range(NC):
            lh_hi = xn_hi[par][:, dc * QT_TOK + i * 128:dc * QT_TOK + (i + 1) * 128]
            lh_lo = xn_lo[par][:, dc * QT_TOK + i * 128:dc * QT_TOK + (i + 1) * 128]
            rh_hi = gw_hi[:, dc * 8:(dc + 1) * 8]
            rh_lo = gw_lo[:, dc * 8:(dc + 1) * 8]
            fns.append((lambda lh, rh, st: lambda e: e.matmul(
                gps[:, 0:8], lh, rh, start=st, stop=False))(lh_hi, rh_hi, dc == 0))
            fns.append((lambda lh, rh: lambda e: e.matmul(
                gps[:, 0:8], lh, rh, start=False, stop=False))(lh_hi, rh_lo))
            fns.append((lambda lh, rh, sp: lambda e: e.matmul(
                gps[:, 0:8], lh, rh, start=False, stop=sp))(
                lh_lo, rh_hi, dc == NC - 1))
        S.emit("pe", fns, reads=[b_xn[par], b_gw], writes=[b_gps])
        S.emit("dve", lambda e: e.reduce_max(m_, gps[:, 0:8], axis=AX),
               reads=[b_gps], writes=[c.b_cols])
        S.emit("dve", lambda e: e.tensor_scalar(nm_, m_, -1.0, None,
                                                op0=ALU.mult),
               reads=[c.b_cols], writes=[c.b_cols])
        S.emit("act", lambda e: e.activation(out=smn[:], in_=gps[:, 0:8],
                                             func=ACTF.Exp, bias=nm_,
                                             accum_out=z_),
               reads=[b_gps, c.b_cols], writes=[b_smn, c.b_cols])
        S.emit("dve", lambda e: e.reciprocal(rz_, z_),
               reads=[c.b_cols], writes=[c.b_cols])
        S.emit("act", lambda e: e.activation(out=smn[:], in_=smn[:],
                                             func=ACTF.Copy, scale=rz_),
               reads=[b_smn, c.b_cols], writes=[b_smn])
        S.emit("dve", lambda e: e.max(out=c.m8[:], in_=smn[:]),
               reads=[b_smn], writes=[c.b_m8])
        S.emit("dve", lambda e: e.tensor_add(s2_, c.m8[:, 0:1], c.m8[:, 1:2]),
               reads=[c.b_m8], writes=[c.b_cols])
        S.emit("dve", lambda e: e.tensor_scalar(s2_, s2_, 1e-20, None,
                                                op0=ALU.add),
               reads=[c.b_cols], writes=[c.b_cols])
        S.emit("dve", lambda e: e.reciprocal(r2_, s2_),
               reads=[c.b_cols], writes=[c.b_cols])
        cmb = comb[par][:, i * 8:(i + 1) * 8]
        S.emit("dve", (lambda cmb: lambda e: e.scalar_tensor_tensor(
            cmb, in0=smn[:], scalar=c.m8[:, 1:2], in1=smn[:],
            op0=ALU.is_ge, op1=ALU.mult))(cmb),
            reads=[b_smn, c.b_m8], writes=[b_comb[par]])
        S.emit("dve", (lambda cmb: lambda e: e.tensor_scalar(
            cmb, cmb, r2_, None, op0=ALU.mult))(cmb),
            reads=[b_comb[par], c.b_cols], writes=[b_comb[par]])

    def emit_combT(qtr, par):
        fns = [(lambda i, par: lambda e: e.transpose(
            gps[0:8, i * 128:(i + 1) * 128],
            comb[par][:, i * 8:(i + 1) * 8], c.idn_f32[:]))(i, par)
            for i in range(QT_TILES)]
        S.emit("pe", fns, reads=[b_comb[par], c.b_consts], writes=[b_gps])
        S.emit("act", (lambda par: lambda e: e.activation(
            out=ct_hi[par][:], in_=gps[0:8, 0:512], func=ACTF.Copy))(par),
            reads=[b_gps], writes=[b_ct[par]])
        S.emit("dve", (lambda par: lambda e: e.tensor_sub(
            ct_lo[par][:], gps[0:8, 0:512], ct_hi[par][:]))(par),
            reads=[b_gps, b_ct[par]], writes=[b_ct[par]])

    def emit_refill(e_next):
        if e_next == NE:
            hsrc, lsrc = dram["c_sub_h"][0:1, :], dram["c_sub_l"][0:1, :]
        elif e_next == NE + 1:
            hsrc, lsrc = dram["c_sub_h"][1:2, :], dram["c_sub_l"][1:2, :]
        else:
            hsrc = dram["c_ub_h"][e_next:e_next + 1, :]
            lsrc = dram["c_ub_l"][e_next:e_next + 1, :]
        S.dma("sp", [
            (lambda hsrc: lambda e_: e_.dma_start(
                out=ubs_h[0:1, :], in_=hsrc))(hsrc),
            (lambda lsrc: lambda e_: e_.dma_start(
                out=ubs_l[0:1, :], in_=lsrc))(lsrc),
        ], reads=[b_ubs, c.b_cache], writes=[b_ubs], sem="dub")

    for i in range(QT_TILES):
        emit_ln_gate_tile(0, 0, i)
    emit_combT(0, 0)

    for qtr in range(4):
        par = qtr % 2
        tts = range(qtr * QT_TILES, (qtr + 1) * QT_TILES)
        load_w(dram["c_up_h"], dram["c_up_l"], 0, upw_hi, upw_lo, b_upw, "dfu")
        emit_refill(0)
        for e in range(NE + 1):
            shared = e == NE
            if shared:
                load_w(dram["c_shd_h"], dram["c_shd_l"], 0, dnw_hi, dnw_lo,
                       b_dnw, "dfd")
            else:
                load_w(dram["c_dn_h"], dram["c_dn_l"], e, dnw_hi, dnw_lo,
                       b_dnw, "dfd")
            for i, tt in enumerate(tts):
                up = ups[i % 2]
                b_up = b_ups[i % 2]
                h32 = yps[:, 0:D]
                b_h32 = b_yps
                for dhb in range(2):
                    fns = []
                    dst = up[:, dhb * 512:(dhb + 1) * 512]
                    fns.append((lambda dst, dhb: lambda e_: e_.matmul(
                        dst, c.ones2[0:1, :], ubs_h[0:1, dhb * 512:(dhb + 1) * 512],
                        start=True, stop=False))(dst, dhb))
                    fns.append((lambda dst, dhb: lambda e_: e_.matmul(
                        dst, c.ones2[0:1, :], ubs_l[0:1, dhb * 512:(dhb + 1) * 512],
                        start=False, stop=False))(dst, dhb))
                    for dc in range(NC):
                        lh_hi = xn_hi[par][:, dc * QT_TOK + i * 128:
                                           dc * QT_TOK + (i + 1) * 128]
                        lh_lo = xn_lo[par][:, dc * QT_TOK + i * 128:
                                           dc * QT_TOK + (i + 1) * 128]
                        rh_hi = upw_hi[:, dc * D + dhb * 512:dc * D + (dhb + 1) * 512]
                        rh_lo = upw_lo[:, dc * D + dhb * 512:dc * D + (dhb + 1) * 512]
                        fns.append((lambda dst, lh, rh: lambda e_: e_.matmul(
                            dst, lh, rh, start=False, stop=False))(dst, lh_hi, rh_hi))
                        fns.append((lambda dst, lh, rh: lambda e_: e_.matmul(
                            dst, lh, rh, start=False, stop=False))(dst, lh_hi, rh_lo))
                        fns.append((lambda dst, lh, rh, sp: lambda e_: e_.matmul(
                            dst, lh, rh, start=False, stop=sp))(
                            dst, lh_lo, rh_hi, dc == NC - 1))
                    S.emit("pe", fns, reads=[b_xn[par], b_upw, b_ubs, c.b_consts],
                           writes=[b_up])
                    S.emit("act", (lambda up, dhb: lambda e_: e_.activation(
                        out=h32[:, dhb * 512:(dhb + 1) * 512],
                        in_=up[:, dhb * 512:(dhb + 1) * 512],
                        func=ACTF.Silu))(up, dhb),
                        reads=[b_up], writes=[b_h32])
                hh = hp_hi[:, i * D:(i + 1) * D]
                hl = hp_lo[:, i * D:(i + 1) * D]
                if shared:
                    S.emit("act", (lambda hh: lambda e_: e_.activation(
                        out=hh, in_=h32[:], func=ACTF.Copy))(hh),
                        reads=[b_h32], writes=[b_hp])
                    S.emit("dve", (lambda hl, hh: lambda e_: e_.tensor_sub(
                        hl, h32[:], hh))(hl, hh),
                        reads=[b_h32, b_hp], writes=[b_hp])
                else:
                    cc = comb[par][:, i * 8 + e:i * 8 + e + 1]
                    S.emit("act", (lambda hh, cc: lambda e_: e_.activation(
                        out=hh, in_=h32[:], func=ACTF.Copy, scale=cc))(hh, cc),
                        reads=[b_h32, b_comb[par]], writes=[b_hp])
                    S.emit("dve", (lambda hl, hh, cc: lambda e_:
                           e_.scalar_tensor_tensor(hl, in0=h32[:], scalar=cc,
                                                   in1=hh, op0=ALU.mult,
                                                   op1=ALU.subtract))(hl, hh, cc),
                           reads=[b_h32, b_comb[par], b_hp], writes=[b_hp])
            if e < NE:
                if e == NE - 1:
                    load_w(dram["c_shu_h"], dram["c_shu_l"], 0, upw_hi, upw_lo,
                           b_upw, "dfu")
                else:
                    load_w(dram["c_up_h"], dram["c_up_l"], e + 1, upw_hi, upw_lo,
                           b_upw, "dfu")
                emit_refill(e + 1)
            else:
                emit_refill(NE + 1)
            if qtr < 3 and 4 <= e <= 7:
                emit_ln_gate_tile(qtr + 1, (qtr + 1) % 2, e - 4)
            if qtr < 3 and e == NE:
                emit_combT(qtr + 1, (qtr + 1) % 2)
            def emit_tr(i):
                hh = hp_hi[:, i * D:(i + 1) * D]
                hl = hp_lo[:, i * D:(i + 1) * D]
                for src, dst_t in ((hh, ht_hi[i % 2]), (hl, ht_lo[i % 2])):
                    fns = [(lambda dc, src: lambda e_: e_.transpose(
                        trf[:, dc * 128:(dc + 1) * 128],
                        src[:, dc * 128:(dc + 1) * 128], c.idn_bf[:]))(dc, src)
                        for dc in range(NC)]
                    S.emit("pe", fns, reads=[b_hp, c.b_consts], writes=[b_trf])
                    S.emit("act", (lambda dst_t: lambda e_: e_.activation(
                        out=dst_t[:], in_=trf[:], func=ACTF.Copy))(dst_t),
                        reads=[b_trf], writes=[b_ht[i % 2]])

            def emit_down(i, tt):
                fns = []
                for dob in range(2):
                    dst = yps[:, dob * 512:(dob + 1) * 512]
                    first = True
                    if shared:
                        fns.append((lambda dst, dob: lambda e_: e_.matmul(
                            dst, c.ones2[0:1, :],
                            ubs_h[0:1, dob * 512:(dob + 1) * 512],
                            start=True, stop=False))(dst, dob))
                        fns.append((lambda dst, dob: lambda e_: e_.matmul(
                            dst, c.ones2[0:1, :],
                            ubs_l[0:1, dob * 512:(dob + 1) * 512],
                            start=False, stop=False))(dst, dob))
                        first = False
                        cth = ct_hi[par][:, i * 128:(i + 1) * 128]
                        ctl = ct_lo[par][:, i * 128:(i + 1) * 128]
                        dbh = db_hi[:, dob * 512:(dob + 1) * 512]
                        dbl = db_lo[:, dob * 512:(dob + 1) * 512]
                        for lh, rh in ((cth, dbh), (cth, dbl), (ctl, dbh)):
                            fns.append((lambda dst, lh, rh: lambda e_: e_.matmul(
                                dst, lh, rh, start=False, stop=False))(dst, lh, rh))
                    for dc in range(NC):
                        lh_hi = ht_hi[i % 2][:, dc * 128:(dc + 1) * 128]
                        lh_lo = ht_lo[i % 2][:, dc * 128:(dc + 1) * 128]
                        rh_hi = dnw_hi[:, dc * D + dob * 512:dc * D + (dob + 1) * 512]
                        rh_lo = dnw_lo[:, dc * D + dob * 512:dc * D + (dob + 1) * 512]
                        fns.append((lambda dst, lh, rh, st: lambda e_: e_.matmul(
                            dst, lh, rh, start=st, stop=False))(
                            dst, lh_hi, rh_hi, first and dc == 0))
                        fns.append((lambda dst, lh, rh: lambda e_: e_.matmul(
                            dst, lh, rh, start=False, stop=False))(dst, lh_hi, rh_lo))
                        fns.append((lambda dst, lh, rh, sp: lambda e_: e_.matmul(
                            dst, lh, rh, start=False, stop=sp))(
                            dst, lh_lo, rh_hi, dc == NC - 1))
                S.emit("pe", fns, reads=[b_ht[i % 2], b_dnw, b_ct[par], b_bias,
                                         b_ubs, c.b_consts],
                       writes=[b_yps])
                for dob in range(2):
                    xs = c.X[:, tt * D + dob * 512:tt * D + (dob + 1) * 512]
                    S.emit("dve", (lambda xs, dob: lambda e_: e_.tensor_add(
                        xs, xs, yps[:, dob * 512:(dob + 1) * 512]))(xs, dob),
                        reads=[b_yps, c.b_x[tt]], writes=[c.b_x[tt]])

            tl = list(tts)
            for i, tt in enumerate(tl):
                emit_tr(i)
                if i >= 1:
                    emit_down(i - 1, tl[i - 1])
            emit_down(len(tl) - 1, tl[-1])
    est.close()



def _prep_shared(inputs, pfx_map):
    f32 = np.float32
    bf = ml_dtypes.bfloat16
    d = {}
    for pfx, nm in pfx_map.items():
        wq = np.asarray(inputs[f"{nm}_Wq"], f32)
        wk = np.asarray(inputs[f"{nm}_Wk"], f32)
        wv = np.asarray(inputs[f"{nm}_Wv"], f32)
        wo = np.asarray(inputs[f"{nm}_Wo"], f32)

        def pan(w):
            a = w.T.reshape(8, 128, 8, 128).transpose(2, 1, 0, 3)
            return np.ascontiguousarray(a.reshape(D, D))

        d[f"{pfx}_wq"] = pan(wq)
        d[f"{pfx}_wk"] = pan(wk)
        d[f"{pfx}_wv"] = pan(wv)
        d[f"{pfx}_wo"] = np.ascontiguousarray(wo.T.reshape(D, D))
        d[f"{pfx}_bq"] = np.ascontiguousarray(
            np.asarray(inputs[f"{nm}_bq"], f32).reshape(8, 128).T)
        d[f"{pfx}_bk"] = np.ascontiguousarray(
            np.asarray(inputs[f"{nm}_bk"], f32).reshape(8, 128).T)
        d[f"{pfx}_bv"] = np.ascontiguousarray(
            np.asarray(inputs[f"{nm}_bv"], f32).reshape(8, 128).T)
        d[f"{pfx}_bo"] = np.asarray(inputs[f"{nm}_bo"], f32).reshape(1, D)
    d["up_w"] = np.ascontiguousarray(
        np.asarray(inputs["L0_ffn_up_w"], f32).reshape(NE * D, D))
    d["down_w"] = np.ascontiguousarray(
        np.asarray(inputs["L0_ffn_down_w"], f32).reshape(NE * D, D))
    d["sh_up_w"] = np.ascontiguousarray(np.asarray(inputs["L0_ffn_sh_up_w"], f32))
    d["sh_down_w"] = np.ascontiguousarray(np.asarray(inputs["L0_ffn_sh_down_w"], f32))
    d["up_b"] = np.ascontiguousarray(np.asarray(inputs["L0_ffn_up_b"], f32))
    d["down_b"] = np.ascontiguousarray(np.asarray(inputs["L0_ffn_down_b"], f32))
    d["sh_up_b"] = np.asarray(inputs["L0_ffn_sh_up_b"], f32).reshape(1, D)
    d["sh_down_b"] = np.asarray(inputs["L0_ffn_sh_down_b"], f32).reshape(1, D)
    gt = np.asarray(inputs["L0_ffn_gate_w"], f32).T
    d["gate_sw"] = np.ascontiguousarray(
        gt.reshape(8, 128, 8).transpose(1, 0, 2).reshape(128, 64))
    d["idn_bf"] = np.eye(128, dtype=bf)
    d["idn_f32"] = np.eye(128, dtype=f32)
    d["ones2"] = np.ones((2, 128), dtype=bf)
    d["masks4"] = _mask_variants()
    return d


def kernel(**inputs):
    pfx_map = {"a": "L0_attn_a", "b": "L0_attn_b"}
    shared = _prep_shared(inputs, pfx_map)
    x = np.ascontiguousarray(np.asarray(inputs["L0_attn_a_x"], np.float32))
    nc = build_program()
    in_maps = []
    for core in range(8):
        m = dict(shared)
        m["x"] = np.ascontiguousarray(x[core])
        in_maps.append(m)
    res = run_bass_kernel_spmd(nc, in_maps, core_ids=list(range(8)))
    out = np.stack([np.asarray(r["out"], np.float32) for r in res.results], axis=0)
    return out
